# version 03
# baseline (speedup 1.0000x reference)
import numpy as np
import ml_dtypes
import concourse.bass as bass
import concourse.mybir as mybir
from concourse.bass_utils import run_bass_kernel_spmd

F32 = mybir.dt.float32
BF16 = mybir.dt.bfloat16
AF = mybir.ActivationFunctionType
ALU = mybir.AluOpType
AX = mybir.AxisListType

TOK = 1024
TH = 512
DK = 32
D = 4096


class Prog:

    def __init__(self, nc):
        self.nc = nc
        self.cnt = {}
        self.sems = {}

    def sem(self, name):
        if name not in self.sems:
            self.sems[name] = self.nc.alloc_semaphore(name)
            self.cnt[name] = 0
        return self.sems[name]

    def inc(self, name, k=1):
        h = self.sem(name)
        self.cnt[name] += k
        return h, k

    def at(self, name):
        self.sem(name)
        return self.cnt[name]


def sb(nc, name, shape, dtype):
    return nc.alloc_sbuf_tensor(name, shape, dtype)


def arena(nc, name, nbytes):
    t = nc.alloc_sbuf_tensor(name, [128, nbytes], mybir.dt.uint8)
    return nc.lookup_mloc(t).addr


def overlay(nc, name, shape, dtype, offset):
    return nc.alloc_sbuf_tensor_at(name, shape, dtype, offset=offset)


def psum(nc, name, shape, dtype=F32):
    return nc.alloc_psum_tensor(name, shape, dtype)



def alloc_ln(nc, tag):
    c = {}
    c["mean"] = sb(nc, f"{tag}_mean", [1, TH], F32)
    c["msq"] = sb(nc, f"{tag}_msq", [1, TH], F32)
    c["var"] = sb(nc, f"{tag}_var", [1, TH], F32)
    c["sd"] = sb(nc, f"{tag}_sd", [1, TH], F32)
    c["rstd"] = sb(nc, f"{tag}_rstd", [1, TH], F32)
    c["mb"] = sb(nc, f"{tag}_mb", [128, TH], F32)
    return c


def emit_ln(p, cons, ln, x_dram, h_sb, rb_sb, xh_sb, xsq_sb, ps, tag,
            pre_wait=None):
    nc = p.nc
    X, A, TQ, T, V = tag + "X", tag + "A", tag + "TQ", tag + "T", tag + "V"
    for t0 in (0, 1):
        baseA = p.at(A)
        baseTQ = p.at(TQ)
        with nc.Block(f"{tag}_ln{t0}") as B:
            @B.sync
            def _(s):
                if t0 == 0 and pre_wait is not None:
                    s.wait_ge(p.sem(pre_wait[0]), pre_wait[1])
                if t0 == 1:
                    s.wait_ge(p.sem(V), p.at(V))
                src = x_dram[:, t0 * TH:(t0 + 1) * TH].rearrange("(k q) t -> q k t", q=128)
                s.dma_start(out=xh_sb[:], in_=src).then_inc(*p.inc(X, 16))

            @B.scalar
            def _(a):
                a.wait_ge(p.sem(X), p.at(X))
                for k in range(DK):
                    if k >= 8:
                        a.wait_ge(p.sem(TQ), baseTQ + k - 7)
                    a.activation(xsq_sb[:, k % 8, :], xh_sb[:, k, :], AF.Square).then_inc(
                        *p.inc(A, 1))

            @B.tensor
            def _(t):
                t.wait_ge(p.sem(X), p.at(X))
                for k in range(DK):
                    t.matmul(ps["s1"], cons["ones_f"][:], xh_sb[:, k, :],
                             start=(k == 0), stop=(k == DK - 1))
                for k in range(DK):
                    t.wait_ge(p.sem(A), baseA + k + 1)
                    t.matmul(ps["s2"], cons["ones_b"][:], xsq_sb[:, k % 8, :],
                             start=(k == 0), stop=(k == DK - 1)).then_inc(*p.inc(TQ, 1))

            @B.vector
            def _(v):
                v.wait_ge(p.sem(TQ), p.at(TQ))
                v.tensor_scalar_mul(ln["mean"][:], ps["s1"], 1.0 / D)
                v.tensor_scalar_mul(ln["msq"][:], ps["s2"], 1.0 / D)
                v.tensor_mul(ln["var"][:], ln["mean"][:], ln["mean"][:])
                v.tensor_sub(ln["var"][:], ln["msq"][:], ln["var"][:])
                v.tensor_scalar_add(ln["var"][:], ln["var"][:], 1e-5).then_inc(*p.inc(V, 1))

            @B.scalar
            def _(a):
                a.wait_ge(p.sem(V), p.at(V))
                a.activation(ln["sd"][:], ln["var"][:], AF.Sqrt).then_inc(*p.inc(A, 1))

            @B.vector
            def _(v):
                v.wait_ge(p.sem(A), p.at(A))
                v.reciprocal(ln["rstd"][:], ln["sd"][:]).then_inc(*p.inc(V, 1))

            @B.tensor
            def _(t):
                t.wait_ge(p.sem(V), p.at(V))
                t.matmul(ps["b1"], cons["ones_r"][:], ln["mean"][:], start=True, stop=True)
                t.matmul(ps["b2"], cons["ones_r"][:], ln["rstd"][:], start=True,
                         stop=True).then_inc(*p.inc(T, 1))

            @B.vector
            def _(v):
                v.wait_ge(p.sem(T), p.at(T))
                v.tensor_copy(rb_sb[:, t0 * TH:(t0 + 1) * TH], ps["b2"])
                v.tensor_copy(ln["mb"][:], ps["b1"])
                for k in range(DK):
                    ins = v.tensor_sub(h_sb[:, k, t0 * TH:(t0 + 1) * TH], xh_sb[:, k, :],
                                       ln["mb"][:])
                ins.then_inc(*p.inc(V, 1))


def get_cons(p):
    nc = p.nc
    if hasattr(nc, "_kcons"):
        return nc._kcons
    cons = {}
    cons["ones_f"] = sb(nc, "c_ones_f", [128, 1], F32)
    cons["ones_b"] = sb(nc, "c_ones_b", [128, 1], BF16)
    cons["ones_r"] = sb(nc, "c_ones_r", [1, 128], F32)
    with nc.Block("cinit") as B:
        @B.gpsimd
        def _(g):
            g.memset(cons["ones_f"][:], 1.0)
            g.memset(cons["ones_b"][:], 1.0)
            ins = g.memset(cons["ones_r"][:], 1.0)
            ins.then_inc(*p.inc("init", 1))
    nc._kcons = cons
    return cons



def emit_l0attn(p, io, upto=5, entry=None):
    nc = p.nc
    cons = get_cons(p)
    ln = alloc_ln(nc, "a0")

    rb_sb = sb(nc, "a0_rb", [128, TOK], F32)
    sig_sb = sb(nc, "a0_sig", [128, 8, TOK], BF16)
    tmp1 = sb(nc, "a0_tmp1", [128, TH], F32)
    tmp2 = sb(nc, "a0_tmp2", [128, TH], F32)
    tmp3 = sb(nc, "a0_tmp3", [128, TH], F32)
    mh_sb = sb(nc, "a0_mh", [128, 2, 128], BF16)
    zh_sb = sb(nc, "a0_zh", [128, 2, 1], BF16)
    den_sb = sb(nc, "a0_den", [1, TH], F32)
    rden_sb = sb(nc, "a0_rden", [1, TH], F32)
    rdb_sb = sb(nc, "a0_rdb", [128, TH], F32)
    ret_sb = sb(nc, "a0_ret", [128, 8, TOK], BF16)
    bq_sb = sb(nc, "a0_bq", [128, 8], F32)
    stg = sb(nc, "a0_stg", [128, 4, TH], F32)
    xsq_sb = sb(nc, "a0_xsq", [128, 8, TH], BF16)
    baseA = arena(nc, "a0_arA", 65536)
    baseB = arena(nc, "a0_arB", 65536)
    h_sb = overlay(nc, "a0_h", [128, DK, TOK], BF16, baseA)
    woe_sb = overlay(nc, "a0_woe", [128, 8, 4096], BF16, baseA)
    xh_sb = overlay(nc, "a0_xh", [128, DK, TH], F32, baseB)
    wq_sb = overlay(nc, "a0_wq", [128, DK, 512], BF16, baseB)
    wo_pan = overlay(nc, "a0_wopan", [128, 2, 4096], BF16, baseB)
    id_b = sb(nc, "c_id_b", [128, 128], BF16)

    ps_all = psum(nc, "a0_ps", [128, 8, TH])
    ps = {
        "s1": ps_all[0:1, 0, :],
        "s2": ps_all[0:1, 1, :],
        "b1": ps_all[:, 2, :],
        "b2": ps_all[:, 3, :],
    }
    class _PS:
        def __init__(self, banks):
            self.banks = banks
        def __getitem__(self, key):
            _, half, _ = key
            return ps_all[:, self.banks[half], :]
    ps_q = _PS([4, 5])
    ps_q2 = _PS([6, 7])

    with nc.Block("a0_pre") as B:
        @B.sync
        def _(s):
            if entry is not None:
                s.wait_ge(p.sem(entry[0]), entry[1])
            s.dma_start(out=bq_sb[:], in_=io["bq"]).then_inc(*p.inc("a0pre", 16))
            s.dma_start(out=id_b[:], in_=io["idb"]).then_inc(*p.inc("a0pre", 16))

    emit_ln(p, cons, ln, io["x_in"], h_sb, rb_sb, xh_sb, xsq_sb, ps, "a0",
            pre_wait=entry)
    V_h = p.at("a0V")

    if upto < 2:
        with nc.Block("a0_fin") as B:
            @B.sync
            def _(s):
                s.wait_ge(p.sem("a0V"), V_h)
        return
    for og in range(2):
        with nc.Block(f"a0_q{og}") as B:
            @B.gpsimd
            def _(g):
                if og > 0:
                    g.wait_ge(p.sem("qT"), 8)
                g.dma_start(
                    out=wq_sb[:],
                    in_=io["wq"][:, og * 512:(og + 1) * 512].rearrange(
                        "(k q) o -> q k o", q=128)).then_inc(*p.inc("qW", 16))

            @B.tensor
            def _(t):
                t.wait_ge(p.sem("qW"), p.at("qW"))
                t.wait_ge(p.sem("a0V"), V_h)
                for oi in range(4):
                    for half in range(2):
                        ti = og * 8 + oi * 2 + half
                        pst = ps_q if oi % 2 == 0 else ps_q2
                        if ti >= 4:
                            t.wait_ge(p.sem("qV"), ti - 3)
                        for k in range(DK):
                            ins = t.matmul(pst[:, half, :], wq_sb[:, k, oi * 128:(oi + 1) * 128],
                                     h_sb[:, k, half * TH:(half + 1) * TH],
                                     start=(k == 0), stop=(k == DK - 1))
                        ins.then_inc(*p.inc("qT", 1))

            @B.vector
            def _(v):
                for oi in range(4):
                    for half in range(2):
                        ti = og * 8 + oi * 2 + half
                        ot = og * 4 + oi
                        pst = ps_q if oi % 2 == 0 else ps_q2
                        hs = slice(half * TH, (half + 1) * TH)
                        v.wait_ge(p.sem("qT"), ti + 1)
                        v.tensor_tensor(tmp1[:], pst[:, half, :], rb_sb[:, hs], ALU.mult)
                        v.tensor_scalar(tmp2[:], tmp1[:], bq_sb[:, ot:ot + 1], 0.0,
                                        ALU.add, ALU.min).then_inc(*p.inc("qV2", 1))
                        v.tensor_scalar(tmp3[:], tmp1[:], bq_sb[:, ot:ot + 1], 0.0,
                                        ALU.add, ALU.max)
                        v.wait_ge(p.sem("qA"), ti + 1)
                        v.tensor_add(sig_sb[:, ot, hs], tmp3[:], tmp2[:]).then_inc(
                            *p.inc("qV", 1))

            @B.scalar
            def _(a):
                for oi in range(4):
                    for half in range(2):
                        ti = og * 8 + oi * 2 + half
                        a.wait_ge(p.sem("qV2"), ti + 1)
                        a.activation(tmp2[:], tmp2[:], AF.Exp).then_inc(*p.inc("qA", 1))

    if upto < 3:
        with nc.Block("a0_fin") as B:
            @B.sync
            def _(s):
                s.wait_ge(p.sem("qV"), 16)
        return
    ps_den = ps["s1"]
    for h in range(8):
        with nc.Block(f"a0_r{h}") as B:
            @B.gpsimd
            def _(g):
                if h >= 2:
                    g.wait_ge(p.sem("rT1"), (h - 2) * 2 + 2)
                g.dma_start(out=mh_sb[:, h % 2, :], in_=io["m"][h]).then_inc(*p.inc("rW", 16))
                g.dma_start(out=zh_sb[:, h % 2, :],
                            in_=io["z"][h].unsqueeze(1)).then_inc(*p.inc("rW", 16))

            @B.tensor
            def _(t):
                t.wait_ge(p.sem("rW"), p.at("rW"))
                t.wait_ge(p.sem("qV"), 16)
                for half in range(2):
                    i = h * 2 + half
                    hs = slice(half * TH, (half + 1) * TH)
                    if i >= 2:
                        t.wait_ge(p.sem("rV2"), i - 1)
                    t.matmul(ps_q[:, half, :], mh_sb[:, h % 2, :], sig_sb[:, h, hs],
                             start=True, stop=True)
                    t.matmul(ps_den, zh_sb[:, h % 2, :], sig_sb[:, h, hs],
                             start=True, stop=True).then_inc(*p.inc("rT1", 1))
                    t.wait_ge(p.sem("rV1"), i + 1)
                    t.wait_ge(p.sem("rV2"), i)
                    t.matmul(ps["b1"], cons["ones_r"][:], rden_sb[:], start=True,
                             stop=True).then_inc(*p.inc("rT2", 1))

            @B.vector
            def _(v):
                for half in range(2):
                    i = h * 2 + half
                    hs = slice(half * TH, (half + 1) * TH)
                    v.wait_ge(p.sem("rT1"), i + 1)
                    v.tensor_scalar_add(den_sb[:], ps_den, 1e-6)
                    v.reciprocal(rden_sb[:], den_sb[:]).then_inc(*p.inc("rV1", 1))
                    v.wait_ge(p.sem("rT2"), i + 1)
                    v.tensor_copy(rdb_sb[:], ps["b1"])
                    v.tensor_tensor(ret_sb[:, h, hs], ps_q[:, half, :], rdb_sb[:],
                                    ALU.mult).then_inc(*p.inc("rV2", 1))

    if upto < 4:
        with nc.Block("a0_fin") as B:
            @B.sync
            def _(s):
                s.wait_ge(p.sem("rV2"), 16)
        return
    for hkv in range(8):
        with nc.Block(f"a0_wo{hkv}") as B:
            @B.gpsimd
            def _(g):
                if hkv == 0:
                    g.wait_ge(p.sem("a0V"), V_h)
                    g.wait_ge(p.sem("qT"), 16)
                for gg in range(4):
                    pi = hkv * 4 + gg
                    if pi >= 2:
                        g.wait_ge(p.sem("woT"), pi - 1)
                    g.dma_start(out=wo_pan[:, pi % 2, :], in_=io["wo4"][hkv, gg]).then_inc(
                        *p.inc("woW", 16))

            @B.tensor
            def _(t):
                if hkv == 0:
                    t.wait_ge(p.sem("a0pre"), 32)
                    t.wait_ge(p.sem("rV1"), 16)
                    t.wait_ge(p.sem("rV2"), 16)
                else:
                    t.wait_ge(p.sem("woV"), 8 * hkv)
                for gg in range(4):
                    pi = hkv * 4 + gg
                    t.wait_ge(p.sem("woW"), 16 * (pi + 1))
                    for ob in range(8):
                        ins = t.matmul(ps_all[:, ob, :], id_b[:],
                                       wo_pan[:, pi % 2, ob * TH:(ob + 1) * TH],
                                       start=(gg == 0), stop=(gg == 3))
                    ins.then_inc(*p.inc("woT", 1))

            @B.vector
            def _(v):
                v.wait_ge(p.sem("woT"), 4 * hkv + 4)
                if hkv == 0:
                    v.wait_ge(p.sem("qT"), 16)
                for ob in range(8):
                    v.tensor_copy(woe_sb[:, hkv, ob * TH:(ob + 1) * TH],
                                  ps_all[:, ob, :]).then_inc(*p.inc("woV", 1))

    if upto < 5 or upto >= 40:
        with nc.Block("a0_fin") as B:
            @B.sync
            def _(s):
                s.wait_ge(p.sem("woV"), 64)
        return
    for half in range(2):
        with nc.Block(f"a0_o{half}") as B:
            @B.sync
            def _(s):
                if half == 0:
                    s.wait_ge(p.sem("woV"), 64)
                else:
                    s.wait_ge(p.sem("oV"), 32)
                src = io["x_in"][:, half * TH:(half + 1) * TH].rearrange(
                    "(k q) t -> q k t", q=128)
                s.dma_start(out=xh_sb[:], in_=src).then_inc(*p.inc("oX", 16))

            @B.tensor
            def _(t):
                t.wait_ge(p.sem("woV"), 64)
                for ot in range(32):
                    i = half * 32 + ot
                    pst = ps_q if ot % 2 == 0 else ps_q2
                    if ot >= 2:
                        t.wait_ge(p.sem("oV"), i - 1)
                    for hkv in range(8):
                        ins = t.matmul(pst[:, half, :], woe_sb[:, hkv, ot * 128:(ot + 1) * 128],
                                 ret_sb[:, hkv, half * TH:(half + 1) * TH],
                                 start=(hkv == 0), stop=(hkv == 7))
                    ins.then_inc(*p.inc("oT", 1))

            @B.vector
            def _(v):
                v.wait_ge(p.sem("oX"), p.at("oX"))
                for ot in range(32):
                    i = half * 32 + ot
                    pst = ps_q if ot % 2 == 0 else ps_q2
                    v.wait_ge(p.sem("oT"), i + 1)
                    if i >= 4:
                        v.wait_ge(p.sem("oS"), 16 * (i - 3))
                    v.tensor_add(stg[:, ot % 4, :], pst[:, half, :],
                                 xh_sb[:, ot, :]).then_inc(*p.inc("oV", 1))

            @B.sync
            def _(s):
                for ot in range(32):
                    i = half * 32 + ot
                    s.wait_ge(p.sem("oV"), i + 1)
                    s.dma_start(out=io["x_out"][ot * 128:(ot + 1) * 128,
                                                half * TH:(half + 1) * TH],
                                in_=stg[:, ot % 4, :]).then_inc(*p.inc("oS", 16))

    with nc.Block("a0_fin") as B:
        @B.sync
        def _(s):
            s.wait_ge(p.sem("oS"), p.at("oS"))



def emit_ffn(p, io, cfg, entry=None):
    nc = p.nc
    tag = cfg["tag"]
    E, ER = cfg["E"], cfg["ER"]

    cons = get_cons(p)
    ln = alloc_ln(nc, tag)

    rb_sb = sb(nc, f"{tag}_rb", [128, TOK], F32)
    logits = sb(nc, f"{tag}_logits", [8, TOK], F32)
    lt_sb = sb(nc, f"{tag}_lt", [128, E], F32)
    m1 = sb(nc, f"{tag}_m1", [128, 1], F32)
    m2 = sb(nc, f"{tag}_m2", [128, 1], F32)
    nm1 = sb(nc, f"{tag}_nm1", [128, 1], F32)
    dm = sb(nc, f"{tag}_dm", [128, 1], F32)
    r12 = sb(nc, f"{tag}_r12", [128, 1], F32)
    ind = sb(nc, f"{tag}_ind", [128, E], F32)
    msk = sb(nc, f"{tag}_msk", [128, E], F32)
    expv = sb(nc, f"{tag}_expv", [128, E], F32)
    comb = sb(nc, f"{tag}_comb", [128, E], F32)
    comb_f = sb(nc, f"{tag}_combf", [8, TOK], F32)
    hl_sb = sb(nc, f"{tag}_hl", [128, TOK], BF16)
    hw_sb = sb(nc, f"{tag}_hw", [128, TOK], BF16)
    bcat_sb = sb(nc, f"{tag}_bcat", [128, 4096], BF16)
    e8_sb = sb(nc, f"{tag}_e8", [8, 128], F32)
    wr_sb = sb(nc, f"{tag}_wr", [128, DK, 8], BF16)
    idf_sb = sb(nc, f"{tag}_idf", [128, 128], F32)
    bb_sb = sb(nc, f"{tag}_bb", [128, 32], F32)
    eb_sb = sb(nc, f"{tag}_eb", [8, 1], F32)

    baseA = arena(nc, f"{tag}_arA", 65536)
    baseB = arena(nc, f"{tag}_arB", 65536)
    baseC = arena(nc, f"{tag}_arC", 32768)
    h_sb = overlay(nc, f"{tag}_h", [128, DK, TOK], BF16, baseA)
    xh_sb = overlay(nc, f"{tag}_xh", [128, DK, TH], F32, baseB)
    at_sb = overlay(nc, f"{tag}_at", [128, DK, 128], BF16, baseC)
    wb_sb = overlay(nc, f"{tag}_wb", [128, DK, 512], BF16, baseC)
    xsq_sb = overlay(nc, f"{tag}_xsq", [128, 8, TH], BF16, baseC)
    stg = overlay(nc, f"{tag}_stg", [128, 4, TH], F32, baseB)
    xr_sb = overlay(nc, f"{tag}_xr", [128, 4, TH], F32, baseB + 8192)
    t0_sb = overlay(nc, f"{tag}_t0", [128, TH], F32, baseB + 16384)

    ps_all = psum(nc, f"{tag}_ps", [128, 8, TH])
    ps = {"s1": ps_all[0:1, 0, :], "s2": ps_all[0:1, 1, :],
          "b1": ps_all[:, 2, :], "b2": ps_all[:, 3, :]}


    PRE, LNV = tag + "pre", tag + "V"
    dbg_lt = sb(nc, f"{tag}_dbglt", [128, 48], F32)
    with nc.Block(f"{tag}_pre") as B:
        @B.gpsimd
        def _(g):
            g.dma_start(out=wr_sb[:], in_=io["wr"].rearrange("(k q) o -> q k o", q=128)
                        ).then_inc(*p.inc(PRE, 16))
            g.dma_start(out=idf_sb[:], in_=io["idf"]).then_inc(*p.inc(PRE, 16))
            g.dma_start(out=e8_sb[0:E, :], in_=io["e8"]).then_inc(*p.inc(PRE, 16))
            g.dma_start(out=bcat_sb[0:ER, :], in_=io["bcat"]).then_inc(*p.inc(PRE, 16))
            if cfg["base_bias"]:
                g.dma_start(out=bb_sb[:], in_=io["bb"]).then_inc(*p.inc(PRE, 16))
            if cfg["logit_bias"]:
                g.dma_start(out=eb_sb[0:E, :], in_=io["eb"].unsqueeze(1)).then_inc(
                    *p.inc(PRE, 16))

    emit_ln(p, cons, ln, io["x_in"], h_sb, rb_sb, xh_sb, xsq_sb, ps, tag,
            pre_wait=entry)
    V_h = p.at(LNV)

    with nc.Block(f"{tag}_rt") as B:
        @B.tensor
        def _(t):
            t.wait_ge(p.sem(PRE), 16)
            t.wait_ge(p.sem(LNV), V_h)
            for half in range(2):
                for k in range(DK):
                    ins = t.matmul(ps_all[0:8, half, 0:TH], wr_sb[:, k, 0:8],
                                   h_sb[:, k, half * TH:(half + 1) * TH],
                                   start=(k == 0), stop=(k == DK - 1))
                ins.then_inc(*p.inc("fT1", 1))

        @B.vector
        def _(v):
            for half in range(2):
                hs = slice(half * TH, (half + 1) * TH)
                v.wait_ge(p.sem("fT1"), half + 1)
                if cfg["logit_bias"]:
                    v.tensor_tensor(logits[0:E, hs], ps_all[0:E, half, 0:TH],
                                    rb_sb[0:E, hs], ALU.mult)
                    ins = v.tensor_scalar_add(logits[0:E, hs], logits[0:E, hs],
                                              eb_sb[0:E, :])
                else:
                    ins = v.tensor_tensor(logits[0:E, hs], ps_all[0:E, half, 0:TH],
                                          rb_sb[0:E, hs], ALU.mult)
                ins.then_inc(*p.inc("fV1", 1))

    for tc in range(8):
        with nc.Block(f"{tag}_tk{tc}") as B:
            @B.tensor
            def _(t):
                half = tc // 4
                t.wait_ge(p.sem("fV1"), half + 1)
                if tc >= 1:
                    t.wait_ge(p.sem("fV2"), 2 * tc - 1)
                t.transpose(ps_all[:, 2, 0:E], logits[0:E, tc * 128:(tc + 1) * 128],
                            idf_sb[0:E, 0:E]).then_inc(*p.inc("fT2", 2))
                t.wait_ge(p.sem("fV2"), 2 * tc + 1)
                t.transpose(ps_all[0:E, 3, 0:128], comb[:, 0:E], idf_sb[:]).then_inc(
                    *p.inc("fT2", 2))

            @B.vector
            def _(v):
                v.wait_ge(p.sem("fT2"), 4 * tc + 2)
                v.tensor_copy(lt_sb[:, 0:E], ps_all[:, 2, 0:E])
                v.drain()
                v.reduce_max(m1[:], lt_sb[:, 0:E], axis=AX.X)
                v.drain()
                if cfg["top1"]:
                    ins = v.tensor_scalar(comb[:, 0:E], lt_sb[:, 0:E], m1[:], None,
                                          ALU.is_ge)
                    ins = v.drain()
                else:
                    v.tensor_scalar(ind[:, 0:E], lt_sb[:, 0:E], m1[:], None, ALU.is_ge)
                    v.drain()
                    v.tensor_scalar_mul(msk[:, 0:E], ind[:, 0:E], -60.0)
                    v.drain()
                    v.tensor_add(msk[:, 0:E], msk[:, 0:E], lt_sb[:, 0:E])
                    v.drain()
                    v.reduce_max(m2[:], msk[:, 0:E], axis=AX.X)
                    v.drain()
                    v.tensor_scalar(ind[:, 0:E], lt_sb[:, 0:E], m2[:], None, ALU.is_ge)
                    v.tensor_scalar_mul(nm1[:], m1[:], -1.0)
                    v.drain()
                    v.tensor_sub(dm[:], m2[:], m1[:])
                    v.drain().then_inc(*p.inc("fV3", 1))
                    v.wait_ge(p.sem("fA1"), 2 * tc + 2)
                    v.tensor_scalar_add(dm[:], dm[:], 1.0)
                    v.drain()
                    v.reciprocal(r12[:], dm[:])
                    v.drain()
                    v.tensor_mul(comb[:, 0:E], expv[:, 0:E], ind[:, 0:E])
                    v.drain()
                    v.tensor_scalar(comb[:, 0:E], comb[:, 0:E], r12[:],
                                    cfg["scaling"], ALU.mult, ALU.mult)
                    ins = v.drain()
                ins.then_inc(*p.inc("fV2", 1))
                if tc == 0 and "dbg_lt" in io:
                    v.tensor_copy(dbg_lt[:, 0:E], lt_sb[:, 0:E])
                    v.tensor_copy(dbg_lt[:, 8:8 + E], expv[:, 0:E])
                    v.tensor_copy(dbg_lt[:, 16:16 + E], comb[:, 0:E])
                    v.tensor_copy(dbg_lt[:, 24:25], m1[:])
                    v.tensor_copy(dbg_lt[:, 25:26], m2[:])
                    v.tensor_copy(dbg_lt[:, 26:27], r12[:])
                    v.tensor_copy(dbg_lt[:, 32:32 + E], msk[:, 0:E])
                    v.tensor_copy(dbg_lt[:, 40:40 + E], ind[:, 0:E]).then_inc(*p.inc("fVD", 1))
                v.wait_ge(p.sem("fT2"), 4 * tc + 4)
                v.tensor_copy(comb_f[0:E, tc * 128:(tc + 1) * 128],
                              ps_all[0:E, 3, 0:128]).then_inc(*p.inc("fV2", 1))

            if not cfg["top1"]:
                @B.scalar
                def _(a):
                    a.wait_ge(p.sem("fV3"), tc + 1)
                    a.activation(expv[:, 0:E], lt_sb[:, 0:E], AF.Exp, bias=nm1[:])
                    a.activation(dm[:], dm[:], AF.Exp)
                    a.drain().then_inc(*p.inc("fA1", 2))

    with nc.Block(f"{tag}_hl") as B:
        @B.gpsimd
        def _(g):
            g.wait_ge(p.sem(LNV), V_h)
            g.dma_start(out=at_sb[:, :, 0:ER], in_=io["at"].rearrange(
                "(k q) o -> q k o", q=128)).then_inc(*p.inc("fWa", 16))

        @B.tensor
        def _(t):
            t.wait_ge(p.sem("fWa"), 16)
            for half in range(2):
                for k in range(DK):
                    ins = t.matmul(ps_all[0:ER, 4 + half, 0:TH], at_sb[:, k, 0:ER],
                                   h_sb[:, k, half * TH:(half + 1) * TH],
                                   start=(k == 0), stop=(k == DK - 1))
                ins.then_inc(*p.inc("fT3", 1))
                t.wait_ge(p.sem("fV2"), 16)
                ins = t.matmul(ps_all[:, 2, 0:TH], e8_sb[0:E, :],
                               comb_f[0:E, half * TH:(half + 1) * TH],
                               start=True, stop=True)
                ins.then_inc(*p.inc("fT3", 1))

        @B.vector
        def _(v):
            for half in range(2):
                hs = slice(half * TH, (half + 1) * TH)
                v.wait_ge(p.sem("fT3"), 2 * half + 1)
                v.tensor_copy(hl_sb[0:ER, hs], ps_all[0:ER, 4 + half, 0:TH])
                v.wait_ge(p.sem("fT3"), 2 * half + 2)
                v.tensor_tensor(hw_sb[0:ER, hs], ps_all[0:ER, 2, 0:TH],
                                hl_sb[0:ER, hs], ALU.mult).then_inc(*p.inc("fV4", 1))

    for og in range(8):
        with nc.Block(f"{tag}_b{og}") as B:
            @B.gpsimd
            def _(g):
                if og == 0:
                    g.wait_ge(p.sem("fV4"), 2)
                else:
                    g.wait_ge(p.sem("fT4"), 8 * og)
                g.dma_start(out=wb_sb[:], in_=io["wb"][:, og * 512:(og + 1) * 512].rearrange(
                    "(k q) o -> q k o", q=128)).then_inc(*p.inc("fWb", 16))

            @B.scalar
            def _(a):
                if og == 0:
                    a.wait_ge(p.sem(LNV), V_h)
                for oi in range(4):
                    for half in range(2):
                        ti = og * 8 + oi * 2 + half
                        ot = og * 4 + oi
                        if ti >= 4:
                            a.wait_ge(p.sem("fV5"), ti - 3)
                        a.dma_start(out=xr_sb[:, ti % 4, :],
                                    in_=io["x_in"][ot * 128:(ot + 1) * 128,
                                                   half * TH:(half + 1) * TH]).then_inc(
                            *p.inc("fXr", 16))

            @B.tensor
            def _(t):
                t.wait_ge(p.sem("fWb"), 16 * (og + 1))
                if og == 0:
                    t.wait_ge(p.sem("fV4"), 2)
                for oi in range(4):
                    for half in range(2):
                        ti = og * 8 + oi * 2 + half
                        bank = 4 + (oi % 2) * 2 + half
                        if ti >= 4:
                            t.wait_ge(p.sem("fV5"), ti - 3)
                        for k in range(DK):
                            t.matmul(ps_all[:, bank, :], wb_sb[:, k, oi * 128:(oi + 1) * 128],
                                     h_sb[:, k, half * TH:(half + 1) * TH],
                                     start=(k == 0), stop=False)
                        ins = t.matmul(ps_all[:, bank, :],
                                       bcat_sb[0:ER, (og * 4 + oi) * 128:(og * 4 + oi + 1) * 128],
                                       hw_sb[0:ER, half * TH:(half + 1) * TH],
                                       start=False, stop=True)
                        ins.then_inc(*p.inc("fT4", 1))

            @B.vector
            def _(v):
                for oi in range(4):
                    for half in range(2):
                        ti = og * 8 + oi * 2 + half
                        ot = og * 4 + oi
                        bank = 4 + (oi % 2) * 2 + half
                        hs = slice(half * TH, (half + 1) * TH)
                        v.wait_ge(p.sem("fT4"), ti + 1)
                        v.wait_ge(p.sem("fXr"), 16 * (ti + 1))
                        if ti >= 4:
                            v.wait_ge(p.sem("fS"), 16 * (ti - 3))
                        v.tensor_tensor(t0_sb[:], ps_all[:, bank, :], rb_sb[:, hs],
                                        ALU.mult)
                        if cfg["base_bias"]:
                            ins = v.scalar_tensor_tensor(
                                stg[:, ti % 4, :], t0_sb[:], bb_sb[:, ot:ot + 1],
                                xr_sb[:, ti % 4, :], ALU.add, ALU.add)
                        else:
                            ins = v.tensor_add(stg[:, ti % 4, :], t0_sb[:],
                                               xr_sb[:, ti % 4, :])
                        ins.then_inc(*p.inc("fV5", 1))

            @B.sync
            def _(s):
                for oi in range(4):
                    for half in range(2):
                        ti = og * 8 + oi * 2 + half
                        ot = og * 4 + oi
                        s.wait_ge(p.sem("fV5"), ti + 1)
                        s.dma_start(out=io["x_out"][ot * 128:(ot + 1) * 128,
                                                    half * TH:(half + 1) * TH],
                                    in_=stg[:, ti % 4, :]).then_inc(*p.inc("fS", 16))

    if "dbg_logits" in io:
        with nc.Block(f"{tag}_dbg") as B:
            @B.sync
            def _(s):
                s.wait_ge(p.sem("fV1"), 2)
                s.dma_start(out=io["dbg_logits"], in_=logits[0:8, :]).then_inc(*p.inc("fDB", 16))
                s.wait_ge(p.sem("fV2"), 16)
                s.dma_start(out=io["dbg_comb"], in_=comb_f[0:8, :]).then_inc(*p.inc("fDB", 16))
                s.wait_ge(p.sem("fVD"), 1)
                s.dma_start(out=io["dbg_lt"], in_=dbg_lt[:]).then_inc(*p.inc("fDB", 16))
                s.wait_ge(p.sem("fV4"), 2)
                s.dma_start(out=io["dbg_hl"], in_=hl_sb[:]).then_inc(*p.inc("fDB", 16))
                s.dma_start(out=io["dbg_hw"], in_=hw_sb[:]).then_inc(*p.inc("fDB", 16))
                s.wait_ge(p.sem("fDB"), p.at("fDB"))

    with nc.Block(f"{tag}_fin") as B:
        @B.sync
        def _(s):
            s.wait_ge(p.sem("fS"), p.at("fS"))



NOC = 145


def emit_l1a1(p, io, entry=None):
    nc = p.nc
    cons = get_cons(p)
    ln = alloc_ln(nc, "i1")

    rb_sb = sb(nc, "i1_rb", [128, TOK], F32)
    stgb = sb(nc, "i1_stgb", [128, 4, TH], BF16)
    stgf = sb(nc, "i1_stgf", [128, 4, TH], F32)

    baseA = arena(nc, "i1_arA", 65536)
    baseB = arena(nc, "i1_arB", 65536)
    baseC = arena(nc, "i1_arC", 8192)
    h_sb = overlay(nc, "i1_h", [128, DK, TOK], BF16, baseA)
    xh_sb = overlay(nc, "i1_xh", [128, DK, TH], F32, baseB)
    wp_sb = overlay(nc, "i1_wp", [128, 2, DK, 512], BF16, baseB)
    xsq_sb = overlay(nc, "i1_xsq", [128, 8, TH], BF16, baseC)

    ps_all = psum(nc, "i1_ps", [128, 8, TH])
    ps = {"s1": ps_all[0:1, 0, :], "s2": ps_all[0:1, 1, :],
          "b1": ps_all[:, 2, :], "b2": ps_all[:, 3, :]}

    emit_ln(p, cons, ln, io["x_in"], h_sb, rb_sb, xh_sb, xsq_sb, ps, "i1",
            pre_wait=entry)
    V_h = p.at("i1V")

    def out_route(ot):
        if ot < 64:
            return io["gate"], ot, False
        if ot < 144:
            return io["hbc"], ot - 64, False
        return io["dtr"], 0, True

    ogs = [list(range(og * 4, min(og * 4 + 4, NOC))) for og in range((NOC + 3) // 4)]
    ti = 0
    ti_of = {}
    for og, ots in enumerate(ogs):
        for oi in range(len(ots)):
            for half in range(2):
                ti_of[(og, oi, half)] = ti
                ti += 1
    NTI = ti

    for og, ots in enumerate(ogs):
        ow = 512 if len(ots) == 4 else 128 * len(ots)
        with nc.Block(f"i1_b{og}") as B:
            @B.gpsimd
            def _(g):
                if og == 0:
                    g.wait_ge(p.sem("i1V"), V_h)
                if og >= 2:
                    g.wait_ge(p.sem("iT"), ti_of[(og - 2, len(ogs[og - 2]) - 1, 1)] + 1)
                g.dma_start(
                    out=wp_sb[:, og % 2, :, 0:ow],
                    in_=io["wi"][:, ots[0] * 128:ots[0] * 128 + ow].rearrange(
                        "(k q) o -> q k o", q=128)).then_inc(*p.inc("iW", 16))

            @B.tensor
            def _(t):
                t.wait_ge(p.sem("iW"), 16 * (og + 1))
                if og == 0:
                    t.wait_ge(p.sem("i1V"), V_h)
                for oi in range(len(ots)):
                    for half in range(2):
                        tj = ti_of[(og, oi, half)]
                        bank = 4 + (oi % 2) * 2 + half
                        if tj >= 4:
                            t.wait_ge(p.sem("iV"), tj - 3)
                        for k in range(DK):
                            ins = t.matmul(
                                ps_all[:, bank, :],
                                wp_sb[:, og % 2, k, oi * 128:(oi + 1) * 128],
                                h_sb[:, k, half * TH:(half + 1) * TH],
                                start=(k == 0), stop=(k == DK - 1))
                        ins.then_inc(*p.inc("iT", 1))

            @B.vector
            def _(v):
                for oi in range(len(ots)):
                    for half in range(2):
                        tj = ti_of[(og, oi, half)]
                        ot = ots[oi]
                        bank = 4 + (oi % 2) * 2 + half
                        hs = slice(half * TH, (half + 1) * TH)
                        dest, _, isf = out_route(ot)
                        v.wait_ge(p.sem("iT"), tj + 1)
                        if tj >= 4:
                            v.wait_ge(p.sem("iS"), 16 * (tj - 3))
                        stgx = stgf if isf else stgb
                        ins = v.tensor_tensor(stgx[:, tj % 4, :], ps_all[:, bank, :],
                                              rb_sb[:, hs], ALU.mult)
                        ins.then_inc(*p.inc("iV", 1))

            @B.sync
            def _(s):
                for oi in range(len(ots)):
                    for half in range(2):
                        tj = ti_of[(og, oi, half)]
                        ot = ots[oi]
                        dest, orow, isf = out_route(ot)
                        s.wait_ge(p.sem("iV"), tj + 1)
                        stgx = stgf if isf else stgb
                        s.dma_start(out=dest[orow * 128:orow * 128 + 128,
                                             half * TH:(half + 1) * TH],
                                    in_=stgx[:, tj % 4, :]).then_inc(*p.inc("iS", 16))

    with nc.Block("i1_fin") as B:
        @B.sync
        def _(s):
            s.wait_ge(p.sem("iS"), p.at("iS"))



def emit_l1a2(p, io):
    nc = p.nc
    cons = get_cons(p)

    CIN = sb(nc, "c2_cin", [128, 2, 1027], BF16)
    ACC1 = sb(nc, "c2_acc1", [128, TOK], F32)
    ACC2 = sb(nc, "c2_acc2", [128, TOK], F32)
    PO = sb(nc, "c2_po", [128, 2, TOK], BF16)
    CW = sb(nc, "c2_cw", [128, 80, 4], F32)
    CB = sb(nc, "c2_cb", [128, 80], F32)
    DTB = sb(nc, "c2_dtb", [128, 1], F32)
    ALOG = sb(nc, "c2_alog", [128, 1], F32)
    NEGA = sb(nc, "c2_nega", [128, 1], F32)
    DT = sb(nc, "c2_dt", [128, TOK], F32)
    TMP = sb(nc, "c2_tmp", [128, TOK], F32)
    S_sb = sb(nc, "c2_s", [128, 8192], F32)
    XSL = sb(nc, "c2_xsl", [128, 64, 128], BF16)
    BSL = sb(nc, "c2_bsl", [128, 8, 128], BF16)
    XDEC = sb(nc, "c2_xdec", [128, 8192], BF16)
    BT = sb(nc, "c2_bt", [128, 8, 128], BF16)
    ADT = sb(nc, "c2_adt", [128, 128], F32)
    ACUM = sb(nc, "c2_acum", [128, 128], F32)
    DS = sb(nc, "c2_ds", [128, 128], F32)
    XMUL = sb(nc, "c2_xmul", [128, 128], F32)
    XMULT = sb(nc, "c2_xmult", [128, 128], F32)
    DEC = sb(nc, "c2_dec", [128, 1], F32)
    DECB = sb(nc, "c2_decb", [128, 128], F32)
    DROW = sb(nc, "c2_drow", [1, 128], F32)
    ONES128 = sb(nc, "c2_ones", [128, 128], F32)
    T1 = sb(nc, "c2_t1", [128, 1024], F32)
    idf_sb = sb(nc, "c2_idf", [128, 128], F32)

    ps_all = psum(nc, "c2_ps", [128, 6, TH])

    with nc.Block("c2_pre") as B:
        @B.sync
        def _(s):
            s.dma_start(out=CW[:], in_=io["cw"]).then_inc(*p.inc("cP", 16))
            s.dma_start(out=CB[:], in_=io["cb"]).then_inc(*p.inc("cP", 16))
            s.dma_start(out=DTB[:], in_=io["dtbias"]).then_inc(*p.inc("cP", 16))
            s.dma_start(out=ALOG[:], in_=io["alog"]).then_inc(*p.inc("cP", 16))
            s.dma_start(out=idf_sb[:], in_=io["idf"]).then_inc(*p.inc("cP", 16))
            s.dma_start(out=T1[0:128, 0:1024], in_=io["dtr"]).then_inc(*p.inc("cP", 16))

        @B.gpsimd
        def _(g):
            g.memset(ONES128[:], 1.0)
            g.memset(S_sb[:], 0.0)
            g.drain().then_inc(*p.inc("cG", 1))

        @B.scalar
        def _(a):
            a.wait_ge(p.sem("cP"), p.at("cP"))
            a.activation(NEGA[:], ALOG[:], AF.Exp)
            a.drain().then_inc(*p.inc("cA", 1))

        @B.vector
        def _(v):
            v.wait_ge(p.sem("cA"), 1)
            v.tensor_scalar_mul(NEGA[:], NEGA[:], -1.0)
            v.tensor_scalar_add(T1[:, 0:1024], T1[:, 0:1024], DTB[:])
            v.drain().then_inc(*p.inc("cV", 1))

        @B.scalar
        def _(a):
            a.wait_ge(p.sem("cV"), 1)
            a.activation(DT[:], T1[:, 0:1024], AF.Abs)
            a.drain()
            a.activation(DT[:], DT[:], AF.Exp, scale=-1.0)
            a.drain()
            a.activation(DT[:], DT[:], AF.Ln, bias=1.0)
            a.drain().then_inc(*p.inc("cA", 1))

        @B.vector
        def _(v):
            v.wait_ge(p.sem("cA"), 2)
            v.tensor_single_scalar(TMP[:], T1[:, 0:1024], 0.0, ALU.max)
            v.drain()
            v.tensor_add(DT[:], DT[:], TMP[:])
            v.drain().then_inc(*p.inc("cV", 1))

    for ct in range(80):
        with nc.Block(f"c2_cv{ct}") as B:
            @B.gpsimd
            def _(g):
                if ct >= 2:
                    g.wait_ge(p.sem("cVc"), ct - 1)
                g.dma_start(out=CIN[:, ct % 2, 0:3],
                            in_=io["halo"][ct * 128:(ct + 1) * 128, :]).then_inc(
                    *p.inc("cW2", 16))
                g.dma_start(out=CIN[:, ct % 2, 3:1027],
                            in_=io["hbc"][ct * 128:(ct + 1) * 128, :]).then_inc(
                    *p.inc("cW2", 16))

            @B.vector
            def _(v):
                v.wait_ge(p.sem("cW2"), 32 * (ct + 1))
                if ct >= 1:
                    v.wait_ge(p.sem("cAc"), ct)
                c = CIN[:, ct % 2, :]
                v.tensor_scalar_mul(ACC1[:], c[:, 0:1024], CW[:, ct, 0:1])
                v.scalar_tensor_tensor(ACC2[:], c[:, 1:1025], CW[:, ct, 1:2], ACC1[:],
                                       ALU.mult, ALU.add)
                v.scalar_tensor_tensor(ACC1[:], c[:, 2:1026], CW[:, ct, 2:3], ACC2[:],
                                       ALU.mult, ALU.add)
                v.scalar_tensor_tensor(ACC2[:], c[:, 3:1027], CW[:, ct, 3:4], ACC1[:],
                                       ALU.mult, ALU.add)
                v.tensor_scalar_add(ACC1[:], ACC2[:], CB[:, ct:ct + 1])
                ins = v.drain()
                ins.then_inc(*p.inc("cVc", 1))

            @B.scalar
            def _(a):
                a.wait_ge(p.sem("cVc"), ct + 1)
                if ct >= 2:
                    a.wait_ge(p.sem("cS"), 16 * (ct - 1))
                a.activation(PO[:, ct % 2, :], ACC1[:], AF.Silu)
                a.drain().then_inc(*p.inc("cAc", 1))

            @B.sync
            def _(s):
                s.wait_ge(p.sem("cAc"), ct + 1)
                s.dma_start(out=io["postc"][ct * 128:(ct + 1) * 128, :],
                            in_=PO[:, ct % 2, :]).then_inc(*p.inc("cS", 16))

    with nc.Block("c2_dto") as B:
        @B.sync
        def _(s):
            s.wait_ge(p.sem("cV"), 2)
            s.dma_start(out=io["dt"], in_=DT[:]).then_inc(*p.inc("cS", 16))

    psx = psum(nc, "c2_psx", [128, 2, 8, 128], BF16)
    idb_sb = sb(nc, "c2_idb", [128, 128], BF16)
    TMPS = sb(nc, "c2_tmps", [128, TH], F32)

    with nc.Block("c2_pre2") as B:
        @B.sync
        def _(s):
            s.dma_start(out=idb_sb[:], in_=io["idb"]).then_inc(*p.inc("cP", 16))

    for c in range(8):
        with nc.Block(f"c2_st{c}") as B:
            zW0 = p.at("sW")
            @B.gpsimd
            def _(g):
                g.wait_ge(p.sem("cS"), 16 * 80)
                if c > 0:
                    g.wait_ge(p.sem("sT"), p.at("sT"))
                g.dma_start(out=XSL[:], in_=io["postc"].rearrange(
                    "(j q) t -> q j t", q=128)[:, 0:64, c * 128:(c + 1) * 128]).then_inc(
                    *p.inc("sW", 16))
                g.dma_start(out=BSL[:], in_=io["postc"].rearrange(
                    "(j q) t -> q j t", q=128)[:, 64:72, c * 128:(c + 1) * 128]).then_inc(
                    *p.inc("sW", 16))

            @B.vector
            def _(v):
                v.wait_ge(p.sem("cV"), 2)
                if c > 0:
                    v.wait_ge(p.sem("sA"), p.at("sA"))
                v.tensor_scalar_mul(ADT[:], DT[:, c * 128:(c + 1) * 128], NEGA[:])
                v.drain()
                v.tensor_tensor_scan(ACUM[:], ADT[:], ONES128[:], 0.0, ALU.add, ALU.mult)
                v.drain()
                v.tensor_scalar(DS[:], ACUM[:], ACUM[:, 127:128], None, ALU.subtract)
                v.drain().then_inc(*p.inc("sV", 1))
            zV_ds = p.at("sV")

            @B.scalar
            def _(a):
                a.wait_ge(p.sem("sV"), zV_ds)
                a.activation(DS[:], DS[:], AF.Exp, scale=-1.0)
                a.activation(DEC[:], ACUM[:, 127:128], AF.Exp)
                a.drain().then_inc(*p.inc("sA", 1))
            zA_ds = p.at("sA")

            @B.vector
            def _(v):
                v.wait_ge(p.sem("sA"), zA_ds)
                v.tensor_mul(XMUL[:], DT[:, c * 128:(c + 1) * 128], DS[:])
                v.drain().then_inc(*p.inc("sV", 1))
            zV_xm = p.at("sV")

            @B.tensor
            def _(t):
                t.wait_ge(p.sem("cP"), p.at("cP"))
                t.wait_ge(p.sem("sV"), zV_xm)
                if c > 0:
                    t.wait_ge(p.sem("sV2"), p.at("sV2"))
                t.transpose(ps_all[:, 0, 0:128], XMUL[:], idf_sb[:])
                t.transpose(ps_all[0:1, 1, 0:128], DEC[:], idf_sb[:]).then_inc(*p.inc("sT", 1))
            zT_tr = p.at("sT")

            @B.vector
            def _(v):
                v.wait_ge(p.sem("sT"), zT_tr)
                v.tensor_copy(XMULT[:], ps_all[:, 0, 0:128])
                v.tensor_copy(DROW[:], ps_all[0:1, 1, 0:128])
                v.drain().then_inc(*p.inc("sV2", 1))
            zV2_cp = p.at("sV2")

            @B.tensor
            def _(t):
                t.wait_ge(p.sem("sV2"), zV2_cp)
                t.matmul(ps_all[:, 1, 0:128], cons["ones_r"][:], DROW[:],
                         start=True, stop=True).then_inc(*p.inc("sT", 1))
            zT_db = p.at("sT")

            @B.vector
            def _(v):
                v.wait_ge(p.sem("sT"), zT_db)
                v.tensor_copy(DECB[:], ps_all[:, 1, 0:128])
                v.drain().then_inc(*p.inc("sV2", 1))
            zV2_db = p.at("sV2")

            zT_xt = []
            @B.tensor
            def _(t):
                t.wait_ge(p.sem("sW"), zW0 + 32)
                t.wait_ge(p.sem("sV2"), zV2_db)
                for bt in range(8):
                    if bt >= 2:
                        t.wait_ge(p.sem("sV2"), zV2_db + bt - 1)
                    for j in range(8):
                        ins = t.transpose(psx[:, bt % 2, j, :], XSL[:, bt * 8 + j, :],
                                          idb_sb[:])
                    ins.then_inc(*p.inc("sT", 1))
                    zT_xt.append(p.at("sT"))
                t.wait_ge(p.sem("sV2"), zV2_db + 7)
                for j in range(8):
                    ins = t.transpose(psx[:, 0, j, :], BSL[:, j, :], idb_sb[:])
                ins.then_inc(*p.inc("sT", 1))
            zT_bt = p.at("sT")

            @B.vector
            def _(v):
                for bt in range(8):
                    v.wait_ge(p.sem("sT"), zT_xt[bt])
                    ins = v.tensor_tensor(
                        XDEC[:, bt * 1024:(bt + 1) * 1024].rearrange(
                            "t (h z) -> t h z", h=16),
                        psx[:, bt % 2, :, :].rearrange("t j z -> t (j z)").rearrange(
                            "t (h z) -> t h z", h=16),
                        XMULT[:, bt * 16:(bt + 1) * 16].unsqueeze(2).broadcast_to(
                            [128, 16, 64]),
                        ALU.mult)
                    ins.then_inc(*p.inc("sV2", 1))
                v.wait_ge(p.sem("sT"), zT_bt)
                v.tensor_copy(BT[:].rearrange("t j z -> t (j z)"),
                              psx[:, 0, :, :].rearrange("t j z -> t (j z)"))
                v.drain().then_inc(*p.inc("sV2", 1))
            zV2_bt = p.at("sV2")

            zT_rw = []
            @B.tensor
            def _(t):
                t.wait_ge(p.sem("sV2"), zV2_bt)
                for g in range(8):
                    bk = 2 + (g % 2) * 2
                    if g >= 2:
                        t.wait_ge(p.sem("sV3"), 8 * c + g - 1)
                    for half in range(2):
                        ins = t.matmul(ps_all[:, bk + half, :], BT[:, g, :],
                                       XDEC[:, g * 1024 + half * TH:
                                            g * 1024 + (half + 1) * TH],
                                       start=True, stop=True)
                    ins.then_inc(*p.inc("sT", 1))
                    zT_rw.append(p.at("sT"))

            @B.vector
            def _(v):
                for g in range(8):
                    bk = 2 + (g % 2) * 2
                    v.wait_ge(p.sem("sT"), zT_rw[g])
                    for half in range(2):
                        gs = slice(g * 1024 + half * TH, g * 1024 + (half + 1) * TH)
                        v.tensor_tensor(TMPS[:].rearrange("n (h z) -> n h z", h=8),
                                        S_sb[:, gs].rearrange("n (h z) -> n h z", h=8),
                                        DECB[:, g * 16 + half * 8:g * 16 + (half + 1) * 8
                                             ].unsqueeze(2).broadcast_to([128, 8, 64]),
                                        ALU.mult)
                        ins = v.tensor_add(S_sb[:, gs], TMPS[:], ps_all[:, bk + half, :])
                    ins.then_inc(*p.inc("sV3", 1))

    with nc.Block("c2_sout") as B:
        @B.sync
        def _(s):
            s.wait_ge(p.sem("sV3"), p.at("sV3"))
            s.dma_start(out=io["send"], in_=S_sb[:]).then_inc(*p.inc("cS", 16))
            s.wait_ge(p.sem("cS"), p.at("cS"))



def emit_l1a3(p, io):
    nc = p.nc
    cons = get_cons(p)

    DT = sb(nc, "a3_dt", [128, TOK], F32)
    NEGA = sb(nc, "a3_nega", [128, 1], F32)
    ALOG = sb(nc, "a3_alog", [128, 1], F32)
    S_sb = sb(nc, "a3_s", [128, 8192], F32)
    SBF = sb(nc, "a3_sbf", [128, 8192], BF16)
    ADT = sb(nc, "a3_adt", [128, 128], F32)
    ACUM = sb(nc, "a3_acum", [128, 128], F32)
    ACUMT = sb(nc, "a3_acumt", [128, 128], F32)
    DTT = sb(nc, "a3_dtt", [128, 128], F32)
    EPOS = sb(nc, "a3_epos", [128, 128], F32)
    DS = sb(nc, "a3_ds", [128, 128], F32)
    DST = sb(nc, "a3_dst", [128, 128], F32)
    DEC = sb(nc, "a3_dec", [128, 1], F32)
    DECB = sb(nc, "a3_decb", [128, 128], F32)
    DROW = sb(nc, "a3_drow", [1, 128], F32)
    ONES128 = sb(nc, "a3_ones", [128, 128], F32)
    XSL = sb(nc, "a3_xsl", [128, 64, 128], BF16)
    BCSL = sb(nc, "a3_bcsl", [128, 16, 128], BF16)
    XDT = sb(nc, "a3_xdt", [128, 8192], BF16)
    BT = sb(nc, "a3_bt", [128, 8, 128], BF16)
    GMT = sb(nc, "a3_gmt", [128, 8, 128], BF16)
    ROWS = sb(nc, "a3_rows", [1, 2, 1024], F32)
    MREP = sb(nc, "a3_mrep", [128, 1024], BF16)
    DIFF = sb(nc, "a3_diff", [128, 2, 1024], F32)
    LT8 = sb(nc, "a3_lt8", [128, 2, 1024], BF16)
    MT_Q = sb(nc, "a3_mtq", [128, 4, 1024], BF16)
    YTM = sb(nc, "a3_ytm", [128, 2048], BF16)
    TMPS = sb(nc, "a3_tmps", [128, TH], F32)
    DVEC = sb(nc, "a3_dvec", [128, 64], F32)
    NRMW = sb(nc, "a3_nrmw", [128, 64], F32)
    idf_sb = sb(nc, "a3_idf", [128, 128], F32)
    idb_sb = sb(nc, "a3_idb", [128, 128], BF16)
    RMSV = sb(nc, "a3_rmsv", [1, TH], F32)
    RSTB = sb(nc, "a3_rstb", [128, TH], F32)
    YSQ = overlay(nc, "a3_ysq", [128, 8, TH], BF16, nc.lookup_mloc(DIFF).addr)
    SG = overlay(nc, "a3_sg", [128, 2, TH], BF16, nc.lookup_mloc(LT8).addr)
    GT = overlay(nc, "a3_gt", [128, 2, TH], BF16, nc.lookup_mloc(LT8).addr + 2048)
    T2 = overlay(nc, "a3_t2", [128, TH], BF16, nc.lookup_mloc(MT_Q).addr)
    STG = overlay(nc, "a3_stg", [128, 4, TH], F32, nc.lookup_mloc(XDT).addr)
    XR = overlay(nc, "a3_xr", [128, 4, TH], F32, nc.lookup_mloc(XDT).addr + 8192)
    WP = overlay(nc, "a3_wp", [128, 2, 8, TH], BF16, nc.lookup_mloc(XSL).addr)

    baseY = arena(nc, "a3_arY", 65536)
    y_fm = overlay(nc, "a3_yfm", [128, 64, TH], BF16, baseY)

    ps0 = psum(nc, "a3_ps0", [128, 1, 128], F32)
    psx = psum(nc, "a3_psx", [128, 2, 8, 128], BF16)
    psg = psum(nc, "a3_psg", [128, 4, 128], F32)
    ps4 = psum(nc, "a3_ps4", [128, 4, TH], F32)

    scr = nc.dram_tensor("a3_scr", [128, 128], F32, kind="Internal").ap()

    with nc.Block("a3_pre") as B:
        @B.sync
        def _(s):
            s.dma_start(out=DT[:], in_=io["dt"]).then_inc(*p.inc("yP", 16))
            s.dma_start(out=ALOG[:], in_=io["alog"]).then_inc(*p.inc("yP", 16))
            s.dma_start(out=idf_sb[:], in_=io["idf"]).then_inc(*p.inc("yP", 16))
            s.dma_start(out=idb_sb[:], in_=io["idb"]).then_inc(*p.inc("yP", 16))
            s.dma_start(out=MREP[:], in_=io["mrep"]).then_inc(*p.inc("yP", 16))
            s.dma_start(out=DVEC[:], in_=io["dvec"]).then_inc(*p.inc("yP", 16))
            s.dma_start(out=NRMW[:], in_=io["normw"]).then_inc(*p.inc("yP", 16))
            s.dma_start(out=S_sb[:], in_=io["sinit"]).then_inc(*p.inc("yP", 16))

        @B.gpsimd
        def _(g):
            g.memset(ONES128[:], 1.0)
            g.drain().then_inc(*p.inc("yG", 1))

        @B.scalar
        def _(a):
            a.wait_ge(p.sem("yP"), p.at("yP"))
            a.activation(NEGA[:], ALOG[:], AF.Exp)
            a.drain().then_inc(*p.inc("yA", 1))

        @B.vector
        def _(v):
            v.wait_ge(p.sem("yA"), 1)
            v.tensor_scalar_mul(NEGA[:], NEGA[:], -1.0)
            v.drain()
            v.tensor_copy(SBF[:, 0:4096], S_sb[:, 0:4096])
            v.tensor_copy(SBF[:, 4096:8192], S_sb[:, 4096:8192])
            v.drain().then_inc(*p.inc("yV", 1))

    for st in range(2):
        for cc in range(4):
            ci = st * 4 + cc
            cs = slice(ci * 128, (ci + 1) * 128)
            with nc.Block(f"a3_c{ci}") as B:
                @B.gpsimd
                def _(g):
                    if ci > 0:
                        g.wait_ge(p.sem("yVS"), 8 * ci)
                        g.wait_ge(p.sem("yVT"), 8 * ci)
                    if ci == 4:
                        g.wait_ge(p.sem("yTO"), 64)
                    g.dma_start(out=XSL[:], in_=io["postc"].rearrange(
                        "(j q) t -> q j t", q=128)[:, 0:64, cs]).then_inc(*p.inc("yW", 16))
                    g.dma_start(out=BCSL[:], in_=io["postc"].rearrange(
                        "(j q) t -> q j t", q=128)[:, 64:80, cs]).then_inc(*p.inc("yW", 16))
                zW_sl = p.at("yW")

                @B.vector
                def _(v):
                    v.wait_ge(p.sem("yV"), p.at("yV"))
                    if ci > 0:
                        v.wait_ge(p.sem("yA"), p.at("yA"))
                    v.tensor_scalar_mul(ADT[:], DT[:, cs], NEGA[:])
                    v.drain()
                    v.tensor_tensor_scan(ACUM[:], ADT[:], ONES128[:], 0.0, ALU.add, ALU.mult)
                    v.drain()
                    v.tensor_scalar(DS[:], ACUM[:], ACUM[:, 127:128], None, ALU.subtract)
                    v.drain().then_inc(*p.inc("yV", 1))
                zV_ds = p.at("yV")

                @B.sync
                def _(s):
                    s.wait_ge(p.sem("yV"), zV_ds)
                    s.wait_ge(p.sem("yS"), p.at("yS"))
                    s.dma_start(out=scr, in_=ACUM[:]).then_inc(*p.inc("yS", 16))
                zS_acum = p.at("yS")

                @B.tensor
                def _(t):
                    t.wait_ge(p.sem("yV"), zV_ds)
                    if ci > 0:
                        t.wait_ge(p.sem("yV2"), p.at("yV2"))
                    t.transpose(ps0[:, 0, :], ACUM[:], idf_sb[:]).then_inc(*p.inc("yT", 1))
                zT_a = p.at("yT")

                @B.vector
                def _(v):
                    v.wait_ge(p.sem("yT"), zT_a)
                    v.tensor_copy(ACUMT[:], ps0[:, 0, :])
                    v.drain().then_inc(*p.inc("yV2", 1))
                zV2_a = p.at("yV2")

                @B.tensor
                def _(t):
                    t.wait_ge(p.sem("yV2"), zV2_a)
                    t.transpose(ps0[:, 0, :], DT[:, cs], idf_sb[:]).then_inc(*p.inc("yT", 1))
                zT_d = p.at("yT")

                @B.vector
                def _(v):
                    v.wait_ge(p.sem("yT"), zT_d)
                    v.tensor_copy(DTT[:], ps0[:, 0, :])
                    v.drain().then_inc(*p.inc("yV2", 1))
                zV2_d = p.at("yV2")

                @B.scalar
                def _(a):
                    a.wait_ge(p.sem("yV2"), zV2_d)
                    if ci > 0:
                        a.wait_ge(p.sem("yVY"), 4 * ci)
                    a.activation(EPOS[:], ACUMT[:], AF.Exp)
                    a.activation(DS[:], DS[:], AF.Exp, scale=-1.0)
                    a.activation(DEC[:], ACUM[:, 127:128], AF.Exp)
                    a.drain().then_inc(*p.inc("yA", 1))
                zA_e = p.at("yA")

                @B.tensor
                def _(t):
                    t.wait_ge(p.sem("yA"), zA_e)
                    t.transpose(ps0[:, 0, :], DS[:], idf_sb[:]).then_inc(*p.inc("yT", 1))
                zT_ds = p.at("yT")

                @B.vector
                def _(v):
                    v.wait_ge(p.sem("yT"), zT_ds)
                    v.tensor_copy(DST[:], ps0[:, 0, :])
                    v.drain().then_inc(*p.inc("yV2", 1))
                zV2_ds = p.at("yV2")

                @B.tensor
                def _(t):
                    t.wait_ge(p.sem("yV2"), zV2_ds)
                    t.transpose(ps0[0:1, 0, :], DEC[:], idf_sb[:]).then_inc(*p.inc("yT", 1))
                zT_dr = p.at("yT")

                @B.vector
                def _(v):
                    v.wait_ge(p.sem("yT"), zT_dr)
                    v.tensor_copy(DROW[:], ps0[0:1, 0, :])
                    v.drain().then_inc(*p.inc("yV2", 1))
                zV2_dr = p.at("yV2")

                @B.tensor
                def _(t):
                    t.wait_ge(p.sem("yV2"), zV2_dr)
                    t.matmul(ps0[:, 0, :], cons["ones_r"][:], DROW[:],
                             start=True, stop=True).then_inc(*p.inc("yT", 1))
                zT_db = p.at("yT")

                @B.vector
                def _(v):
                    v.wait_ge(p.sem("yT"), zT_db)
                    v.tensor_copy(DECB[:], ps0[:, 0, :])
                    v.drain().then_inc(*p.inc("yV2", 1))
                zV2_db = p.at("yV2")

                zT_xt = []
                @B.tensor
                def _(t):
                    t.wait_ge(p.sem("yW"), zW_sl)
                    t.wait_ge(p.sem("yV2"), zV2_db)
                    t.wait_ge(p.sem("yVT"), 8 * ci)
                    for bt in range(8):
                        if bt >= 2:
                            t.wait_ge(p.sem("yV2"), zV2_db + bt - 1)
                        for j in range(8):
                            ins = t.transpose(psx[:, bt % 2, j, :],
                                              XSL[:, bt * 8 + j, :], idb_sb[:])
                        ins.then_inc(*p.inc("yT", 1))
                        zT_xt.append(p.at("yT"))

                @B.vector
                def _(v):
                    if ci == 4:
                        v.wait_ge(p.sem("ySO"), 512)
                        v.wait_ge(p.sem("yVO"), 32)
                    for bt in range(8):
                        v.wait_ge(p.sem("yT"), zT_xt[bt])
                        ins = v.tensor_tensor(
                            XDT[:, bt * 1024:(bt + 1) * 1024].rearrange(
                                "t (h z) -> t h z", h=16),
                            psx[:, bt % 2, :, :].rearrange("t j z -> t (j z)").rearrange(
                                "t (h z) -> t h z", h=16),
                            DTT[:, bt * 16:(bt + 1) * 16].unsqueeze(2).broadcast_to(
                                [128, 16, 64]),
                            ALU.mult)
                        ins.then_inc(*p.inc("yV2", 1))
                zV2_xt = p.at("yV2")

                @B.tensor
                def _(t):
                    t.wait_ge(p.sem("yW"), zW_sl)
                    for gq in range(2):
                        if ci > 0 or gq > 0:
                            t.wait_ge(p.sem("yV2"), zV2_xt + gq)
                        for g4 in range(4):
                            g = gq * 4 + g4
                            ins = t.matmul(psg[:, g4, :], BCSL[:, g, :],
                                           BCSL[:, 8 + g, :], start=True, stop=True)
                        ins.then_inc(*p.inc("yT", 1))
                zT_gm = p.at("yT")

                @B.vector
                def _(v):
                    for gq in range(2):
                        v.wait_ge(p.sem("yT"), zT_gm - 1 + gq)
                        ins = v.tensor_copy(
                            GMT[:, gq * 4:(gq + 1) * 4, :].rearrange("s j l -> s (j l)"),
                            psg[:].rearrange("s j l -> s (j l)"))
                        ins.then_inc(*p.inc("yV2", 1))
                zV2_gm = p.at("yV2")

                @B.tensor
                def _(t):
                    t.wait_ge(p.sem("yV2"), zV2_xt)
                    for j in range(8):
                        ins = t.transpose(psx[:, 0, j, :], BCSL[:, j, :], idb_sb[:])
                    ins.then_inc(*p.inc("yT", 1))
                zT_bt = p.at("yT")

                @B.vector
                def _(v):
                    v.wait_ge(p.sem("yT"), zT_bt)
                    v.tensor_copy(BT[:].rearrange("t j z -> t (j z)"),
                                  psx[:, 0, :, :].rearrange("t j z -> t (j z)"))
                    v.drain().then_inc(*p.inc("yV2", 1))
                zV2_bt = p.at("yV2")

                pass

            for Q in range(4):
                with nc.Block(f"a3_c{ci}q{Q}") as BQ:
                    zS_rows = []
                    zT_lm = []
                    zV_mt = []
                    @BQ.sync
                    def _(s):
                        s.wait_ge(p.sem("yS"), zS_acum)
                        for hh in range(4):
                            h8 = Q * 4 + hh
                            if 16 * ci + Q * 4 + hh >= 2:
                                s.wait_ge(p.sem("yTL"), 16 * ci + Q * 4 + hh - 1)
                            s.dma_start(
                                out=ROWS[:, h8 % 2, :],
                                in_=scr[h8 * 8:(h8 + 1) * 8, :].rearrange(
                                    "h t -> (h t)").unsqueeze(0)).then_inc(
                                *p.inc("yS", 16))
                            zS_rows.append(p.at("yS"))

                    @BQ.tensor
                    def _(t):
                        if Q > 0:
                            t.wait_ge(p.sem("yVY"), 4 * ci + Q)
                        elif ci > 0:
                            t.wait_ge(p.sem("yVS"), 8 * ci)
                            if ci == 4:
                                t.wait_ge(p.sem("yVO"), 32)
                        for hh in range(4):
                            h8 = Q * 4 + hh
                            rot = hh % 2
                            t.wait_ge(p.sem("yS"), zS_rows[hh])
                            idx = Q * 4 + hh
                            if idx < 2:
                                if ci > 0:
                                    t.wait_ge(p.sem("yVL"), 16 * ci)
                            else:
                                t.wait_ge(p.sem("yVL"), 16 * ci + idx - 1)
                            for half in range(2):
                                hs = slice(half * TH, (half + 1) * TH)
                                t.matmul(ps4[:, rot * 2 + half, :], cons["ones_r"][:],
                                         ROWS[:, h8 % 2, hs], start=True, stop=False)
                                ins = t.matmul(ps4[:, rot * 2 + half, :], idb_sb[:],
                                               MREP[:, hs], start=False, stop=True)
                            ins.then_inc(*p.inc("yTL", 1))
                            zT_lm.append(p.at("yTL"))

                    @BQ.vector
                    def _(v):
                        for hh in range(4):
                            h8 = Q * 4 + hh
                            rot = hh % 2
                            idx = Q * 4 + hh
                            v.wait_ge(p.sem("yTL"), zT_lm[hh])
                            if idx >= 2:
                                v.wait_ge(p.sem("yAL"), 16 * ci + idx - 1)
                            ins = v.tensor_tensor(
                                DIFF[:, rot, :].rearrange("s (h l) -> s h l", h=8),
                                ps4[:, rot * 2:rot * 2 + 2, :].rearrange(
                                    "s b l -> s (b l)").rearrange(
                                    "s (h l) -> s h l", h=8),
                                ACUMT[:, h8 * 8:(h8 + 1) * 8].unsqueeze(2).broadcast_to(
                                    [128, 8, 128]),
                                ALU.subtract)
                            ins.then_inc(*p.inc("yVL", 1))

                    @BQ.scalar
                    def _(a):
                        for hh in range(4):
                            rot = hh % 2
                            idx = Q * 4 + hh
                            a.wait_ge(p.sem("yVL"), 16 * ci + idx + 1)
                            if idx >= 2:
                                a.wait_ge(p.sem("yVM"), 16 * ci + idx - 1)
                            a.activation(LT8[:, rot, :], DIFF[:, rot, :], AF.Exp)
                            a.drain().then_inc(*p.inc("yAL", 1))

                    @BQ.vector
                    def _(v):
                        for hh in range(4):
                            h8 = Q * 4 + hh
                            rot = hh % 2
                            g = h8 // 2
                            v.wait_ge(p.sem("yAL"), 16 * ci + Q * 4 + hh + 1)
                            ins = v.tensor_tensor(
                                MT_Q[:, hh, :].rearrange("s (h l) -> s h l", h=8),
                                LT8[:, rot, :].rearrange("s (h l) -> s h l", h=8),
                                GMT[:, g, :].unsqueeze(1).broadcast_to([128, 8, 128]),
                                ALU.mult)
                            ins.then_inc(*p.inc("yVM", 1))

                    @BQ.tensor
                    def _(t):
                        t.wait_ge(p.sem("yVM"), 16 * ci + 4 * (Q + 1))
                        t.wait_ge(p.sem("yV2"), zV2_xt)
                        t.wait_ge(p.sem("yVS"), 8 * ci)
                        for gg in range(2):
                            g = Q * 2 + gg
                            for half in range(2):
                                ins = t.matmul(
                                    ps4[:, gg * 2 + half, :],
                                    BCSL[:, 8 + g, :],
                                    SBF[:, g * 1024 + half * TH:g * 1024 + (half + 1) * TH],
                                    start=True, stop=True)
                        ins.then_inc(*p.inc("yTY1", 1))

                    @BQ.vector
                    def _(v):
                        v.wait_ge(p.sem("yTY1"), 4 * ci + Q + 1)
                        for b4 in range(4):
                            hbase = Q * 32 + b4 * 8
                            ins = v.tensor_tensor(
                                YTM[:, b4 * 512:(b4 + 1) * 512].rearrange(
                                    "t (h z) -> t h z", h=8),
                                ps4[:, b4, :].rearrange("t (h z) -> t h z", h=8),
                                EPOS[:, hbase:hbase + 8].unsqueeze(2).broadcast_to(
                                    [128, 8, 64]),
                                ALU.mult)
                        ins.then_inc(*p.inc("yVY1", 1))

                    @BQ.tensor
                    def _(t):
                        t.wait_ge(p.sem("yVY1"), 4 * ci + Q + 1)
                        for hh in range(4):
                            for j in range(8):
                                h = Q * 32 + hh * 8 + j
                                ins = t.matmul(
                                    ps4[:, hh, j * 64:(j + 1) * 64],
                                    MT_Q[:, hh, j * 128:(j + 1) * 128],
                                    XDT[:, h * 64:(h + 1) * 64],
                                    start=(j == 0),
                                    stop=(j == 7))
                        ins.then_inc(*p.inc("yTY", 1))

                    @BQ.vector
                    def _(v):
                        v.wait_ge(p.sem("yTY"), 4 * ci + Q + 1)
                        for b4 in range(4):
                            ins = v.tensor_add(YTM[:, b4 * 512:(b4 + 1) * 512],
                                               ps4[:, b4, :],
                                               YTM[:, b4 * 512:(b4 + 1) * 512])
                        ins.then_inc(*p.inc("yVY", 1))

                    @BQ.tensor
                    def _(t):
                        t.wait_ge(p.sem("yVY"), 4 * ci + Q + 1)
                        for bt in range(2):
                            if True:
                                t.wait_ge(p.sem("yVT"), p.at("yVT"))
                            for j in range(8):
                                ins = t.transpose(psx[:, bt, j, :],
                                                  YTM[:, (bt * 8 + j) * 128:
                                                      (bt * 8 + j + 1) * 128], idb_sb[:])
                            ins.then_inc(*p.inc("yTT", 1))

                    @BQ.vector
                    def _(v):
                        if ci == 4 and Q == 0:
                            v.wait_ge(p.sem("yTO"), 64)
                        for bt in range(2):
                            v.wait_ge(p.sem("yTT"), 8 * ci + Q * 2 + bt + 1)
                            for j in range(8):
                                tile = Q * 16 + bt * 8 + j
                                ins = v.scalar_tensor_tensor(
                                    y_fm[:, tile, cc * 128:(cc + 1) * 128],
                                    XSL[:, tile, :], DVEC[:, tile:tile + 1],
                                    psx[:, bt, j, :], ALU.mult, ALU.add)
                            ins.then_inc(*p.inc("yVT", 1))

            with nc.Block(f"a3_c{ci}s") as BS:
                @BS.vector
                def _(v):
                    v.wait_ge(p.sem("yTY"), 4 * ci + 4)
                    for bt in range(8):
                        ins = v.tensor_tensor(
                            XDT[:, bt * 1024:(bt + 1) * 1024].rearrange(
                                "t (h z) -> t h z", h=16),
                            XDT[:, bt * 1024:(bt + 1) * 1024].rearrange(
                                "t (h z) -> t h z", h=16),
                            DST[:, bt * 16:(bt + 1) * 16].unsqueeze(2).broadcast_to(
                                [128, 16, 64]),
                            ALU.mult)
                    ins.then_inc(*p.inc("yVD", 1))
                zV_xd = p.at("yVD")

                zT_rw = []
                @BS.tensor
                def _(t):
                    t.wait_ge(p.sem("yVD"), zV_xd)
                    t.wait_ge(p.sem("yVY"), 4 * ci + 4)
                    for g in range(8):
                        bk = (g % 2) * 2
                        if g >= 2:
                            t.wait_ge(p.sem("yVS"), 8 * ci + g - 1)
                        for half in range(2):
                            ins = t.matmul(ps4[:, bk + half, :], BT[:, g, :],
                                           XDT[:, g * 1024 + half * TH:
                                               g * 1024 + (half + 1) * TH],
                                           start=True, stop=True)
                        ins.then_inc(*p.inc("yTS", 1))
                        zT_rw.append(p.at("yTS"))

                @BS.vector
                def _(v):
                    for g in range(8):
                        bk = (g % 2) * 2
                        v.wait_ge(p.sem("yTS"), zT_rw[g])
                        for half in range(2):
                            gs = slice(g * 1024 + half * TH, g * 1024 + (half + 1) * TH)
                            v.tensor_tensor(
                                TMPS[:].rearrange("n (h z) -> n h z", h=8),
                                S_sb[:, gs].rearrange("n (h z) -> n h z", h=8),
                                DECB[:, g * 16 + half * 8:g * 16 + (half + 1) * 8
                                     ].unsqueeze(2).broadcast_to([128, 8, 64]),
                                ALU.mult)
                            v.tensor_add(S_sb[:, gs], TMPS[:], ps4[:, bk + half, :])
                            ins = v.tensor_copy(SBF[:, gs], S_sb[:, gs])
                        ins.then_inc(*p.inc("yVS", 1))

        with nc.Block(f"a3_t{st}") as BT_:
            @BT_.scalar
            def _(a):
                a.wait_ge(p.sem("yVT"), p.at("yVT"))
                for j in range(64):
                    if j >= 8:
                        a.wait_ge(p.sem("yTR"), 64 * st + j - 7)
                    a.activation(YSQ[:, j % 8, :], y_fm[:, j, :], AF.Square)
                    a.drain().then_inc(*p.inc("yAR", 1))

            @BT_.tensor
            def _(t):
                t.wait_ge(p.sem("yVS"), 32 * (st + 1))
                for j in range(64):
                    t.wait_ge(p.sem("yAR"), 64 * st + j + 1)
                    ins = t.matmul(ps4[0:1, 0, :], cons["ones_b"][:], YSQ[:, j % 8, :],
                                   start=(j == 0), stop=(j == 63))
                    ins.then_inc(*p.inc("yTR", 1))

            @BT_.vector
            def _(v):
                v.wait_ge(p.sem("yTR"), 64 * (st + 1))
                v.tensor_scalar(RMSV[:], ps4[0:1, 0, :], 1.0 / 8192, 1e-5,
                                ALU.mult, ALU.add)
                v.drain().then_inc(*p.inc("yVR", 1))

            @BT_.scalar
            def _(a):
                a.wait_ge(p.sem("yVR"), 3 * st + 1)
                a.activation(RMSV[:], RMSV[:], AF.Sqrt)
                a.drain().then_inc(*p.inc("yAR2", 1))

            @BT_.vector
            def _(v):
                v.wait_ge(p.sem("yAR2"), st + 1)
                v.reciprocal(RMSV[:], RMSV[:])
                v.drain().then_inc(*p.inc("yVR", 1))

            @BT_.tensor
            def _(t):
                t.wait_ge(p.sem("yVR"), 3 * st + 2)
                t.matmul(ps4[:, 1, 0:TH], cons["ones_r"][:], RMSV[:],
                         start=True, stop=True).then_inc(*p.inc("yTR2", 1))

            @BT_.vector
            def _(v):
                v.wait_ge(p.sem("yTR2"), st + 1)
                v.tensor_copy(RSTB[:], ps4[:, 1, 0:TH])
                v.drain().then_inc(*p.inc("yVR", 1))
            zV_rst = p.at("yVR")

            @BT_.sync
            def _(s):
                for j in range(64):
                    if j >= 2:
                        s.wait_ge(p.sem("yAG"), 64 * st + j - 1)
                    s.dma_start(out=GT[:, j % 2, :],
                                in_=io["gate"][j * 128:(j + 1) * 128,
                                               st * TH:(st + 1) * TH]).then_inc(
                        *p.inc("ySG", 16))

            @BT_.scalar
            def _(a):
                for j in range(64):
                    a.wait_ge(p.sem("ySG"), 1024 * st + 16 * (j + 1))
                    if j >= 2:
                        a.wait_ge(p.sem("yVG"), 64 * st + j - 1)
                    a.activation(SG[:, j % 2, :], GT[:, j % 2, :], AF.Silu)
                    a.drain().then_inc(*p.inc("yAG", 1))

            @BT_.vector
            def _(v):
                v.wait_ge(p.sem("yVR"), zV_rst)
                for j in range(64):
                    v.wait_ge(p.sem("yAG"), 64 * st + j + 1)
                    v.scalar_tensor_tensor(T2[:], y_fm[:, j, :], NRMW[:, j:j + 1],
                                           RSTB[:], ALU.mult, ALU.mult)
                    ins = v.tensor_mul(y_fm[:, j, :], T2[:], SG[:, j % 2, :])
                    ins.then_inc(*p.inc("yVG", 1))
            zV_g = p.at("yVG")

            @BT_.gpsimd
            def _(g):
                g.wait_ge(p.sem("yVT"), 32 * (st + 1))
                g.wait_ge(p.sem("yVS"), 32 * (st + 1))
                for og in range(8):
                    for chp in range(8):
                        pi = og * 8 + chp
                        if 64 * st + pi >= 2:
                            g.wait_ge(p.sem("yTO"), 64 * st + pi - 1)
                        g.dma_start(
                            out=WP[:, pi % 2, :, :],
                            in_=io["wo"][chp * 1024:(chp + 1) * 1024,
                                         og * 512:(og + 1) * 512].rearrange(
                                "(j q) d -> q j d", q=128)).then_inc(*p.inc("yWO", 16))

            @BT_.scalar
            def _(a):
                for og in range(8):
                    for dt4 in range(4):
                        ri = og * 4 + dt4
                        if st > 0 or ri >= 4:
                            a.wait_ge(p.sem("yVO"), 32 * st + ri - 3)
                        a.dma_start(out=XR[:, ri % 4, :],
                                    in_=io["x_in"][(og * 4 + dt4) * 128:
                                                   (og * 4 + dt4 + 1) * 128,
                                                   st * TH:(st + 1) * TH]).then_inc(
                            *p.inc("yXR", 16))

            @BT_.tensor
            def _(t):
                t.wait_ge(p.sem("yVG"), zV_g)
                t.wait_ge(p.sem("yVS"), 32 * (st + 1))
                for og in range(8):
                    if og >= 1:
                        t.wait_ge(p.sem("yVO"), 32 * st + 4 * og)
                    elif st > 0:
                        t.wait_ge(p.sem("yVO"), 32 * st)
                    for chp in range(8):
                        pi = og * 8 + chp
                        t.wait_ge(p.sem("yWO"), 1024 * st + 16 * (pi + 1))
                        for j in range(8):
                            for dt4 in range(4):
                                ins = t.matmul(ps4[:, dt4, :],
                                               WP[:, pi % 2, j, dt4 * 128:(dt4 + 1) * 128],
                                               y_fm[:, chp * 8 + j, :],
                                               start=(chp == 0 and j == 0),
                                               stop=(chp == 7 and j == 7))
                        ins.then_inc(*p.inc("yTO", 1))

            @BT_.vector
            def _(v):
                for og in range(8):
                    for dt4 in range(4):
                        ri = og * 4 + dt4
                        v.wait_ge(p.sem("yTO"), 64 * st + 8 * (og + 1))
                        v.wait_ge(p.sem("yXR"), 512 * st + 16 * (ri + 1))
                        if st > 0 or ri >= 4:
                            v.wait_ge(p.sem("ySO"), 512 * st + 16 * (ri - 3))
                        ins = v.tensor_add(STG[:, ri % 4, :], ps4[:, dt4, :],
                                           XR[:, ri % 4, :])
                        ins.then_inc(*p.inc("yVO", 1))

            @BT_.sync
            def _(s):
                for og in range(8):
                    for dt4 in range(4):
                        ri = og * 4 + dt4
                        s.wait_ge(p.sem("yVO"), 32 * st + ri + 1)
                        s.dma_start(out=io["x_out"][(og * 4 + dt4) * 128:
                                                    (og * 4 + dt4 + 1) * 128,
                                                    st * TH:(st + 1) * TH],
                                    in_=STG[:, ri % 4, :]).then_inc(*p.inc("ySO", 16))

    with nc.Block("a3_fin") as B:
        @B.sync
        def _(s):
            s.wait_ge(p.sem("ySO"), p.at("ySO"))

    return p



def _c(a):
    return np.ascontiguousarray(a)


def build_l0attn(upto=5):
    nc = bass.Bass("TRN2", target_bir_lowering=False)
    io = {
        "x_in": nc.dram_tensor("x_in", [D, TOK], F32, kind="ExternalInput").ap(),
        "wq": nc.dram_tensor("wq", [D, 1024], F32, kind="ExternalInput").ap(),
        "bq": nc.dram_tensor("bq", [128, 8], F32, kind="ExternalInput").ap(),
        "m": nc.dram_tensor("m", [8, 128, 128], F32, kind="ExternalInput").ap(),
        "z": nc.dram_tensor("z", [8, 128], F32, kind="ExternalInput").ap(),
        "wo4": nc.dram_tensor("wo4", [8, 4, 128, 4096], F32, kind="ExternalInput").ap(),
        "idb": nc.dram_tensor("idb", [128, 128], BF16, kind="ExternalInput").ap(),
        "x_out": nc.dram_tensor("x_out", [D, TOK], F32, kind="ExternalOutput").ap(),
    }
    p = Prog(nc)
    emit_l0attn(p, io, upto=upto)
    return nc


def host_prep_l0attn(inputs):
    Wq = np.asarray(inputs["L0_attn_Wq"], np.float32)
    sel = np.concatenate([np.arange(4 * h * 128, 4 * h * 128 + 128) for h in range(8)])
    return {
        "wq": _c(Wq.T[:, sel]),
        "bq": _c(np.asarray(inputs["L0_attn_bq"], np.float32)[sel].reshape(8, 128).T),
        "m": _c(np.asarray(inputs["L0_attn_M"], np.float32)),
        "z": _c(np.asarray(inputs["L0_attn_z"], np.float32)),
        "wo4": _c(np.asarray(inputs["L0_attn_Wo"], np.float32).T.reshape(8, 4, 128, 4096)),
        "idb": np.eye(128, dtype=ml_dtypes.bfloat16),
    }


def build_ffn(cfg, dbg=False):
    nc = bass.Bass("TRN2", target_bir_lowering=False)
    E, ER = cfg["E"], cfg["ER"]
    io = {
        "x_in": nc.dram_tensor("x_in", [D, TOK], F32, kind="ExternalInput").ap(),
        "wb": nc.dram_tensor("wb", [D, D], F32, kind="ExternalInput").ap(),
        "wr": nc.dram_tensor("wr", [D, 8], F32, kind="ExternalInput").ap(),
        "at": nc.dram_tensor("at", [D, ER], F32, kind="ExternalInput").ap(),
        "bcat": nc.dram_tensor("bcat", [ER, D], F32, kind="ExternalInput").ap(),
        "e8": nc.dram_tensor("e8", [E, 128], F32, kind="ExternalInput").ap(),
        "idf": nc.dram_tensor("idf", [128, 128], F32, kind="ExternalInput").ap(),
        "x_out": nc.dram_tensor("x_out", [D, TOK], F32, kind="ExternalOutput").ap(),
    }
    if cfg["base_bias"]:
        io["bb"] = nc.dram_tensor("bb", [128, 32], F32, kind="ExternalInput").ap()
    if cfg["logit_bias"]:
        io["eb"] = nc.dram_tensor("eb", [E], F32, kind="ExternalInput").ap()
    if dbg:
        io["dbg_logits"] = nc.dram_tensor("dbg_logits", [8, TOK], F32, kind="ExternalOutput").ap()
        io["dbg_comb"] = nc.dram_tensor("dbg_comb", [8, TOK], F32, kind="ExternalOutput").ap()
        io["dbg_hl"] = nc.dram_tensor("dbg_hl", [128, TOK], mybir.dt.bfloat16, kind="ExternalOutput").ap()
        io["dbg_hw"] = nc.dram_tensor("dbg_hw", [128, TOK], mybir.dt.bfloat16, kind="ExternalOutput").ap()
        io["dbg_lt"] = nc.dram_tensor("dbg_lt", [128, 48], F32, kind="ExternalOutput").ap()
    p = Prog(nc)
    emit_ffn(p, io, cfg)
    return nc


CFG_L0FFN = dict(tag="f0", E=8, ER=128, top1=False, scaling=2.0, base_bias=True,
                 logit_bias=False)
CFG_L1FFN = dict(tag="f1", E=4, ER=32, top1=True, scaling=1.0, base_bias=False,
                 logit_bias=True)


def host_prep_l0ffn(inputs):
    A = np.asarray(inputs["L0_ffn_A"], np.float32)
    Bm = np.asarray(inputs["L0_ffn_B"], np.float32)
    e8 = np.zeros((8, 128), np.float32)
    for e in range(8):
        e8[e, e * 16:(e + 1) * 16] = 1.0
    return {
        "wb": _c(np.asarray(inputs["L0_ffn_W_base"], np.float32).T),
        "wr": _c(np.asarray(inputs["L0_ffn_W_router"], np.float32).T),
        "at": _c(A.reshape(128, D).T),
        "bcat": _c(Bm.transpose(0, 2, 1).reshape(128, D)),
        "e8": e8,
        "idf": np.eye(128, dtype=np.float32),
        "bb": _c(np.asarray(inputs["L0_ffn_b_base"], np.float32).reshape(32, 128).T),
    }


def host_prep_l1ffn(inputs):
    A = np.asarray(inputs["L1_ffn_lora_A"], np.float32)
    Bm = np.asarray(inputs["L1_ffn_lora_B"], np.float32)
    e8 = np.zeros((4, 128), np.float32)
    for e in range(4):
        e8[e, e * 8:(e + 1) * 8] = 1.0
    return {
        "wb": _c(np.asarray(inputs["L1_ffn_W_up"], np.float32).T),
        "wr": _c(np.concatenate([np.asarray(inputs["L1_ffn_gate_w"], np.float32).T,
                                 np.zeros((D, 4), np.float32)], axis=1)),
        "at": _c(A.reshape(32, D).T),
        "bcat": _c(Bm.transpose(0, 2, 1).reshape(32, D)),
        "e8": e8,
        "idf": np.eye(128, dtype=np.float32),
        "eb": _c(np.asarray(inputs["L1_ffn_expert_bias"], np.float32)),
    }


def run_ffn(x_fm, shared, cfg):
    nc = build_ffn(cfg)
    in_maps = []
    for c in range(8):
        m = dict(shared)
        m["x_in"] = _c(x_fm[:, c * TOK:(c + 1) * TOK])
        in_maps.append(m)
    res = run_bass_kernel_spmd(nc, in_maps, core_ids=list(range(8)))
    return np.concatenate([res.results[c]["x_out"] for c in range(8)], axis=1)


def build_l1a1():
    nc = bass.Bass("TRN2", target_bir_lowering=False)
    io = {
        "x_in": nc.dram_tensor("x_in", [D, TOK], F32, kind="ExternalInput").ap(),
        "wi": nc.dram_tensor("wi", [D, 18560], F32, kind="ExternalInput").ap(),
        "gate": nc.dram_tensor("gate", [8192, TOK], BF16, kind="ExternalOutput").ap(),
        "hbc": nc.dram_tensor("hbc", [10240, TOK], BF16, kind="ExternalOutput").ap(),
        "dtr": nc.dram_tensor("dtr", [128, TOK], F32, kind="ExternalOutput").ap(),
    }
    p = Prog(nc)
    emit_l1a1(p, io)
    return nc


def build_l1a2():
    nc = bass.Bass("TRN2", target_bir_lowering=False)
    io = {
        "hbc": nc.dram_tensor("hbc", [10240, TOK], BF16, kind="ExternalInput").ap(),
        "halo": nc.dram_tensor("halo", [10240, 3], BF16, kind="ExternalInput").ap(),
        "dtr": nc.dram_tensor("dtr", [128, TOK], F32, kind="ExternalInput").ap(),
        "cw": nc.dram_tensor("cw", [128, 80, 4], F32, kind="ExternalInput").ap(),
        "cb": nc.dram_tensor("cb", [128, 80], F32, kind="ExternalInput").ap(),
        "dtbias": nc.dram_tensor("dtbias", [128, 1], F32, kind="ExternalInput").ap(),
        "alog": nc.dram_tensor("alog", [128, 1], F32, kind="ExternalInput").ap(),
        "idf": nc.dram_tensor("idf", [128, 128], F32, kind="ExternalInput").ap(),
        "idb": nc.dram_tensor("idb", [128, 128], BF16, kind="ExternalInput").ap(),
        "postc": nc.dram_tensor("postc", [10240, TOK], BF16, kind="ExternalOutput").ap(),
        "dt": nc.dram_tensor("dt", [128, TOK], F32, kind="ExternalOutput").ap(),
        "send": nc.dram_tensor("send", [128, 8192], F32, kind="ExternalOutput").ap(),
    }
    p = Prog(nc)
    emit_l1a2(p, io)
    return nc


def build_l1a3():
    nc = bass.Bass("TRN2", target_bir_lowering=False)
    io = {
        "postc": nc.dram_tensor("postc", [10240, TOK], BF16, kind="ExternalInput").ap(),
        "gate": nc.dram_tensor("gate", [8192, TOK], BF16, kind="ExternalInput").ap(),
        "dt": nc.dram_tensor("dt", [128, TOK], F32, kind="ExternalInput").ap(),
        "x_in": nc.dram_tensor("x_in", [D, TOK], F32, kind="ExternalInput").ap(),
        "sinit": nc.dram_tensor("sinit", [128, 8192], F32, kind="ExternalInput").ap(),
        "alog": nc.dram_tensor("alog", [128, 1], F32, kind="ExternalInput").ap(),
        "normw": nc.dram_tensor("normw", [128, 64], F32, kind="ExternalInput").ap(),
        "dvec": nc.dram_tensor("dvec", [128, 64], F32, kind="ExternalInput").ap(),
        "mrep": nc.dram_tensor("mrep", [128, 1024], BF16, kind="ExternalInput").ap(),
        "wo": nc.dram_tensor("wo", [8192, D], F32, kind="ExternalInput").ap(),
        "idf": nc.dram_tensor("idf", [128, 128], F32, kind="ExternalInput").ap(),
        "idb": nc.dram_tensor("idb", [128, 128], BF16, kind="ExternalInput").ap(),
        "x_out": nc.dram_tensor("x_out", [D, TOK], F32, kind="ExternalOutput").ap(),
    }
    p = Prog(nc)
    emit_l1a3(p, io)
    return nc


def build_launchA():
    nc = bass.Bass("TRN2", target_bir_lowering=False)
    E, ER = CFG_L0FFN["E"], CFG_L0FFN["ER"]
    io = {
        "x_in": nc.dram_tensor("x_in", [D, TOK], F32, kind="ExternalInput").ap(),
        "wq": nc.dram_tensor("wq", [D, 1024], F32, kind="ExternalInput").ap(),
        "bq": nc.dram_tensor("bq", [128, 8], F32, kind="ExternalInput").ap(),
        "m": nc.dram_tensor("m", [8, 128, 128], F32, kind="ExternalInput").ap(),
        "z": nc.dram_tensor("z", [8, 128], F32, kind="ExternalInput").ap(),
        "wo4": nc.dram_tensor("wo4", [8, 4, 128, 4096], F32, kind="ExternalInput").ap(),
        "idb": nc.dram_tensor("idb", [128, 128], BF16, kind="ExternalInput").ap(),
        "wb": nc.dram_tensor("wb", [D, D], F32, kind="ExternalInput").ap(),
        "wr": nc.dram_tensor("wr", [D, 8], F32, kind="ExternalInput").ap(),
        "at": nc.dram_tensor("at", [D, ER], F32, kind="ExternalInput").ap(),
        "bcat": nc.dram_tensor("bcat", [ER, D], F32, kind="ExternalInput").ap(),
        "e8": nc.dram_tensor("e8", [E, 128], F32, kind="ExternalInput").ap(),
        "idf": nc.dram_tensor("idf", [128, 128], F32, kind="ExternalInput").ap(),
        "bb": nc.dram_tensor("bb", [128, 32], F32, kind="ExternalInput").ap(),
        "wi": nc.dram_tensor("wi", [D, 18560], F32, kind="ExternalInput").ap(),
        "x2": nc.dram_tensor("x2", [D, TOK], F32, kind="ExternalOutput").ap(),
        "gate": nc.dram_tensor("gate", [8192, TOK], BF16, kind="ExternalOutput").ap(),
        "hbc": nc.dram_tensor("hbc", [10240, TOK], BF16, kind="ExternalOutput").ap(),
        "dtr": nc.dram_tensor("dtr", [128, TOK], F32, kind="ExternalOutput").ap(),
    }
    x1 = nc.dram_tensor("x1_scr", [D, TOK], F32, kind="Internal").ap()
    p = Prog(nc)
    get_cons(p)
    base0 = nc.sbuf_base
    emit_l0attn(p, dict(io, x_out=x1))
    done = ("oS", p.at("oS"))
    nc.sbuf_base = base0
    nc.psum_base = 0
    emit_ffn(p, dict(io, x_in=x1, x_out=io["x2"]), CFG_L0FFN, entry=done)
    done = ("fS", p.at("fS"))
    nc.sbuf_base = base0
    nc.psum_base = 0
    emit_l1a1(p, dict(io, x_in=io["x2"]), entry=done)
    with nc.Block("la_fin") as B:
        @B.sync
        def _(s):
            s.wait_ge(p.sem("iS"), p.at("iS"))
            s.wait_ge(p.sem("fS"), p.at("fS"))
    return nc


def build_launchC():
    nc = bass.Bass("TRN2", target_bir_lowering=False)
    E, ER = CFG_L1FFN["E"], CFG_L1FFN["ER"]
    io = {
        "postc": nc.dram_tensor("postc", [10240, TOK], BF16, kind="ExternalInput").ap(),
        "gate": nc.dram_tensor("gate", [8192, TOK], BF16, kind="ExternalInput").ap(),
        "dt": nc.dram_tensor("dt", [128, TOK], F32, kind="ExternalInput").ap(),
        "x_in": nc.dram_tensor("x_in", [D, TOK], F32, kind="ExternalInput").ap(),
        "sinit": nc.dram_tensor("sinit", [128, 8192], F32, kind="ExternalInput").ap(),
        "alog": nc.dram_tensor("alog", [128, 1], F32, kind="ExternalInput").ap(),
        "normw": nc.dram_tensor("normw", [128, 64], F32, kind="ExternalInput").ap(),
        "dvec": nc.dram_tensor("dvec", [128, 64], F32, kind="ExternalInput").ap(),
        "mrep": nc.dram_tensor("mrep", [128, 1024], BF16, kind="ExternalInput").ap(),
        "wo": nc.dram_tensor("wo", [8192, D], F32, kind="ExternalInput").ap(),
        "idf": nc.dram_tensor("idf", [128, 128], F32, kind="ExternalInput").ap(),
        "idb": nc.dram_tensor("idb", [128, 128], BF16, kind="ExternalInput").ap(),
        "wb": nc.dram_tensor("wb", [D, D], F32, kind="ExternalInput").ap(),
        "wr": nc.dram_tensor("wr", [D, 8], F32, kind="ExternalInput").ap(),
        "at": nc.dram_tensor("at", [D, ER], F32, kind="ExternalInput").ap(),
        "bcat": nc.dram_tensor("bcat", [ER, D], F32, kind="ExternalInput").ap(),
        "e8": nc.dram_tensor("e8", [E, 128], F32, kind="ExternalInput").ap(),
        "eb": nc.dram_tensor("eb", [E], F32, kind="ExternalInput").ap(),
        "x_out": nc.dram_tensor("x_out", [D, TOK], F32, kind="ExternalOutput").ap(),
    }
    x3 = nc.dram_tensor("x3_scr", [D, TOK], F32, kind="Internal").ap()
    p = Prog(nc)
    get_cons(p)
    base0 = nc.sbuf_base
    emit_l1a3(p, dict(io, x_out=x3))
    done = ("ySO", p.at("ySO"))
    nc.sbuf_base = base0
    nc.psum_base = 0
    emit_ffn(p, dict(io, x_in=x3), CFG_L1FFN, entry=done)
    with nc.Block("lc_fin") as B:
        @B.sync
        def _(s):
            s.wait_ge(p.sem("fS"), p.at("fS"))
    return nc


def host_prep_l1a(inputs):
    cw = np.asarray(inputs["L1_attn_conv_w"], np.float32)[:, 0, :]
    return {
        "wi": _c(np.asarray(inputs["L1_attn_in_proj_w"], np.float32).T),
        "cw": _c(cw.reshape(80, 128, 4).transpose(1, 0, 2)),
        "cb": _c(np.asarray(inputs["L1_attn_conv_b"], np.float32).reshape(80, 128).T),
        "dtbias": _c(np.asarray(inputs["L1_attn_dt_bias"], np.float32).reshape(128, 1)),
        "alog": _c(np.asarray(inputs["L1_attn_A_log"], np.float32).reshape(128, 1)),
        "idf": np.eye(128, dtype=np.float32),
        "idb": np.eye(128, dtype=ml_dtypes.bfloat16),
        "wo": _c(np.asarray(inputs["L1_attn_out_proj_w"], np.float32).T),
        "normw": _c(np.asarray(inputs["L1_attn_norm_w"], np.float32).reshape(64, 128).T),
        "dvec": _c(np.repeat(np.asarray(inputs["L1_attn_D"], np.float32), 64
                             ).reshape(64, 128).T),
        "mrep": np.tile(np.where(np.triu(np.ones((128, 128), bool)), 0.0, -30000.0
                                 ).astype(ml_dtypes.bfloat16), (1, 8)),
    }


def x_to_fm(x):
    return _c(np.asarray(x, np.float32).reshape(8192, 4096).T)


def fm_to_x(xfm):
    return _c(xfm.T).reshape(4, 2048, 4096)


def run_l0attn(x_fm, inputs):
    nc = build_l0attn()
    shared = host_prep_l0attn(inputs)
    in_maps = []
    for c in range(8):
        m = dict(shared)
        m["x_in"] = _c(x_fm[:, c * TOK:(c + 1) * TOK])
        in_maps.append(m)
    res = run_bass_kernel_spmd(nc, in_maps, core_ids=list(range(8)))
    return np.concatenate([res.results[c]["x_out"] for c in range(8)], axis=1)


def kernel(**inputs):
    _ = inputs["L0_attn_Wk"], inputs["L0_attn_Wv"]
    x0 = np.asarray(inputs["L0_attn_hidden_states"], np.float32)
    x_fm = x_to_fm(x0)

    sA = {}
    sA.update(host_prep_l0attn(inputs))
    sA.update(host_prep_l0ffn(inputs))
    shared = host_prep_l1a(inputs)
    sA["wi"] = shared["wi"]
    sA["idb"] = np.eye(128, dtype=ml_dtypes.bfloat16)

    nc = build_launchA()
    in_maps = []
    for c in range(8):
        m = dict(sA)
        m["x_in"] = _c(x_fm[:, c * TOK:(c + 1) * TOK])
        in_maps.append(m)
    resA = run_bass_kernel_spmd(nc, in_maps, core_ids=list(range(8)))

    nc = build_l1a2()
    in_maps = []
    for c in range(8):
        r = resA.results[c]
        if c % 2 == 1:
            halo = _c(resA.results[c - 1]["hbc"][:, TOK - 3:TOK])
        else:
            halo = np.zeros((10240, 3), ml_dtypes.bfloat16)
        m = {k: shared[k] for k in ("cw", "cb", "dtbias", "alog", "idf", "idb")}
        m.update({"hbc": r["hbc"], "halo": halo, "dtr": r["dtr"]})
        in_maps.append(m)
    resB = run_bass_kernel_spmd(nc, in_maps, core_ids=list(range(8)))

    sC = host_prep_l1ffn(inputs)
    nc = build_launchC()
    in_maps = []
    for c in range(8):
        rB = resB.results[c]
        sinit = (resB.results[c - 1]["send"] if c % 2 == 1
                 else np.zeros((128, 8192), np.float32))
        m = dict(sC)
        m.update({k: shared[k] for k in ("alog", "normw", "dvec", "mrep", "wo")})
        m["idf"] = np.eye(128, dtype=np.float32)
        m["idb"] = np.eye(128, dtype=ml_dtypes.bfloat16)
        m.update({"postc": rB["postc"], "gate": resA.results[c]["gate"], "dt": rB["dt"],
                  "x_in": _c(resA.results[c]["x2"]), "sinit": _c(sinit)})
        in_maps.append(m)
    resC = run_bass_kernel_spmd(nc, in_maps, core_ids=list(range(8)))
    x4 = np.concatenate([resC.results[c]["x_out"] for c in range(8)], axis=1)
    return fm_to_x(x4)


# version 07
# speedup vs baseline: 1.0214x; 1.0214x over previous
import numpy as np
import ml_dtypes
import concourse.bass as bass
import concourse.mybir as mybir
from concourse.bass_utils import run_bass_kernel_spmd

F32 = mybir.dt.float32
BF16 = mybir.dt.bfloat16
AF = mybir.ActivationFunctionType
ALU = mybir.AluOpType
AX = mybir.AxisListType

TOK = 1024
TH = 512
DK = 32
D = 4096


class Prog:

    def __init__(self, nc):
        self.nc = nc
        self.cnt = {}
        self.sems = {}

    def sem(self, name):
        if name not in self.sems:
            self.sems[name] = self.nc.alloc_semaphore(name)
            self.cnt[name] = 0
        return self.sems[name]

    def inc(self, name, k=1):
        h = self.sem(name)
        self.cnt[name] += k
        return h, k

    def at(self, name):
        self.sem(name)
        return self.cnt[name]


def sb(nc, name, shape, dtype):
    return nc.alloc_sbuf_tensor(name, shape, dtype)


def arena(nc, name, nbytes):
    t = nc.alloc_sbuf_tensor(name, [128, nbytes], mybir.dt.uint8)
    return nc.lookup_mloc(t).addr


def overlay(nc, name, shape, dtype, offset):
    return nc.alloc_sbuf_tensor_at(name, shape, dtype, offset=offset)


def psum(nc, name, shape, dtype=F32):
    return nc.alloc_psum_tensor(name, shape, dtype)



def alloc_ln(nc, tag):
    c = {}
    c["mean"] = sb(nc, f"{tag}_mean", [1, TH], F32)
    c["msq"] = sb(nc, f"{tag}_msq", [1, TH], F32)
    c["var"] = sb(nc, f"{tag}_var", [1, TH], F32)
    c["sd"] = sb(nc, f"{tag}_sd", [1, TH], F32)
    c["rstd"] = sb(nc, f"{tag}_rstd", [1, TH], F32)
    c["mb"] = sb(nc, f"{tag}_mb", [128, TH], F32)
    return c


def emit_ln(p, cons, ln, x_dram, h_sb, rb_sb, xh_sb, xsq_sb, ps, tag,
            pre_wait=None):
    nc = p.nc
    X, A, TQ, T, V = tag + "X", tag + "A", tag + "TQ", tag + "T", tag + "V"
    for t0 in (0, 1):
        baseA = p.at(A)
        baseTQ = p.at(TQ)
        with nc.Block(f"{tag}_ln{t0}", no_gpsimd_drain=True) as B:
            @B.sync
            def _(s):
                if t0 == 0 and pre_wait is not None:
                    s.wait_ge(p.sem(pre_wait[0]), pre_wait[1])
                if t0 == 1:
                    s.wait_ge(p.sem(V), p.at(V))
                src = x_dram[:, t0 * TH:(t0 + 1) * TH].rearrange("(k q) t -> q k t", q=128)
                s.dma_start(out=xh_sb[:], in_=src).then_inc(*p.inc(X, 16))

            @B.scalar
            def _(a):
                a.wait_ge(p.sem(X), p.at(X))
                for k in range(DK):
                    if k >= 8:
                        a.wait_ge(p.sem(TQ), baseTQ + k - 7)
                    a.activation(xsq_sb[:, k % 8, :], xh_sb[:, k, :], AF.Square).then_inc(
                        *p.inc(A, 1))

            @B.tensor
            def _(t):
                t.wait_ge(p.sem(X), p.at(X))
                for k in range(DK):
                    t.matmul(ps["s1"], cons["ones_f"][:], xh_sb[:, k, :],
                             start=(k == 0), stop=(k == DK - 1))
                for k in range(DK):
                    t.wait_ge(p.sem(A), baseA + k + 1)
                    t.matmul(ps["s2"], cons["ones_b"][:], xsq_sb[:, k % 8, :],
                             start=(k == 0), stop=(k == DK - 1)).then_inc(*p.inc(TQ, 1))

            @B.vector
            def _(v):
                v.wait_ge(p.sem(TQ), p.at(TQ))
                v.tensor_scalar_mul(ln["mean"][:], ps["s1"], 1.0 / D)
                v.tensor_scalar_mul(ln["msq"][:], ps["s2"], 1.0 / D)
                v.tensor_mul(ln["var"][:], ln["mean"][:], ln["mean"][:])
                v.tensor_sub(ln["var"][:], ln["msq"][:], ln["var"][:])
                v.tensor_scalar_add(ln["var"][:], ln["var"][:], 1e-5).then_inc(*p.inc(V, 1))

            @B.scalar
            def _(a):
                a.wait_ge(p.sem(V), p.at(V))
                a.activation(ln["sd"][:], ln["var"][:], AF.Sqrt).then_inc(*p.inc(A, 1))

            @B.vector
            def _(v):
                v.wait_ge(p.sem(A), p.at(A))
                v.reciprocal(ln["rstd"][:], ln["sd"][:]).then_inc(*p.inc(V, 1))

            @B.tensor
            def _(t):
                t.wait_ge(p.sem(V), p.at(V))
                t.matmul(ps["b1"], cons["ones_r"][:], ln["mean"][:], start=True, stop=True)
                t.matmul(ps["b2"], cons["ones_r"][:], ln["rstd"][:], start=True,
                         stop=True).then_inc(*p.inc(T, 1))

            @B.vector
            def _(v):
                v.wait_ge(p.sem(T), p.at(T))
                v.tensor_copy(rb_sb[:, t0 * TH:(t0 + 1) * TH], ps["b2"])
                v.tensor_copy(ln["mb"][:], ps["b1"])
                for k in range(DK):
                    ins = v.tensor_sub(h_sb[:, k, t0 * TH:(t0 + 1) * TH], xh_sb[:, k, :],
                                       ln["mb"][:])
                ins.then_inc(*p.inc(V, 1))


def get_cons(p):
    nc = p.nc
    if hasattr(nc, "_kcons"):
        return nc._kcons
    cons = {}
    cons["ones_f"] = sb(nc, "c_ones_f", [128, 1], F32)
    cons["ones_b"] = sb(nc, "c_ones_b", [128, 1], BF16)
    cons["ones_r"] = sb(nc, "c_ones_r", [1, 128], F32)
    with nc.Block("cinit", no_gpsimd_drain=True) as B:
        @B.gpsimd
        def _(g):
            g.memset(cons["ones_f"][:], 1.0)
            g.memset(cons["ones_b"][:], 1.0)
            ins = g.memset(cons["ones_r"][:], 1.0)
            ins.then_inc(*p.inc("init", 1))
    nc._kcons = cons
    return cons



def emit_l0attn(p, io, upto=5, entry=None):
    nc = p.nc
    cons = get_cons(p)
    ln = alloc_ln(nc, "a0")

    rb_sb = sb(nc, "a0_rb", [128, TOK], F32)
    sig_sb = sb(nc, "a0_sig", [128, 8, TOK], BF16)
    tmp1 = sb(nc, "a0_tmp1", [128, TH], F32)
    tmp2 = sb(nc, "a0_tmp2", [128, TH], F32)
    tmp3 = sb(nc, "a0_tmp3", [128, TH], F32)
    mh_sb = sb(nc, "a0_mh", [128, 2, 128], BF16)
    zh_sb = sb(nc, "a0_zh", [128, 2, 1], BF16)
    den_sb = sb(nc, "a0_den", [1, TH], F32)
    rden_sb = sb(nc, "a0_rden", [1, TH], F32)
    rdb_sb = sb(nc, "a0_rdb", [128, TH], F32)
    ret_sb = sb(nc, "a0_ret", [128, 8, TOK], BF16)
    bq_sb = sb(nc, "a0_bq", [128, 8], F32)
    stg = sb(nc, "a0_stg", [128, 4, TH], F32)
    xsq_sb = sb(nc, "a0_xsq", [128, 8, TH], BF16)
    baseA = arena(nc, "a0_arA", 65536)
    baseB = arena(nc, "a0_arB", 65536)
    h_sb = overlay(nc, "a0_h", [128, DK, TOK], BF16, baseA)
    woe_sb = overlay(nc, "a0_woe", [128, 8, 4096], BF16, baseA)
    xh_sb = overlay(nc, "a0_xh", [128, DK, TH], F32, baseB)
    wq_sb = overlay(nc, "a0_wq", [128, DK, 512], BF16, baseB)
    wo_pan = overlay(nc, "a0_wopan", [128, 2, 4096], BF16, baseB)
    id_b = sb(nc, "c_id_b", [128, 128], BF16)

    ps_all = psum(nc, "a0_ps", [128, 8, TH])
    ps = {
        "s1": ps_all[0:1, 0, :],
        "s2": ps_all[0:1, 1, :],
        "b1": ps_all[:, 2, :],
        "b2": ps_all[:, 3, :],
    }
    class _PS:
        def __init__(self, banks):
            self.banks = banks
        def __getitem__(self, key):
            _, half, _ = key
            return ps_all[:, self.banks[half], :]
    ps_q = _PS([4, 5])
    ps_q2 = _PS([6, 7])

    with nc.Block("a0_pre", no_gpsimd_drain=True) as B:
        @B.sync
        def _(s):
            if entry is not None:
                s.wait_ge(p.sem(entry[0]), entry[1])
            s.dma_start(out=bq_sb[:], in_=io["bq"]).then_inc(*p.inc("a0pre", 16))
            s.dma_start(out=id_b[:], in_=io["idb"]).then_inc(*p.inc("a0pre", 16))

    emit_ln(p, cons, ln, io["x_in"], h_sb, rb_sb, xh_sb, xsq_sb, ps, "a0",
            pre_wait=entry)
    V_h = p.at("a0V")

    if upto < 2:
        with nc.Block("a0_fin", no_gpsimd_drain=True) as B:
            @B.sync
            def _(s):
                s.wait_ge(p.sem("a0V"), V_h)
        return
    for og in range(2):
        with nc.Block(f"a0_q{og}", no_gpsimd_drain=True) as B:
            @B.gpsimd
            def _(g):
                if og > 0:
                    g.wait_ge(p.sem("qT"), 8)
                g.dma_start(out=wq_sb[:], in_=io["wq"][og]).then_inc(*p.inc("qW", 16))

            @B.tensor
            def _(t):
                t.wait_ge(p.sem("qW"), p.at("qW"))
                t.wait_ge(p.sem("a0V"), V_h)
                for oi in range(4):
                    for half in range(2):
                        ti = og * 8 + oi * 2 + half
                        pst = ps_q if oi % 2 == 0 else ps_q2
                        if ti >= 4:
                            t.wait_ge(p.sem("qV"), ti - 3)
                        for k in range(DK):
                            ins = t.matmul(pst[:, half, :], wq_sb[:, k, oi * 128:(oi + 1) * 128],
                                     h_sb[:, k, half * TH:(half + 1) * TH],
                                     start=(k == 0), stop=(k == DK - 1))
                        ins.then_inc(*p.inc("qT", 1))

            @B.vector
            def _(v):
                for oi in range(4):
                    for half in range(2):
                        ti = og * 8 + oi * 2 + half
                        ot = og * 4 + oi
                        pst = ps_q if oi % 2 == 0 else ps_q2
                        hs = slice(half * TH, (half + 1) * TH)
                        v.wait_ge(p.sem("qT"), ti + 1)
                        v.tensor_tensor(tmp1[:], pst[:, half, :], rb_sb[:, hs], ALU.mult)
                        v.tensor_scalar(tmp2[:], tmp1[:], bq_sb[:, ot:ot + 1], 0.0,
                                        ALU.add, ALU.min).then_inc(*p.inc("qV2", 1))
                        v.tensor_scalar(tmp3[:], tmp1[:], bq_sb[:, ot:ot + 1], 0.0,
                                        ALU.add, ALU.max)
                        v.wait_ge(p.sem("qA"), ti + 1)
                        v.tensor_add(sig_sb[:, ot, hs], tmp3[:], tmp2[:]).then_inc(
                            *p.inc("qV", 1))

            @B.scalar
            def _(a):
                for oi in range(4):
                    for half in range(2):
                        ti = og * 8 + oi * 2 + half
                        a.wait_ge(p.sem("qV2"), ti + 1)
                        a.activation(tmp2[:], tmp2[:], AF.Exp).then_inc(*p.inc("qA", 1))

    if upto < 3:
        with nc.Block("a0_fin", no_gpsimd_drain=True) as B:
            @B.sync
            def _(s):
                s.wait_ge(p.sem("qV"), 16)
        return
    ps_den = ps["s1"]
    for h in range(8):
        with nc.Block(f"a0_r{h}", no_gpsimd_drain=True) as B:
            @B.gpsimd
            def _(g):
                if h >= 2:
                    g.wait_ge(p.sem("rT1"), (h - 2) * 2 + 2)
                g.dma_start(out=mh_sb[:, h % 2, :], in_=io["m"][h]).then_inc(*p.inc("rW", 16))
                g.dma_start(out=zh_sb[:, h % 2, :],
                            in_=io["z"][h].unsqueeze(1)).then_inc(*p.inc("rW", 16))

            @B.tensor
            def _(t):
                t.wait_ge(p.sem("rW"), p.at("rW"))
                t.wait_ge(p.sem("qV"), 16)
                for half in range(2):
                    i = h * 2 + half
                    hs = slice(half * TH, (half + 1) * TH)
                    if i >= 2:
                        t.wait_ge(p.sem("rV2"), i - 1)
                    t.matmul(ps_q[:, half, :], mh_sb[:, h % 2, :], sig_sb[:, h, hs],
                             start=True, stop=True)
                    t.matmul(ps_den, zh_sb[:, h % 2, :], sig_sb[:, h, hs],
                             start=True, stop=True).then_inc(*p.inc("rT1", 1))
                    t.wait_ge(p.sem("rV1"), i + 1)
                    t.wait_ge(p.sem("rV2"), i)
                    t.matmul(ps["b1"], cons["ones_r"][:], rden_sb[:], start=True,
                             stop=True).then_inc(*p.inc("rT2", 1))

            @B.vector
            def _(v):
                for half in range(2):
                    i = h * 2 + half
                    hs = slice(half * TH, (half + 1) * TH)
                    v.wait_ge(p.sem("rT1"), i + 1)
                    v.tensor_scalar_add(den_sb[:], ps_den, 1e-6)
                    v.reciprocal(rden_sb[:], den_sb[:]).then_inc(*p.inc("rV1", 1))
                    v.wait_ge(p.sem("rT2"), i + 1)
                    v.tensor_copy(rdb_sb[:], ps["b1"])
                    v.tensor_tensor(ret_sb[:, h, hs], ps_q[:, half, :], rdb_sb[:],
                                    ALU.mult).then_inc(*p.inc("rV2", 1))

    if upto < 4:
        with nc.Block("a0_fin", no_gpsimd_drain=True) as B:
            @B.sync
            def _(s):
                s.wait_ge(p.sem("rV2"), 16)
        return
    for hkv in range(8):
        with nc.Block(f"a0_wo{hkv}", no_gpsimd_drain=True) as B:
            @B.gpsimd
            def _(g):
                if hkv == 0:
                    g.wait_ge(p.sem("a0V"), V_h)
                    g.wait_ge(p.sem("qT"), 16)
                for gg in range(4):
                    pi = hkv * 4 + gg
                    if pi >= 2:
                        g.wait_ge(p.sem("woT"), pi - 1)
                    g.dma_start(out=wo_pan[:, pi % 2, :], in_=io["wo4"][hkv, gg]).then_inc(
                        *p.inc("woW", 16))

            @B.tensor
            def _(t):
                if hkv == 0:
                    t.wait_ge(p.sem("a0pre"), 32)
                    t.wait_ge(p.sem("rV1"), 16)
                    t.wait_ge(p.sem("rV2"), 16)
                else:
                    t.wait_ge(p.sem("woV"), 8 * hkv)
                for gg in range(4):
                    pi = hkv * 4 + gg
                    t.wait_ge(p.sem("woW"), 16 * (pi + 1))
                    for ob in range(8):
                        ins = t.matmul(ps_all[:, ob, :], id_b[:],
                                       wo_pan[:, pi % 2, ob * TH:(ob + 1) * TH],
                                       start=(gg == 0), stop=(gg == 3))
                    ins.then_inc(*p.inc("woT", 1))

            @B.vector
            def _(v):
                v.wait_ge(p.sem("woT"), 4 * hkv + 4)
                if hkv == 0:
                    v.wait_ge(p.sem("qT"), 16)
                for ob in range(8):
                    v.tensor_copy(woe_sb[:, hkv, ob * TH:(ob + 1) * TH],
                                  ps_all[:, ob, :]).then_inc(*p.inc("woV", 1))

    if upto < 5 or upto >= 40:
        with nc.Block("a0_fin", no_gpsimd_drain=True) as B:
            @B.sync
            def _(s):
                s.wait_ge(p.sem("woV"), 64)
        return
    for half in range(2):
        with nc.Block(f"a0_o{half}", no_gpsimd_drain=True) as B:
            @B.sync
            def _(s):
                if half == 0:
                    s.wait_ge(p.sem("woV"), 64)
                else:
                    s.wait_ge(p.sem("oV"), 32)
                src = io["x_in"][:, half * TH:(half + 1) * TH].rearrange(
                    "(k q) t -> q k t", q=128)
                s.dma_start(out=xh_sb[:], in_=src).then_inc(*p.inc("oX", 16))

            @B.tensor
            def _(t):
                t.wait_ge(p.sem("woV"), 64)
                for ot in range(32):
                    i = half * 32 + ot
                    pst = ps_q if ot % 2 == 0 else ps_q2
                    if ot >= 2:
                        t.wait_ge(p.sem("oV"), i - 1)
                    for hkv in range(8):
                        ins = t.matmul(pst[:, half, :], woe_sb[:, hkv, ot * 128:(ot + 1) * 128],
                                 ret_sb[:, hkv, half * TH:(half + 1) * TH],
                                 start=(hkv == 0), stop=(hkv == 7))
                    ins.then_inc(*p.inc("oT", 1))

            @B.vector
            def _(v):
                v.wait_ge(p.sem("oX"), p.at("oX"))
                for ot in range(32):
                    i = half * 32 + ot
                    pst = ps_q if ot % 2 == 0 else ps_q2
                    v.wait_ge(p.sem("oT"), i + 1)
                    if i >= 4:
                        v.wait_ge(p.sem("oS"), 16 * (i - 3))
                    v.tensor_add(stg[:, ot % 4, :], pst[:, half, :],
                                 xh_sb[:, ot, :]).then_inc(*p.inc("oV", 1))

            @B.sync
            def _(s):
                for ot in range(32):
                    i = half * 32 + ot
                    s.wait_ge(p.sem("oV"), i + 1)
                    s.dma_start(out=io["x_out"][ot * 128:(ot + 1) * 128,
                                                half * TH:(half + 1) * TH],
                                in_=stg[:, ot % 4, :]).then_inc(*p.inc("oS", 16))

    with nc.Block("a0_fin", no_gpsimd_drain=True) as B:
        @B.sync
        def _(s):
            s.wait_ge(p.sem("oS"), p.at("oS"))



def emit_ffn(p, io, cfg, entry=None):
    nc = p.nc
    tag = cfg["tag"]
    E, ER = cfg["E"], cfg["ER"]

    cons = get_cons(p)
    ln = alloc_ln(nc, tag)

    rb_sb = sb(nc, f"{tag}_rb", [128, TOK], F32)
    logits = sb(nc, f"{tag}_logits", [8, TOK], F32)
    lt_sb = sb(nc, f"{tag}_lt", [128, E], F32)
    m1 = sb(nc, f"{tag}_m1", [128, 1], F32)
    m2 = sb(nc, f"{tag}_m2", [128, 1], F32)
    nm1 = sb(nc, f"{tag}_nm1", [128, 1], F32)
    dm = sb(nc, f"{tag}_dm", [128, 1], F32)
    r12 = sb(nc, f"{tag}_r12", [128, 1], F32)
    ind = sb(nc, f"{tag}_ind", [128, E], F32)
    msk = sb(nc, f"{tag}_msk", [128, E], F32)
    expv = sb(nc, f"{tag}_expv", [128, E], F32)
    comb = sb(nc, f"{tag}_comb", [128, E], F32)
    comb_f = sb(nc, f"{tag}_combf", [8, TOK], F32)
    hl_sb = sb(nc, f"{tag}_hl", [128, TOK], BF16)
    hw_sb = sb(nc, f"{tag}_hw", [128, TOK], BF16)
    bcat_sb = sb(nc, f"{tag}_bcat", [128, 4096], BF16)
    e8_sb = sb(nc, f"{tag}_e8", [8, 128], F32)
    wr_sb = sb(nc, f"{tag}_wr", [128, DK, 8], BF16)
    idf_sb = sb(nc, f"{tag}_idf", [128, 128], F32)
    bb_sb = sb(nc, f"{tag}_bb", [128, 32], F32)
    eb_sb = sb(nc, f"{tag}_eb", [8, 1], F32)

    baseA = arena(nc, f"{tag}_arA", 65536)
    baseB = arena(nc, f"{tag}_arB", 65536)
    baseC = arena(nc, f"{tag}_arC", 32768)
    h_sb = overlay(nc, f"{tag}_h", [128, DK, TOK], BF16, baseA)
    xh_sb = overlay(nc, f"{tag}_xh", [128, DK, TH], F32, baseB)
    at_sb = overlay(nc, f"{tag}_at", [128, DK, 128], BF16, baseC)
    wb_sb = overlay(nc, f"{tag}_wb", [128, DK, 512], BF16, baseC)
    xsq_sb = overlay(nc, f"{tag}_xsq", [128, 8, TH], BF16, baseC)
    stg = overlay(nc, f"{tag}_stg", [128, 4, TH], F32, baseB)
    xr_sb = overlay(nc, f"{tag}_xr", [128, 4, TH], F32, baseB + 8192)
    t0_sb = overlay(nc, f"{tag}_t0", [128, TH], F32, baseB + 16384)

    ps_all = psum(nc, f"{tag}_ps", [128, 8, TH])
    ps = {"s1": ps_all[0:1, 0, :], "s2": ps_all[0:1, 1, :],
          "b1": ps_all[:, 2, :], "b2": ps_all[:, 3, :]}


    PRE, LNV = tag + "pre", tag + "V"
    dbg_lt = sb(nc, f"{tag}_dbglt", [128, 48], F32)
    with nc.Block(f"{tag}_pre", no_gpsimd_drain=True) as B:
        @B.gpsimd
        def _(g):
            g.dma_start(out=wr_sb[:], in_=io["wr"].rearrange("(k q) o -> q k o", q=128)
                        ).then_inc(*p.inc(PRE, 16))
            g.dma_start(out=idf_sb[:], in_=io["idf"]).then_inc(*p.inc(PRE, 16))
            g.dma_start(out=e8_sb[0:E, :], in_=io["e8"]).then_inc(*p.inc(PRE, 16))
            g.dma_start(out=bcat_sb[0:ER, :], in_=io["bcat"]).then_inc(*p.inc(PRE, 16))
            if cfg["base_bias"]:
                g.dma_start(out=bb_sb[:], in_=io["bb"]).then_inc(*p.inc(PRE, 16))
            if cfg["logit_bias"]:
                g.dma_start(out=eb_sb[0:E, :], in_=io["eb"].unsqueeze(1)).then_inc(
                    *p.inc(PRE, 16))

    emit_ln(p, cons, ln, io["x_in"], h_sb, rb_sb, xh_sb, xsq_sb, ps, tag,
            pre_wait=entry)
    V_h = p.at(LNV)

    with nc.Block(f"{tag}_rt", no_gpsimd_drain=True) as B:
        @B.tensor
        def _(t):
            t.wait_ge(p.sem(PRE), 16)
            t.wait_ge(p.sem(LNV), V_h)
            for half in range(2):
                for k in range(DK):
                    ins = t.matmul(ps_all[0:8, half, 0:TH], wr_sb[:, k, 0:8],
                                   h_sb[:, k, half * TH:(half + 1) * TH],
                                   start=(k == 0), stop=(k == DK - 1))
                ins.then_inc(*p.inc("fT1", 1))

        @B.vector
        def _(v):
            for half in range(2):
                hs = slice(half * TH, (half + 1) * TH)
                v.wait_ge(p.sem("fT1"), half + 1)
                if cfg["logit_bias"]:
                    v.tensor_tensor(logits[0:E, hs], ps_all[0:E, half, 0:TH],
                                    rb_sb[0:E, hs], ALU.mult)
                    ins = v.tensor_scalar_add(logits[0:E, hs], logits[0:E, hs],
                                              eb_sb[0:E, :])
                else:
                    ins = v.tensor_tensor(logits[0:E, hs], ps_all[0:E, half, 0:TH],
                                          rb_sb[0:E, hs], ALU.mult)
                ins.then_inc(*p.inc("fV1", 1))

    for tc in range(8):
        with nc.Block(f"{tag}_tk{tc}", no_gpsimd_drain=True) as B:
            @B.tensor
            def _(t):
                half = tc // 4
                t.wait_ge(p.sem("fV1"), half + 1)
                if tc >= 1:
                    t.wait_ge(p.sem("fV2"), 2 * tc - 1)
                t.transpose(ps_all[:, 2, 0:E], logits[0:E, tc * 128:(tc + 1) * 128],
                            idf_sb[0:E, 0:E]).then_inc(*p.inc("fT2", 2))
                t.wait_ge(p.sem("fV2"), 2 * tc + 1)
                t.transpose(ps_all[0:E, 3, 0:128], comb[:, 0:E], idf_sb[:]).then_inc(
                    *p.inc("fT2", 2))

            @B.vector
            def _(v):
                v.wait_ge(p.sem("fT2"), 4 * tc + 2)
                v.tensor_copy(lt_sb[:, 0:E], ps_all[:, 2, 0:E])
                v.drain()
                v.reduce_max(m1[:], lt_sb[:, 0:E], axis=AX.X)
                v.drain()
                if cfg["top1"]:
                    ins = v.tensor_scalar(comb[:, 0:E], lt_sb[:, 0:E], m1[:], None,
                                          ALU.is_ge)
                    ins = v.drain()
                else:
                    v.tensor_scalar(ind[:, 0:E], lt_sb[:, 0:E], m1[:], None, ALU.is_ge)
                    v.drain()
                    v.tensor_scalar_mul(msk[:, 0:E], ind[:, 0:E], -60.0)
                    v.drain()
                    v.tensor_add(msk[:, 0:E], msk[:, 0:E], lt_sb[:, 0:E])
                    v.drain()
                    v.reduce_max(m2[:], msk[:, 0:E], axis=AX.X)
                    v.drain()
                    v.tensor_scalar(ind[:, 0:E], lt_sb[:, 0:E], m2[:], None, ALU.is_ge)
                    v.tensor_scalar_mul(nm1[:], m1[:], -1.0)
                    v.drain()
                    v.tensor_sub(dm[:], m2[:], m1[:])
                    v.drain().then_inc(*p.inc("fV3", 1))
                    v.wait_ge(p.sem("fA1"), 2 * tc + 2)
                    v.tensor_scalar_add(dm[:], dm[:], 1.0)
                    v.drain()
                    v.reciprocal(r12[:], dm[:])
                    v.drain()
                    v.tensor_mul(comb[:, 0:E], expv[:, 0:E], ind[:, 0:E])
                    v.drain()
                    v.tensor_scalar(comb[:, 0:E], comb[:, 0:E], r12[:],
                                    cfg["scaling"], ALU.mult, ALU.mult)
                    ins = v.drain()
                ins.then_inc(*p.inc("fV2", 1))
                if tc == 0 and "dbg_lt" in io:
                    v.tensor_copy(dbg_lt[:, 0:E], lt_sb[:, 0:E])
                    v.tensor_copy(dbg_lt[:, 8:8 + E], expv[:, 0:E])
                    v.tensor_copy(dbg_lt[:, 16:16 + E], comb[:, 0:E])
                    v.tensor_copy(dbg_lt[:, 24:25], m1[:])
                    v.tensor_copy(dbg_lt[:, 25:26], m2[:])
                    v.tensor_copy(dbg_lt[:, 26:27], r12[:])
                    v.tensor_copy(dbg_lt[:, 32:32 + E], msk[:, 0:E])
                    v.tensor_copy(dbg_lt[:, 40:40 + E], ind[:, 0:E]).then_inc(*p.inc("fVD", 1))
                v.wait_ge(p.sem("fT2"), 4 * tc + 4)
                v.tensor_copy(comb_f[0:E, tc * 128:(tc + 1) * 128],
                              ps_all[0:E, 3, 0:128]).then_inc(*p.inc("fV2", 1))

            if not cfg["top1"]:
                @B.scalar
                def _(a):
                    a.wait_ge(p.sem("fV3"), tc + 1)
                    a.activation(expv[:, 0:E], lt_sb[:, 0:E], AF.Exp, bias=nm1[:])
                    a.activation(dm[:], dm[:], AF.Exp)
                    a.drain().then_inc(*p.inc("fA1", 2))

    with nc.Block(f"{tag}_hl", no_gpsimd_drain=True) as B:
        @B.gpsimd
        def _(g):
            g.wait_ge(p.sem(LNV), V_h)
            g.dma_start(out=at_sb[:, :, 0:ER], in_=io["at"][:, :, 0:ER]).then_inc(*p.inc("fWa", 16))

        @B.tensor
        def _(t):
            t.wait_ge(p.sem("fWa"), 16)
            for half in range(2):
                for k in range(DK):
                    ins = t.matmul(ps_all[0:ER, 4 + half, 0:TH], at_sb[:, k, 0:ER],
                                   h_sb[:, k, half * TH:(half + 1) * TH],
                                   start=(k == 0), stop=(k == DK - 1))
                ins.then_inc(*p.inc("fT3", 1))
                t.wait_ge(p.sem("fV2"), 16)
                ins = t.matmul(ps_all[:, 2, 0:TH], e8_sb[0:E, :],
                               comb_f[0:E, half * TH:(half + 1) * TH],
                               start=True, stop=True)
                ins.then_inc(*p.inc("fT3", 1))

        @B.vector
        def _(v):
            for half in range(2):
                hs = slice(half * TH, (half + 1) * TH)
                v.wait_ge(p.sem("fT3"), 2 * half + 1)
                v.tensor_copy(hl_sb[0:ER, hs], ps_all[0:ER, 4 + half, 0:TH])
                v.wait_ge(p.sem("fT3"), 2 * half + 2)
                v.tensor_tensor(hw_sb[0:ER, hs], ps_all[0:ER, 2, 0:TH],
                                hl_sb[0:ER, hs], ALU.mult).then_inc(*p.inc("fV4", 1))

    for og in range(8):
        with nc.Block(f"{tag}_b{og}", no_gpsimd_drain=True) as B:
            @B.gpsimd
            def _(g):
                if og == 0:
                    g.wait_ge(p.sem("fV4"), 2)
                else:
                    g.wait_ge(p.sem("fT4"), 8 * og)
                g.dma_start(out=wb_sb[:], in_=io["wb"][og]).then_inc(*p.inc("fWb", 16))

            @B.scalar
            def _(a):
                if og == 0:
                    a.wait_ge(p.sem(LNV), V_h)
                for oi in range(4):
                    for half in range(2):
                        ti = og * 8 + oi * 2 + half
                        ot = og * 4 + oi
                        if ti >= 4:
                            a.wait_ge(p.sem("fV5"), ti - 3)
                        a.dma_start(out=xr_sb[:, ti % 4, :],
                                    in_=io["x_in"][ot * 128:(ot + 1) * 128,
                                                   half * TH:(half + 1) * TH]).then_inc(
                            *p.inc("fXr", 16))

            @B.tensor
            def _(t):
                t.wait_ge(p.sem("fWb"), 16 * (og + 1))
                if og == 0:
                    t.wait_ge(p.sem("fV4"), 2)
                for oi in range(4):
                    for half in range(2):
                        ti = og * 8 + oi * 2 + half
                        bank = 4 + (oi % 2) * 2 + half
                        if ti >= 4:
                            t.wait_ge(p.sem("fV5"), ti - 3)
                        for k in range(DK):
                            t.matmul(ps_all[:, bank, :], wb_sb[:, k, oi * 128:(oi + 1) * 128],
                                     h_sb[:, k, half * TH:(half + 1) * TH],
                                     start=(k == 0), stop=False)
                        ins = t.matmul(ps_all[:, bank, :],
                                       bcat_sb[0:ER, (og * 4 + oi) * 128:(og * 4 + oi + 1) * 128],
                                       hw_sb[0:ER, half * TH:(half + 1) * TH],
                                       start=False, stop=True)
                        ins.then_inc(*p.inc("fT4", 1))

            @B.vector
            def _(v):
                for oi in range(4):
                    for half in range(2):
                        ti = og * 8 + oi * 2 + half
                        ot = og * 4 + oi
                        bank = 4 + (oi % 2) * 2 + half
                        hs = slice(half * TH, (half + 1) * TH)
                        v.wait_ge(p.sem("fT4"), ti + 1)
                        v.wait_ge(p.sem("fXr"), 16 * (ti + 1))
                        if ti >= 4:
                            v.wait_ge(p.sem("fS"), 16 * (ti - 3))
                        v.tensor_tensor(t0_sb[:], ps_all[:, bank, :], rb_sb[:, hs],
                                        ALU.mult)
                        if cfg["base_bias"]:
                            ins = v.scalar_tensor_tensor(
                                stg[:, ti % 4, :], t0_sb[:], bb_sb[:, ot:ot + 1],
                                xr_sb[:, ti % 4, :], ALU.add, ALU.add)
                        else:
                            ins = v.tensor_add(stg[:, ti % 4, :], t0_sb[:],
                                               xr_sb[:, ti % 4, :])
                        ins.then_inc(*p.inc("fV5", 1))

            @B.sync
            def _(s):
                for oi in range(4):
                    for half in range(2):
                        ti = og * 8 + oi * 2 + half
                        ot = og * 4 + oi
                        s.wait_ge(p.sem("fV5"), ti + 1)
                        s.dma_start(out=io["x_out"][ot * 128:(ot + 1) * 128,
                                                    half * TH:(half + 1) * TH],
                                    in_=stg[:, ti % 4, :]).then_inc(*p.inc("fS", 16))

    if "dbg_logits" in io:
        with nc.Block(f"{tag}_dbg", no_gpsimd_drain=True) as B:
            @B.sync
            def _(s):
                s.wait_ge(p.sem("fV1"), 2)
                s.dma_start(out=io["dbg_logits"], in_=logits[0:8, :]).then_inc(*p.inc("fDB", 16))
                s.wait_ge(p.sem("fV2"), 16)
                s.dma_start(out=io["dbg_comb"], in_=comb_f[0:8, :]).then_inc(*p.inc("fDB", 16))
                s.wait_ge(p.sem("fVD"), 1)
                s.dma_start(out=io["dbg_lt"], in_=dbg_lt[:]).then_inc(*p.inc("fDB", 16))
                s.wait_ge(p.sem("fV4"), 2)
                s.dma_start(out=io["dbg_hl"], in_=hl_sb[:]).then_inc(*p.inc("fDB", 16))
                s.dma_start(out=io["dbg_hw"], in_=hw_sb[:]).then_inc(*p.inc("fDB", 16))
                s.wait_ge(p.sem("fDB"), p.at("fDB"))

    with nc.Block(f"{tag}_fin", no_gpsimd_drain=True) as B:
        @B.sync
        def _(s):
            s.wait_ge(p.sem("fS"), p.at("fS"))



NOC = 145


def emit_l1a1(p, io, entry=None):
    nc = p.nc
    cons = get_cons(p)
    ln = alloc_ln(nc, "i1")

    rb_sb = sb(nc, "i1_rb", [128, TOK], F32)
    stgb = sb(nc, "i1_stgb", [128, 4, TH], BF16)
    stgf = sb(nc, "i1_stgf", [128, 4, TH], F32)

    baseA = arena(nc, "i1_arA", 65536)
    baseB = arena(nc, "i1_arB", 65536)
    baseC = arena(nc, "i1_arC", 8192)
    h_sb = overlay(nc, "i1_h", [128, DK, TOK], BF16, baseA)
    xh_sb = overlay(nc, "i1_xh", [128, DK, TH], F32, baseB)
    wp_sb = overlay(nc, "i1_wp", [128, 2, DK, 512], BF16, baseB)
    xsq_sb = overlay(nc, "i1_xsq", [128, 8, TH], BF16, baseC)

    ps_all = psum(nc, "i1_ps", [128, 8, TH])
    ps = {"s1": ps_all[0:1, 0, :], "s2": ps_all[0:1, 1, :],
          "b1": ps_all[:, 2, :], "b2": ps_all[:, 3, :]}

    emit_ln(p, cons, ln, io["x_in"], h_sb, rb_sb, xh_sb, xsq_sb, ps, "i1",
            pre_wait=entry)
    V_h = p.at("i1V")

    def out_route(ot):
        if ot < 64:
            return "gate", ot, False
        if ot < 144:
            return "hbc", ot - 64, False
        return "dtr", 0, True

    ogs = [list(range(og * 4, min(og * 4 + 4, NOC))) for og in range((NOC + 3) // 4)]
    ti = 0
    ti_of = {}
    for og, ots in enumerate(ogs):
        for oi in range(len(ots)):
            for half in range(2):
                ti_of[(og, oi, half)] = ti
                ti += 1
    NTI = ti

    for og, ots in enumerate(ogs):
        ow = 512 if len(ots) == 4 else 128 * len(ots)
        with nc.Block(f"i1_b{og}", no_gpsimd_drain=True) as B:
            @B.gpsimd
            def _(g):
                if og == 0:
                    g.wait_ge(p.sem("i1V"), V_h)
                if og >= 2:
                    g.wait_ge(p.sem("iT"), ti_of[(og - 2, len(ogs[og - 2]) - 1, 1)] + 1)
                g.dma_start(
                    out=wp_sb[:, og % 2, :, 0:ow],
                    in_=io["wi"][og][:, :, 0:ow]).then_inc(*p.inc("iW", 16))

            @B.tensor
            def _(t):
                t.wait_ge(p.sem("iW"), 16 * (og + 1))
                if og == 0:
                    t.wait_ge(p.sem("i1V"), V_h)
                for oi in range(len(ots)):
                    for half in range(2):
                        tj = ti_of[(og, oi, half)]
                        bank = 4 + (oi % 2) * 2 + half
                        if tj >= 4:
                            t.wait_ge(p.sem("iV"), tj - 3)
                        for k in range(DK):
                            ins = t.matmul(
                                ps_all[:, bank, :],
                                wp_sb[:, og % 2, k, oi * 128:(oi + 1) * 128],
                                h_sb[:, k, half * TH:(half + 1) * TH],
                                start=(k == 0), stop=(k == DK - 1))
                        ins.then_inc(*p.inc("iT", 1))

            @B.vector
            def _(v):
                for oi in range(len(ots)):
                    for half in range(2):
                        tj = ti_of[(og, oi, half)]
                        ot = ots[oi]
                        bank = 4 + (oi % 2) * 2 + half
                        hs = slice(half * TH, (half + 1) * TH)
                        _, _, isf = out_route(ot)
                        v.wait_ge(p.sem("iT"), tj + 1)
                        if tj >= 4:
                            v.wait_ge(p.sem("iS"), 16 * (tj - 3))
                        stgx = stgf if isf else stgb
                        ins = v.tensor_tensor(stgx[:, tj % 4, :], ps_all[:, bank, :],
                                              rb_sb[:, hs], ALU.mult)
                        ins.then_inc(*p.inc("iV", 1))

            @B.sync
            def _(s):
                for oi in range(len(ots)):
                    for half in range(2):
                        tj = ti_of[(og, oi, half)]
                        ot = ots[oi]
                        dest, orow, isf = out_route(ot)
                        s.wait_ge(p.sem("iV"), tj + 1)
                        stgx = stgf if isf else stgb
                        if dest == "gate":
                            dap = io["gate"][half, orow]
                        else:
                            dap = io[dest][orow * 128:orow * 128 + 128,
                                           half * TH:(half + 1) * TH]
                        s.dma_start(out=dap, in_=stgx[:, tj % 4, :]).then_inc(*p.inc("iS", 16))

    with nc.Block("i1_fin", no_gpsimd_drain=True) as B:
        @B.sync
        def _(s):
            s.wait_ge(p.sem("iS"), p.at("iS"))



def emit_l1a2(p, io):
    nc = p.nc
    cons = get_cons(p)

    CIN = sb(nc, "c2_cin", [128, 2, 1027], BF16)
    ACC1 = sb(nc, "c2_acc1", [128, TOK], F32)
    ACC2 = sb(nc, "c2_acc2", [128, TOK], F32)
    PO = sb(nc, "c2_po", [128, 2, TOK], BF16)
    CW = sb(nc, "c2_cw", [128, 80, 4], F32)
    CB = sb(nc, "c2_cb", [128, 80], F32)
    DTB = sb(nc, "c2_dtb", [128, 1], F32)
    ALOG = sb(nc, "c2_alog", [128, 1], F32)
    NEGA = sb(nc, "c2_nega", [128, 1], F32)
    DT = sb(nc, "c2_dt", [128, TOK], F32)
    TMP = sb(nc, "c2_tmp", [128, TOK], F32)
    S_sb = sb(nc, "c2_s", [128, 8192], F32)
    XSL = sb(nc, "c2_xsl", [128, 64, 128], BF16)
    BSL = sb(nc, "c2_bsl", [128, 8, 128], BF16)
    XDEC = sb(nc, "c2_xdec", [128, 8192], BF16)
    BT = sb(nc, "c2_bt", [128, 8, 128], BF16)
    ADT = sb(nc, "c2_adt", [128, 128], F32)
    ACUM = sb(nc, "c2_acum", [128, 128], F32)
    DS = sb(nc, "c2_ds", [128, 128], F32)
    XMUL = sb(nc, "c2_xmul", [128, 128], F32)
    XMULT = sb(nc, "c2_xmult", [128, 128], F32)
    DEC = sb(nc, "c2_dec", [128, 1], F32)
    DECB = sb(nc, "c2_decb", [128, 128], F32)
    DROW = sb(nc, "c2_drow", [1, 128], F32)
    ONES128 = sb(nc, "c2_ones", [128, 128], F32)
    T1 = sb(nc, "c2_t1", [128, 1024], F32)
    idf_sb = sb(nc, "c2_idf", [128, 128], F32)

    ps_all = psum(nc, "c2_ps", [128, 6, TH])

    with nc.Block("c2_pre", no_gpsimd_drain=True) as B:
        @B.sync
        def _(s):
            s.dma_start(out=CW[:], in_=io["cw"]).then_inc(*p.inc("cP", 16))
            s.dma_start(out=CB[:], in_=io["cb"]).then_inc(*p.inc("cP", 16))
            s.dma_start(out=DTB[:], in_=io["dtbias"]).then_inc(*p.inc("cP", 16))
            s.dma_start(out=ALOG[:], in_=io["alog"]).then_inc(*p.inc("cP", 16))
            s.dma_start(out=idf_sb[:], in_=io["idf"]).then_inc(*p.inc("cP", 16))
            s.dma_start(out=T1[0:128, 0:1024], in_=io["dtr"]).then_inc(*p.inc("cP", 16))

        @B.gpsimd
        def _(g):
            g.memset(ONES128[:], 1.0)
            g.memset(S_sb[:], 0.0)
            g.drain().then_inc(*p.inc("cG", 1))

        @B.scalar
        def _(a):
            a.wait_ge(p.sem("cP"), p.at("cP"))
            a.activation(NEGA[:], ALOG[:], AF.Exp)
            a.drain().then_inc(*p.inc("cA", 1))

        @B.vector
        def _(v):
            v.wait_ge(p.sem("cA"), 1)
            v.tensor_scalar_mul(NEGA[:], NEGA[:], -1.0)
            v.tensor_scalar_add(T1[:, 0:1024], T1[:, 0:1024], DTB[:])
            v.drain().then_inc(*p.inc("cV", 1))

        @B.scalar
        def _(a):
            a.wait_ge(p.sem("cV"), 1)
            a.activation(DT[:], T1[:, 0:1024], AF.Abs)
            a.drain()
            a.activation(DT[:], DT[:], AF.Exp, scale=-1.0)
            a.drain()
            a.activation(DT[:], DT[:], AF.Ln, bias=1.0)
            a.drain().then_inc(*p.inc("cA", 1))

        @B.vector
        def _(v):
            v.wait_ge(p.sem("cA"), 2)
            v.tensor_single_scalar(TMP[:], T1[:, 0:1024], 0.0, ALU.max)
            v.drain()
            v.tensor_add(DT[:], DT[:], TMP[:])
            v.drain().then_inc(*p.inc("cV", 1))

    for ct in range(80):
        with nc.Block(f"c2_cv{ct}", no_gpsimd_drain=True) as B:
            @B.gpsimd
            def _(g):
                if ct >= 2:
                    g.wait_ge(p.sem("cVc"), ct - 1)
                g.dma_start(out=CIN[:, ct % 2, 0:3],
                            in_=io["halo"][ct * 128:(ct + 1) * 128, :]).then_inc(
                    *p.inc("cW2", 16))
                g.dma_start(out=CIN[:, ct % 2, 3:1027],
                            in_=io["hbc"][ct * 128:(ct + 1) * 128, :]).then_inc(
                    *p.inc("cW2", 16))

            @B.vector
            def _(v):
                v.wait_ge(p.sem("cW2"), 32 * (ct + 1))
                if ct >= 1:
                    v.wait_ge(p.sem("cAc"), ct)
                c = CIN[:, ct % 2, :]
                v.tensor_scalar_mul(ACC1[:], c[:, 0:1024], CW[:, ct, 0:1])
                v.scalar_tensor_tensor(ACC2[:], c[:, 1:1025], CW[:, ct, 1:2], ACC1[:],
                                       ALU.mult, ALU.add)
                v.scalar_tensor_tensor(ACC1[:], c[:, 2:1026], CW[:, ct, 2:3], ACC2[:],
                                       ALU.mult, ALU.add)
                v.scalar_tensor_tensor(ACC2[:], c[:, 3:1027], CW[:, ct, 3:4], ACC1[:],
                                       ALU.mult, ALU.add)
                v.tensor_scalar_add(ACC1[:], ACC2[:], CB[:, ct:ct + 1])
                ins = v.drain()
                ins.then_inc(*p.inc("cVc", 1))

            @B.scalar
            def _(a):
                a.wait_ge(p.sem("cVc"), ct + 1)
                if ct >= 2:
                    a.wait_ge(p.sem("cS"), 16 * (ct - 1))
                a.activation(PO[:, ct % 2, :], ACC1[:], AF.Silu)
                a.drain().then_inc(*p.inc("cAc", 1))

            @B.sync
            def _(s):
                s.wait_ge(p.sem("cAc"), ct + 1)
                s.dma_start(out=io["postc"][:, :, ct, :].rearrange("c q t -> q c t"),
                            in_=PO[:, ct % 2, :].rearrange("q (c t) -> q c t", c=8)
                            ).then_inc(*p.inc("cS", 16))

    with nc.Block("c2_dto", no_gpsimd_drain=True) as B:
        @B.sync
        def _(s):
            s.wait_ge(p.sem("cV"), 2)
            s.dma_start(out=io["dt"], in_=DT[:]).then_inc(*p.inc("cS", 16))

    psx = psum(nc, "c2_psx", [128, 2, 8, 128], BF16)
    idb_sb = sb(nc, "c2_idb", [128, 128], BF16)
    TMPS = sb(nc, "c2_tmps", [128, TH], F32)

    with nc.Block("c2_pre2", no_gpsimd_drain=True) as B:
        @B.sync
        def _(s):
            s.dma_start(out=idb_sb[:], in_=io["idb"]).then_inc(*p.inc("cP", 16))

    for c in range(8):
        with nc.Block(f"c2_st{c}", no_gpsimd_drain=True) as B:
            zW0 = p.at("sW")
            @B.gpsimd
            def _(g):
                g.wait_ge(p.sem("cS"), 16 * 80)
                if c > 0:
                    g.wait_ge(p.sem("sT"), p.at("sT"))
                g.dma_start(out=XSL[:], in_=io["postc"][c, :, 0:64, :]).then_inc(
                    *p.inc("sW", 16))
                g.dma_start(out=BSL[:], in_=io["postc"][c, :, 64:72, :]).then_inc(
                    *p.inc("sW", 16))

            @B.vector
            def _(v):
                v.wait_ge(p.sem("cV"), 2)
                if c > 0:
                    v.wait_ge(p.sem("sA"), p.at("sA"))
                v.tensor_scalar_mul(ADT[:], DT[:, c * 128:(c + 1) * 128], NEGA[:])
                v.drain()
                v.tensor_tensor_scan(ACUM[:], ADT[:], ONES128[:], 0.0, ALU.add, ALU.mult)
                v.drain()
                v.tensor_scalar(DS[:], ACUM[:], ACUM[:, 127:128], None, ALU.subtract)
                v.drain().then_inc(*p.inc("sV", 1))
            zV_ds = p.at("sV")

            @B.scalar
            def _(a):
                a.wait_ge(p.sem("sV"), zV_ds)
                a.activation(DS[:], DS[:], AF.Exp, scale=-1.0)
                a.activation(DEC[:], ACUM[:, 127:128], AF.Exp)
                a.drain().then_inc(*p.inc("sA", 1))
            zA_ds = p.at("sA")

            @B.vector
            def _(v):
                v.wait_ge(p.sem("sA"), zA_ds)
                v.tensor_mul(XMUL[:], DT[:, c * 128:(c + 1) * 128], DS[:])
                v.drain().then_inc(*p.inc("sV", 1))
            zV_xm = p.at("sV")

            @B.tensor
            def _(t):
                t.wait_ge(p.sem("cP"), p.at("cP"))
                t.wait_ge(p.sem("sV"), zV_xm)
                if c > 0:
                    t.wait_ge(p.sem("sV2"), p.at("sV2"))
                t.transpose(ps_all[:, 0, 0:128], XMUL[:], idf_sb[:])
                t.transpose(ps_all[0:1, 1, 0:128], DEC[:], idf_sb[:]).then_inc(*p.inc("sT", 1))
            zT_tr = p.at("sT")

            @B.vector
            def _(v):
                v.wait_ge(p.sem("sT"), zT_tr)
                v.tensor_copy(XMULT[:], ps_all[:, 0, 0:128])
                v.tensor_copy(DROW[:], ps_all[0:1, 1, 0:128])
                v.drain().then_inc(*p.inc("sV2", 1))
            zV2_cp = p.at("sV2")

            @B.tensor
            def _(t):
                t.wait_ge(p.sem("sV2"), zV2_cp)
                t.matmul(ps_all[:, 1, 0:128], cons["ones_r"][:], DROW[:],
                         start=True, stop=True).then_inc(*p.inc("sT", 1))
            zT_db = p.at("sT")

            @B.vector
            def _(v):
                v.wait_ge(p.sem("sT"), zT_db)
                v.tensor_copy(DECB[:], ps_all[:, 1, 0:128])
                v.drain().then_inc(*p.inc("sV2", 1))
            zV2_db = p.at("sV2")

            zT_xt = []
            @B.tensor
            def _(t):
                t.wait_ge(p.sem("sW"), zW0 + 32)
                t.wait_ge(p.sem("sV2"), zV2_db)
                for bt in range(8):
                    if bt >= 2:
                        t.wait_ge(p.sem("sV2"), zV2_db + bt - 1)
                    for j in range(8):
                        ins = t.transpose(psx[:, bt % 2, j, :], XSL[:, bt * 8 + j, :],
                                          idb_sb[:])
                    ins.then_inc(*p.inc("sT", 1))
                    zT_xt.append(p.at("sT"))
                t.wait_ge(p.sem("sV2"), zV2_db + 7)
                for j in range(8):
                    ins = t.transpose(psx[:, 0, j, :], BSL[:, j, :], idb_sb[:])
                ins.then_inc(*p.inc("sT", 1))
            zT_bt = p.at("sT")

            @B.vector
            def _(v):
                for bt in range(8):
                    v.wait_ge(p.sem("sT"), zT_xt[bt])
                    ins = v.tensor_tensor(
                        XDEC[:, bt * 1024:(bt + 1) * 1024].rearrange(
                            "t (h z) -> t h z", h=16),
                        psx[:, bt % 2, :, :].rearrange("t j z -> t (j z)").rearrange(
                            "t (h z) -> t h z", h=16),
                        XMULT[:, bt * 16:(bt + 1) * 16].unsqueeze(2).broadcast_to(
                            [128, 16, 64]),
                        ALU.mult)
                    ins.then_inc(*p.inc("sV2", 1))
                v.wait_ge(p.sem("sT"), zT_bt)
                v.tensor_copy(BT[:].rearrange("t j z -> t (j z)"),
                              psx[:, 0, :, :].rearrange("t j z -> t (j z)"))
                v.drain().then_inc(*p.inc("sV2", 1))
            zV2_bt = p.at("sV2")

            zT_rw = []
            @B.tensor
            def _(t):
                t.wait_ge(p.sem("sV2"), zV2_bt)
                for g in range(8):
                    bk = 2 + (g % 2) * 2
                    if g >= 2:
                        t.wait_ge(p.sem("sV3"), 8 * c + g - 1)
                    for half in range(2):
                        ins = t.matmul(ps_all[:, bk + half, :], BT[:, g, :],
                                       XDEC[:, g * 1024 + half * TH:
                                            g * 1024 + (half + 1) * TH],
                                       start=True, stop=True)
                    ins.then_inc(*p.inc("sT", 1))
                    zT_rw.append(p.at("sT"))

            @B.vector
            def _(v):
                for g in range(8):
                    bk = 2 + (g % 2) * 2
                    v.wait_ge(p.sem("sT"), zT_rw[g])
                    for half in range(2):
                        gs = slice(g * 1024 + half * TH, g * 1024 + (half + 1) * TH)
                        v.tensor_tensor(TMPS[:].rearrange("n (h z) -> n h z", h=8),
                                        S_sb[:, gs].rearrange("n (h z) -> n h z", h=8),
                                        DECB[:, g * 16 + half * 8:g * 16 + (half + 1) * 8
                                             ].unsqueeze(2).broadcast_to([128, 8, 64]),
                                        ALU.mult)
                        ins = v.tensor_add(S_sb[:, gs], TMPS[:], ps_all[:, bk + half, :])
                    ins.then_inc(*p.inc("sV3", 1))

    with nc.Block("c2_sout", no_gpsimd_drain=True) as B:
        @B.sync
        def _(s):
            s.wait_ge(p.sem("sV3"), p.at("sV3"))
            s.dma_start(out=io["send"], in_=S_sb[:]).then_inc(*p.inc("cS", 16))
            s.wait_ge(p.sem("cS"), p.at("cS"))



def emit_l1a3(p, io):
    nc = p.nc
    cons = get_cons(p)

    DT = sb(nc, "a3_dt", [128, TOK], F32)
    NEGA = sb(nc, "a3_nega", [128, 1], F32)
    ALOG = sb(nc, "a3_alog", [128, 1], F32)
    S_sb = sb(nc, "a3_s", [128, 8192], F32)
    SBF = sb(nc, "a3_sbf", [128, 8192], BF16)
    ADT = sb(nc, "a3_adt", [128, 128], F32)
    ACUM = sb(nc, "a3_acum", [128, 128], F32)
    ACUMT = sb(nc, "a3_acumt", [128, 128], F32)
    DTT = sb(nc, "a3_dtt", [128, 128], F32)
    EPOS = sb(nc, "a3_epos", [128, 128], F32)
    DS = sb(nc, "a3_ds", [128, 128], F32)
    DST = sb(nc, "a3_dst", [128, 128], F32)
    DEC = sb(nc, "a3_dec", [128, 1], F32)
    DECB = sb(nc, "a3_decb", [128, 128], F32)
    DROW = sb(nc, "a3_drow", [1, 128], F32)
    ONES128 = sb(nc, "a3_ones", [128, 128], F32)
    XSL = sb(nc, "a3_xsl", [128, 64, 128], BF16)
    BCSL = sb(nc, "a3_bcsl", [128, 16, 128], BF16)
    XDT = sb(nc, "a3_xdt", [128, 8192], BF16)
    BT = sb(nc, "a3_bt", [128, 8, 128], BF16)
    GMT = sb(nc, "a3_gmt", [128, 8, 128], BF16)
    ROWS = sb(nc, "a3_rows", [1, 2, 1024], F32)
    MREP = sb(nc, "a3_mrep", [128, 1024], BF16)
    DIFF = sb(nc, "a3_diff", [128, 2, 1024], F32)
    LT8 = sb(nc, "a3_lt8", [128, 2, 1024], BF16)
    MT_Q = sb(nc, "a3_mtq", [128, 4, 1024], BF16)
    YTM = sb(nc, "a3_ytm", [128, 2048], BF16)
    TMPS = sb(nc, "a3_tmps", [128, TH], F32)
    DVEC = sb(nc, "a3_dvec", [128, 64], F32)
    NRMW = sb(nc, "a3_nrmw", [128, 64], F32)
    idf_sb = sb(nc, "a3_idf", [128, 128], F32)
    idb_sb = sb(nc, "a3_idb", [128, 128], BF16)
    RMSV = sb(nc, "a3_rmsv", [1, TH], F32)
    RSTB = sb(nc, "a3_rstb", [128, TH], F32)
    YSQ = overlay(nc, "a3_ysq", [128, 8, TH], BF16, nc.lookup_mloc(DIFF).addr)
    SG = overlay(nc, "a3_sg", [128, 2, TH], BF16, nc.lookup_mloc(LT8).addr)
    GT = overlay(nc, "a3_gt", [128, 2, TH], BF16, nc.lookup_mloc(LT8).addr + 2048)
    T2 = overlay(nc, "a3_t2", [128, TH], BF16, nc.lookup_mloc(MT_Q).addr)
    STG = overlay(nc, "a3_stg", [128, 4, TH], F32, nc.lookup_mloc(XDT).addr)
    XR = overlay(nc, "a3_xr", [128, 4, TH], F32, nc.lookup_mloc(XDT).addr + 8192)
    WP = overlay(nc, "a3_wp", [128, 2, 8, TH], BF16, nc.lookup_mloc(XSL).addr)

    baseY = arena(nc, "a3_arY", 65536)
    y_fm = overlay(nc, "a3_yfm", [128, 64, TH], BF16, baseY)

    ps0 = psum(nc, "a3_ps0", [128, 1, 128], F32)
    psx = psum(nc, "a3_psx", [128, 2, 8, 128], BF16)
    psg = psum(nc, "a3_psg", [128, 4, 128], F32)
    ps4 = psum(nc, "a3_ps4", [128, 4, TH], F32)

    scr = nc.dram_tensor("a3_scr", [128, 128], F32, kind="Internal").ap()

    with nc.Block("a3_pre", no_gpsimd_drain=True) as B:
        @B.sync
        def _(s):
            s.dma_start(out=DT[:], in_=io["dt"]).then_inc(*p.inc("yP", 16))
            s.dma_start(out=ALOG[:], in_=io["alog"]).then_inc(*p.inc("yP", 16))
            s.dma_start(out=idf_sb[:], in_=io["idf"]).then_inc(*p.inc("yP", 16))
            s.dma_start(out=idb_sb[:], in_=io["idb"]).then_inc(*p.inc("yP", 16))
            s.dma_start(out=MREP[:], in_=io["mrep"]).then_inc(*p.inc("yP", 16))
            s.dma_start(out=DVEC[:], in_=io["dvec"]).then_inc(*p.inc("yP", 16))
            s.dma_start(out=NRMW[:], in_=io["normw"]).then_inc(*p.inc("yP", 16))
            s.dma_start(out=S_sb[:], in_=io["sinit"]).then_inc(*p.inc("yP", 16))

        @B.gpsimd
        def _(g):
            g.memset(ONES128[:], 1.0)
            g.drain().then_inc(*p.inc("yG", 1))

        @B.scalar
        def _(a):
            a.wait_ge(p.sem("yP"), p.at("yP"))
            a.activation(NEGA[:], ALOG[:], AF.Exp)
            a.drain().then_inc(*p.inc("yA", 1))

        @B.vector
        def _(v):
            v.wait_ge(p.sem("yA"), 1)
            v.tensor_scalar_mul(NEGA[:], NEGA[:], -1.0)
            v.drain()
            v.tensor_copy(SBF[:, 0:4096], S_sb[:, 0:4096])
            v.tensor_copy(SBF[:, 4096:8192], S_sb[:, 4096:8192])
            v.drain().then_inc(*p.inc("yV", 1))

    for st in range(2):
        for cc in range(4):
            ci = st * 4 + cc
            cs = slice(ci * 128, (ci + 1) * 128)
            with nc.Block(f"a3_c{ci}", no_gpsimd_drain=True) as B:
                @B.gpsimd
                def _(g):
                    if ci > 0:
                        g.wait_ge(p.sem("yVS"), 8 * ci)
                        g.wait_ge(p.sem("yVT"), 8 * ci)
                    if ci == 4:
                        g.wait_ge(p.sem("yTO"), 64)
                    g.dma_start(out=XSL[:], in_=io["postc"][ci, :, 0:64, :]).then_inc(
                        *p.inc("yW", 16))
                    g.dma_start(out=BCSL[:], in_=io["postc"][ci, :, 64:80, :]).then_inc(
                        *p.inc("yW", 16))
                zW_sl = p.at("yW")

                @B.vector
                def _(v):
                    v.wait_ge(p.sem("yV"), p.at("yV"))
                    if ci > 0:
                        v.wait_ge(p.sem("yA"), p.at("yA"))
                    v.tensor_scalar_mul(ADT[:], DT[:, cs], NEGA[:])
                    v.drain()
                    v.tensor_tensor_scan(ACUM[:], ADT[:], ONES128[:], 0.0, ALU.add, ALU.mult)
                    v.drain()
                    v.tensor_scalar(DS[:], ACUM[:], ACUM[:, 127:128], None, ALU.subtract)
                    v.drain().then_inc(*p.inc("yV", 1))
                zV_ds = p.at("yV")

                @B.sync
                def _(s):
                    s.wait_ge(p.sem("yV"), zV_ds)
                    s.wait_ge(p.sem("yS"), p.at("yS"))
                    s.dma_start(out=scr, in_=ACUM[:]).then_inc(*p.inc("yS", 16))
                zS_acum = p.at("yS")

                @B.tensor
                def _(t):
                    t.wait_ge(p.sem("yV"), zV_ds)
                    if ci > 0:
                        t.wait_ge(p.sem("yV2"), p.at("yV2"))
                    t.transpose(ps0[:, 0, :], ACUM[:], idf_sb[:]).then_inc(*p.inc("yT", 1))
                zT_a = p.at("yT")

                @B.vector
                def _(v):
                    v.wait_ge(p.sem("yT"), zT_a)
                    v.tensor_copy(ACUMT[:], ps0[:, 0, :])
                    v.drain().then_inc(*p.inc("yV2", 1))
                zV2_a = p.at("yV2")

                @B.tensor
                def _(t):
                    t.wait_ge(p.sem("yV2"), zV2_a)
                    t.transpose(ps0[:, 0, :], DT[:, cs], idf_sb[:]).then_inc(*p.inc("yT", 1))
                zT_d = p.at("yT")

                @B.vector
                def _(v):
                    v.wait_ge(p.sem("yT"), zT_d)
                    v.tensor_copy(DTT[:], ps0[:, 0, :])
                    v.drain().then_inc(*p.inc("yV2", 1))
                zV2_d = p.at("yV2")

                @B.scalar
                def _(a):
                    a.wait_ge(p.sem("yV2"), zV2_d)
                    if ci > 0:
                        a.wait_ge(p.sem("yVY"), 4 * ci)
                    a.activation(EPOS[:], ACUMT[:], AF.Exp)
                    a.activation(DS[:], DS[:], AF.Exp, scale=-1.0)
                    a.activation(DEC[:], ACUM[:, 127:128], AF.Exp)
                    a.drain().then_inc(*p.inc("yA", 1))
                zA_e = p.at("yA")

                @B.tensor
                def _(t):
                    t.wait_ge(p.sem("yA"), zA_e)
                    t.transpose(ps0[:, 0, :], DS[:], idf_sb[:]).then_inc(*p.inc("yT", 1))
                zT_ds = p.at("yT")

                @B.vector
                def _(v):
                    v.wait_ge(p.sem("yT"), zT_ds)
                    v.tensor_copy(DST[:], ps0[:, 0, :])
                    v.drain().then_inc(*p.inc("yV2", 1))
                zV2_ds = p.at("yV2")

                @B.tensor
                def _(t):
                    t.wait_ge(p.sem("yV2"), zV2_ds)
                    t.transpose(ps0[0:1, 0, :], DEC[:], idf_sb[:]).then_inc(*p.inc("yT", 1))
                zT_dr = p.at("yT")

                @B.vector
                def _(v):
                    v.wait_ge(p.sem("yT"), zT_dr)
                    v.tensor_copy(DROW[:], ps0[0:1, 0, :])
                    v.drain().then_inc(*p.inc("yV2", 1))
                zV2_dr = p.at("yV2")

                @B.tensor
                def _(t):
                    t.wait_ge(p.sem("yV2"), zV2_dr)
                    t.matmul(ps0[:, 0, :], cons["ones_r"][:], DROW[:],
                             start=True, stop=True).then_inc(*p.inc("yT", 1))
                zT_db = p.at("yT")

                @B.vector
                def _(v):
                    v.wait_ge(p.sem("yT"), zT_db)
                    v.tensor_copy(DECB[:], ps0[:, 0, :])
                    v.drain().then_inc(*p.inc("yV2", 1))
                zV2_db = p.at("yV2")

                zT_xt = []
                @B.tensor
                def _(t):
                    t.wait_ge(p.sem("yW"), zW_sl)
                    t.wait_ge(p.sem("yV2"), zV2_db)
                    t.wait_ge(p.sem("yVT"), 8 * ci)
                    for bt in range(8):
                        if bt >= 2:
                            t.wait_ge(p.sem("yV2"), zV2_db + bt - 1)
                        for j in range(8):
                            ins = t.transpose(psx[:, bt % 2, j, :],
                                              XSL[:, bt * 8 + j, :], idb_sb[:])
                        ins.then_inc(*p.inc("yT", 1))
                        zT_xt.append(p.at("yT"))

                @B.vector
                def _(v):
                    if ci == 4:
                        v.wait_ge(p.sem("ySO"), 512)
                        v.wait_ge(p.sem("yVO"), 32)
                    for bt in range(8):
                        v.wait_ge(p.sem("yT"), zT_xt[bt])
                        ins = v.tensor_tensor(
                            XDT[:, bt * 1024:(bt + 1) * 1024].rearrange(
                                "t (h z) -> t h z", h=16),
                            psx[:, bt % 2, :, :].rearrange("t j z -> t (j z)").rearrange(
                                "t (h z) -> t h z", h=16),
                            DTT[:, bt * 16:(bt + 1) * 16].unsqueeze(2).broadcast_to(
                                [128, 16, 64]),
                            ALU.mult)
                        ins.then_inc(*p.inc("yV2", 1))
                zV2_xt = p.at("yV2")

                @B.tensor
                def _(t):
                    t.wait_ge(p.sem("yW"), zW_sl)
                    for gq in range(2):
                        if ci > 0 or gq > 0:
                            t.wait_ge(p.sem("yV2"), zV2_xt + gq)
                        for g4 in range(4):
                            g = gq * 4 + g4
                            ins = t.matmul(psg[:, g4, :], BCSL[:, g, :],
                                           BCSL[:, 8 + g, :], start=True, stop=True)
                        ins.then_inc(*p.inc("yT", 1))
                zT_gm = p.at("yT")

                @B.vector
                def _(v):
                    for gq in range(2):
                        v.wait_ge(p.sem("yT"), zT_gm - 1 + gq)
                        ins = v.tensor_copy(
                            GMT[:, gq * 4:(gq + 1) * 4, :].rearrange("s j l -> s (j l)"),
                            psg[:].rearrange("s j l -> s (j l)"))
                        ins.then_inc(*p.inc("yV2", 1))
                zV2_gm = p.at("yV2")

                @B.tensor
                def _(t):
                    t.wait_ge(p.sem("yV2"), zV2_xt)
                    for j in range(8):
                        ins = t.transpose(psx[:, 0, j, :], BCSL[:, j, :], idb_sb[:])
                    ins.then_inc(*p.inc("yT", 1))
                zT_bt = p.at("yT")

                @B.vector
                def _(v):
                    v.wait_ge(p.sem("yT"), zT_bt)
                    v.tensor_copy(BT[:].rearrange("t j z -> t (j z)"),
                                  psx[:, 0, :, :].rearrange("t j z -> t (j z)"))
                    v.drain().then_inc(*p.inc("yV2", 1))
                zV2_bt = p.at("yV2")

                pass

            for Q in range(4):
                with nc.Block(f"a3_c{ci}q{Q}", no_gpsimd_drain=True) as BQ:
                    zS_rows = []
                    zT_lm = []
                    zV_mt = []
                    @BQ.sync
                    def _(s):
                        s.wait_ge(p.sem("yS"), zS_acum)
                        for hh in range(4):
                            h8 = Q * 4 + hh
                            if 16 * ci + Q * 4 + hh >= 2:
                                s.wait_ge(p.sem("yTL"), 16 * ci + Q * 4 + hh - 1)
                            s.dma_start(
                                out=ROWS[:, h8 % 2, :],
                                in_=scr[h8 * 8:(h8 + 1) * 8, :].rearrange(
                                    "h t -> (h t)").unsqueeze(0)).then_inc(
                                *p.inc("yS", 16))
                            zS_rows.append(p.at("yS"))

                    @BQ.tensor
                    def _(t):
                        if Q > 0:
                            t.wait_ge(p.sem("yVY"), 4 * ci + Q)
                        elif ci > 0:
                            t.wait_ge(p.sem("yVS"), 8 * ci)
                            if ci == 4:
                                t.wait_ge(p.sem("yVO"), 32)
                        for hh in range(4):
                            h8 = Q * 4 + hh
                            rot = hh % 2
                            t.wait_ge(p.sem("yS"), zS_rows[hh])
                            idx = Q * 4 + hh
                            if idx < 2:
                                if ci > 0:
                                    t.wait_ge(p.sem("yVL"), 16 * ci)
                            else:
                                t.wait_ge(p.sem("yVL"), 16 * ci + idx - 1)
                            for half in range(2):
                                hs = slice(half * TH, (half + 1) * TH)
                                t.matmul(ps4[:, rot * 2 + half, :], cons["ones_r"][:],
                                         ROWS[:, h8 % 2, hs], start=True, stop=False)
                                ins = t.matmul(ps4[:, rot * 2 + half, :], idb_sb[:],
                                               MREP[:, hs], start=False, stop=True)
                            ins.then_inc(*p.inc("yTL", 1))
                            zT_lm.append(p.at("yTL"))

                    @BQ.vector
                    def _(v):
                        for hh in range(4):
                            h8 = Q * 4 + hh
                            rot = hh % 2
                            idx = Q * 4 + hh
                            v.wait_ge(p.sem("yTL"), zT_lm[hh])
                            if idx >= 2:
                                v.wait_ge(p.sem("yAL"), 16 * ci + idx - 1)
                            ins = v.tensor_tensor(
                                DIFF[:, rot, :].rearrange("s (h l) -> s h l", h=8),
                                ps4[:, rot * 2:rot * 2 + 2, :].rearrange(
                                    "s b l -> s (b l)").rearrange(
                                    "s (h l) -> s h l", h=8),
                                ACUMT[:, h8 * 8:(h8 + 1) * 8].unsqueeze(2).broadcast_to(
                                    [128, 8, 128]),
                                ALU.subtract)
                            ins.then_inc(*p.inc("yVL", 1))

                    @BQ.scalar
                    def _(a):
                        for hh in range(4):
                            rot = hh % 2
                            idx = Q * 4 + hh
                            a.wait_ge(p.sem("yVL"), 16 * ci + idx + 1)
                            if idx >= 2:
                                a.wait_ge(p.sem("yVM"), 16 * ci + idx - 1)
                            a.activation(LT8[:, rot, :], DIFF[:, rot, :], AF.Exp)
                            a.drain().then_inc(*p.inc("yAL", 1))

                    @BQ.vector
                    def _(v):
                        for hh in range(4):
                            h8 = Q * 4 + hh
                            rot = hh % 2
                            g = h8 // 2
                            v.wait_ge(p.sem("yAL"), 16 * ci + Q * 4 + hh + 1)
                            ins = v.tensor_tensor(
                                MT_Q[:, hh, :].rearrange("s (h l) -> s h l", h=8),
                                LT8[:, rot, :].rearrange("s (h l) -> s h l", h=8),
                                GMT[:, g, :].unsqueeze(1).broadcast_to([128, 8, 128]),
                                ALU.mult)
                            ins.then_inc(*p.inc("yVM", 1))

                    @BQ.tensor
                    def _(t):
                        t.wait_ge(p.sem("yVM"), 16 * ci + 4 * (Q + 1))
                        t.wait_ge(p.sem("yV2"), zV2_xt)
                        t.wait_ge(p.sem("yVS"), 8 * ci)
                        for gg in range(2):
                            g = Q * 2 + gg
                            for half in range(2):
                                ins = t.matmul(
                                    ps4[:, gg * 2 + half, :],
                                    BCSL[:, 8 + g, :],
                                    SBF[:, g * 1024 + half * TH:g * 1024 + (half + 1) * TH],
                                    start=True, stop=True)
                        ins.then_inc(*p.inc("yTY1", 1))

                    @BQ.vector
                    def _(v):
                        v.wait_ge(p.sem("yTY1"), 4 * ci + Q + 1)
                        for b4 in range(4):
                            hbase = Q * 32 + b4 * 8
                            ins = v.tensor_tensor(
                                YTM[:, b4 * 512:(b4 + 1) * 512].rearrange(
                                    "t (h z) -> t h z", h=8),
                                ps4[:, b4, :].rearrange("t (h z) -> t h z", h=8),
                                EPOS[:, hbase:hbase + 8].unsqueeze(2).broadcast_to(
                                    [128, 8, 64]),
                                ALU.mult)
                        ins.then_inc(*p.inc("yVY1", 1))

                    @BQ.tensor
                    def _(t):
                        t.wait_ge(p.sem("yVY1"), 4 * ci + Q + 1)
                        for hh in range(4):
                            for j in range(8):
                                h = Q * 32 + hh * 8 + j
                                ins = t.matmul(
                                    ps4[:, hh, j * 64:(j + 1) * 64],
                                    MT_Q[:, hh, j * 128:(j + 1) * 128],
                                    XDT[:, h * 64:(h + 1) * 64],
                                    start=(j == 0),
                                    stop=(j == 7))
                        ins.then_inc(*p.inc("yTY", 1))

                    @BQ.vector
                    def _(v):
                        v.wait_ge(p.sem("yTY"), 4 * ci + Q + 1)
                        for b4 in range(4):
                            ins = v.tensor_add(YTM[:, b4 * 512:(b4 + 1) * 512],
                                               ps4[:, b4, :],
                                               YTM[:, b4 * 512:(b4 + 1) * 512])
                        ins.then_inc(*p.inc("yVY", 1))

                    @BQ.tensor
                    def _(t):
                        t.wait_ge(p.sem("yVY"), 4 * ci + Q + 1)
                        for bt in range(2):
                            if True:
                                t.wait_ge(p.sem("yVT"), p.at("yVT"))
                            for j in range(8):
                                ins = t.transpose(psx[:, bt, j, :],
                                                  YTM[:, (bt * 8 + j) * 128:
                                                      (bt * 8 + j + 1) * 128], idb_sb[:])
                            ins.then_inc(*p.inc("yTT", 1))

                    @BQ.vector
                    def _(v):
                        if ci == 4 and Q == 0:
                            v.wait_ge(p.sem("yTO"), 64)
                        for bt in range(2):
                            v.wait_ge(p.sem("yTT"), 8 * ci + Q * 2 + bt + 1)
                            for j in range(8):
                                tile = Q * 16 + bt * 8 + j
                                ins = v.scalar_tensor_tensor(
                                    y_fm[:, tile, cc * 128:(cc + 1) * 128],
                                    XSL[:, tile, :], DVEC[:, tile:tile + 1],
                                    psx[:, bt, j, :], ALU.mult, ALU.add)
                            ins.then_inc(*p.inc("yVT", 1))

            with nc.Block(f"a3_c{ci}s", no_gpsimd_drain=True) as BS:
                @BS.vector
                def _(v):
                    v.wait_ge(p.sem("yTY"), 4 * ci + 4)
                    for bt in range(8):
                        ins = v.tensor_tensor(
                            XDT[:, bt * 1024:(bt + 1) * 1024].rearrange(
                                "t (h z) -> t h z", h=16),
                            XDT[:, bt * 1024:(bt + 1) * 1024].rearrange(
                                "t (h z) -> t h z", h=16),
                            DST[:, bt * 16:(bt + 1) * 16].unsqueeze(2).broadcast_to(
                                [128, 16, 64]),
                            ALU.mult)
                    ins.then_inc(*p.inc("yVD", 1))
                zV_xd = p.at("yVD")

                zT_rw = []
                @BS.tensor
                def _(t):
                    t.wait_ge(p.sem("yVD"), zV_xd)
                    t.wait_ge(p.sem("yVY"), 4 * ci + 4)
                    for g in range(8):
                        bk = (g % 2) * 2
                        if g >= 2:
                            t.wait_ge(p.sem("yVS"), 8 * ci + g - 1)
                        for half in range(2):
                            ins = t.matmul(ps4[:, bk + half, :], BT[:, g, :],
                                           XDT[:, g * 1024 + half * TH:
                                               g * 1024 + (half + 1) * TH],
                                           start=True, stop=True)
                        ins.then_inc(*p.inc("yTS", 1))
                        zT_rw.append(p.at("yTS"))

                @BS.vector
                def _(v):
                    for g in range(8):
                        bk = (g % 2) * 2
                        v.wait_ge(p.sem("yTS"), zT_rw[g])
                        for half in range(2):
                            gs = slice(g * 1024 + half * TH, g * 1024 + (half + 1) * TH)
                            v.tensor_tensor(
                                TMPS[:].rearrange("n (h z) -> n h z", h=8),
                                S_sb[:, gs].rearrange("n (h z) -> n h z", h=8),
                                DECB[:, g * 16 + half * 8:g * 16 + (half + 1) * 8
                                     ].unsqueeze(2).broadcast_to([128, 8, 64]),
                                ALU.mult)
                            v.tensor_add(S_sb[:, gs], TMPS[:], ps4[:, bk + half, :])
                            ins = v.tensor_copy(SBF[:, gs], S_sb[:, gs])
                        ins.then_inc(*p.inc("yVS", 1))

        with nc.Block(f"a3_t{st}", no_gpsimd_drain=True) as BT_:
            @BT_.scalar
            def _(a):
                a.wait_ge(p.sem("yVT"), p.at("yVT"))
                for j in range(64):
                    if j >= 8:
                        a.wait_ge(p.sem("yTR"), 64 * st + j - 7)
                    a.activation(YSQ[:, j % 8, :], y_fm[:, j, :], AF.Square)
                    a.drain().then_inc(*p.inc("yAR", 1))

            @BT_.tensor
            def _(t):
                t.wait_ge(p.sem("yVS"), 32 * (st + 1))
                for j in range(64):
                    t.wait_ge(p.sem("yAR"), 64 * st + j + 1)
                    ins = t.matmul(ps4[0:1, 0, :], cons["ones_b"][:], YSQ[:, j % 8, :],
                                   start=(j == 0), stop=(j == 63))
                    ins.then_inc(*p.inc("yTR", 1))

            @BT_.vector
            def _(v):
                v.wait_ge(p.sem("yTR"), 64 * (st + 1))
                v.tensor_scalar(RMSV[:], ps4[0:1, 0, :], 1.0 / 8192, 1e-5,
                                ALU.mult, ALU.add)
                v.drain().then_inc(*p.inc("yVR", 1))

            @BT_.scalar
            def _(a):
                a.wait_ge(p.sem("yVR"), 3 * st + 1)
                a.activation(RMSV[:], RMSV[:], AF.Sqrt)
                a.drain().then_inc(*p.inc("yAR2", 1))

            @BT_.vector
            def _(v):
                v.wait_ge(p.sem("yAR2"), st + 1)
                v.reciprocal(RMSV[:], RMSV[:])
                v.drain().then_inc(*p.inc("yVR", 1))

            @BT_.tensor
            def _(t):
                t.wait_ge(p.sem("yVR"), 3 * st + 2)
                t.matmul(ps4[:, 1, 0:TH], cons["ones_r"][:], RMSV[:],
                         start=True, stop=True).then_inc(*p.inc("yTR2", 1))

            @BT_.vector
            def _(v):
                v.wait_ge(p.sem("yTR2"), st + 1)
                v.tensor_copy(RSTB[:], ps4[:, 1, 0:TH])
                v.drain().then_inc(*p.inc("yVR", 1))
            zV_rst = p.at("yVR")

            @BT_.sync
            def _(s):
                for j in range(64):
                    if j >= 2:
                        s.wait_ge(p.sem("yAG"), 64 * st + j - 1)
                    s.dma_start(out=GT[:, j % 2, :],
                                in_=io["gate"][st, j]).then_inc(*p.inc("ySG", 16))

            @BT_.scalar
            def _(a):
                for j in range(64):
                    a.wait_ge(p.sem("ySG"), 1024 * st + 16 * (j + 1))
                    if j >= 2:
                        a.wait_ge(p.sem("yVG"), 64 * st + j - 1)
                    a.activation(SG[:, j % 2, :], GT[:, j % 2, :], AF.Silu)
                    a.drain().then_inc(*p.inc("yAG", 1))

            @BT_.vector
            def _(v):
                v.wait_ge(p.sem("yVR"), zV_rst)
                for j in range(64):
                    v.wait_ge(p.sem("yAG"), 64 * st + j + 1)
                    v.scalar_tensor_tensor(T2[:], y_fm[:, j, :], NRMW[:, j:j + 1],
                                           RSTB[:], ALU.mult, ALU.mult)
                    ins = v.tensor_mul(y_fm[:, j, :], T2[:], SG[:, j % 2, :])
                    ins.then_inc(*p.inc("yVG", 1))
            zV_g = p.at("yVG")

            @BT_.gpsimd
            def _(g):
                g.wait_ge(p.sem("yVT"), 32 * (st + 1))
                g.wait_ge(p.sem("yVS"), 32 * (st + 1))
                for og in range(8):
                    for chp in range(8):
                        pi = og * 8 + chp
                        if 64 * st + pi >= 2:
                            g.wait_ge(p.sem("yTO"), 64 * st + pi - 1)
                        g.dma_start(
                            out=WP[:, pi % 2, :, :],
                            in_=io["wo"][og, chp]).then_inc(*p.inc("yWO", 16))

            @BT_.scalar
            def _(a):
                for og in range(8):
                    for dt4 in range(4):
                        ri = og * 4 + dt4
                        if st > 0 or ri >= 4:
                            a.wait_ge(p.sem("yVO"), 32 * st + ri - 3)
                        a.dma_start(out=XR[:, ri % 4, :],
                                    in_=io["x_in"][(og * 4 + dt4) * 128:
                                                   (og * 4 + dt4 + 1) * 128,
                                                   st * TH:(st + 1) * TH]).then_inc(
                            *p.inc("yXR", 16))

            @BT_.tensor
            def _(t):
                t.wait_ge(p.sem("yVG"), zV_g)
                t.wait_ge(p.sem("yVS"), 32 * (st + 1))
                for og in range(8):
                    if og >= 1:
                        t.wait_ge(p.sem("yVO"), 32 * st + 4 * og)
                    elif st > 0:
                        t.wait_ge(p.sem("yVO"), 32 * st)
                    for chp in range(8):
                        pi = og * 8 + chp
                        t.wait_ge(p.sem("yWO"), 1024 * st + 16 * (pi + 1))
                        for j in range(8):
                            for dt4 in range(4):
                                ins = t.matmul(ps4[:, dt4, :],
                                               WP[:, pi % 2, j, dt4 * 128:(dt4 + 1) * 128],
                                               y_fm[:, chp * 8 + j, :],
                                               start=(chp == 0 and j == 0),
                                               stop=(chp == 7 and j == 7))
                        ins.then_inc(*p.inc("yTO", 1))

            @BT_.vector
            def _(v):
                for og in range(8):
                    for dt4 in range(4):
                        ri = og * 4 + dt4
                        v.wait_ge(p.sem("yTO"), 64 * st + 8 * (og + 1))
                        v.wait_ge(p.sem("yXR"), 512 * st + 16 * (ri + 1))
                        if st > 0 or ri >= 4:
                            v.wait_ge(p.sem("ySO"), 512 * st + 16 * (ri - 3))
                        ins = v.tensor_add(STG[:, ri % 4, :], ps4[:, dt4, :],
                                           XR[:, ri % 4, :])
                        ins.then_inc(*p.inc("yVO", 1))

            @BT_.sync
            def _(s):
                for og in range(8):
                    for dt4 in range(4):
                        ri = og * 4 + dt4
                        s.wait_ge(p.sem("yVO"), 32 * st + ri + 1)
                        s.dma_start(out=io["x_out"][(og * 4 + dt4) * 128:
                                                    (og * 4 + dt4 + 1) * 128,
                                                    st * TH:(st + 1) * TH],
                                    in_=STG[:, ri % 4, :]).then_inc(*p.inc("ySO", 16))

    with nc.Block("a3_fin", no_gpsimd_drain=True) as B:
        @B.sync
        def _(s):
            s.wait_ge(p.sem("ySO"), p.at("ySO"))

    return p



def _c(a):
    return np.ascontiguousarray(a)


def _tile_w(WT, ow=512):
    OC = WT.shape[1]
    nog = (OC + ow - 1) // ow
    out = np.zeros((nog, 128, DK, ow), np.float32)
    A = WT.reshape(DK, 128, OC)
    for og in range(nog):
        w = min(ow, OC - og * ow)
        out[og, :, :, 0:w] = A[:, :, og * ow:og * ow + w].transpose(1, 0, 2)
    return out


def build_l0attn(upto=5):
    nc = bass.Bass("TRN2", target_bir_lowering=False)
    io = {
        "x_in": nc.dram_tensor("x_in", [D, TOK], F32, kind="ExternalInput").ap(),
        "wq": nc.dram_tensor("wq", [2, 128, DK, 512], F32, kind="ExternalInput").ap(),
        "bq": nc.dram_tensor("bq", [128, 8], F32, kind="ExternalInput").ap(),
        "m": nc.dram_tensor("m", [8, 128, 128], F32, kind="ExternalInput").ap(),
        "z": nc.dram_tensor("z", [8, 128], F32, kind="ExternalInput").ap(),
        "wo4": nc.dram_tensor("wo4", [8, 4, 128, 4096], F32, kind="ExternalInput").ap(),
        "idb": nc.dram_tensor("idb", [128, 128], BF16, kind="ExternalInput").ap(),
        "x_out": nc.dram_tensor("x_out", [D, TOK], F32, kind="ExternalOutput").ap(),
    }
    p = Prog(nc)
    emit_l0attn(p, io, upto=upto)
    return nc


def host_prep_l0attn(inputs):
    Wq = np.asarray(inputs["L0_attn_Wq"], np.float32)
    sel = np.concatenate([np.arange(4 * h * 128, 4 * h * 128 + 128) for h in range(8)])
    return {
        "wq": _tile_w(_c(Wq.T[:, sel])),
        "bq": _c(np.asarray(inputs["L0_attn_bq"], np.float32)[sel].reshape(8, 128).T),
        "m": _c(np.asarray(inputs["L0_attn_M"], np.float32)),
        "z": _c(np.asarray(inputs["L0_attn_z"], np.float32)),
        "wo4": _c(np.asarray(inputs["L0_attn_Wo"], np.float32).T.reshape(8, 4, 128, 4096)),
        "idb": np.eye(128, dtype=ml_dtypes.bfloat16),
    }


def build_ffn(cfg, dbg=False):
    nc = bass.Bass("TRN2", target_bir_lowering=False)
    E, ER = cfg["E"], cfg["ER"]
    io = {
        "x_in": nc.dram_tensor("x_in", [D, TOK], F32, kind="ExternalInput").ap(),
        "wb": nc.dram_tensor("wb", [8, 128, DK, 512], F32, kind="ExternalInput").ap(),
        "wr": nc.dram_tensor("wr", [D, 8], F32, kind="ExternalInput").ap(),
        "at": nc.dram_tensor("at", [128, DK, 128], F32, kind="ExternalInput").ap(),
        "bcat": nc.dram_tensor("bcat", [ER, D], F32, kind="ExternalInput").ap(),
        "e8": nc.dram_tensor("e8", [E, 128], F32, kind="ExternalInput").ap(),
        "idf": nc.dram_tensor("idf", [128, 128], F32, kind="ExternalInput").ap(),
        "x_out": nc.dram_tensor("x_out", [D, TOK], F32, kind="ExternalOutput").ap(),
    }
    if cfg["base_bias"]:
        io["bb"] = nc.dram_tensor("bb", [128, 32], F32, kind="ExternalInput").ap()
    if cfg["logit_bias"]:
        io["eb"] = nc.dram_tensor("eb", [E], F32, kind="ExternalInput").ap()
    if dbg:
        io["dbg_logits"] = nc.dram_tensor("dbg_logits", [8, TOK], F32, kind="ExternalOutput").ap()
        io["dbg_comb"] = nc.dram_tensor("dbg_comb", [8, TOK], F32, kind="ExternalOutput").ap()
        io["dbg_hl"] = nc.dram_tensor("dbg_hl", [128, TOK], mybir.dt.bfloat16, kind="ExternalOutput").ap()
        io["dbg_hw"] = nc.dram_tensor("dbg_hw", [128, TOK], mybir.dt.bfloat16, kind="ExternalOutput").ap()
        io["dbg_lt"] = nc.dram_tensor("dbg_lt", [128, 48], F32, kind="ExternalOutput").ap()
    p = Prog(nc)
    emit_ffn(p, io, cfg)
    return nc


CFG_L0FFN = dict(tag="f0", E=8, ER=128, top1=False, scaling=2.0, base_bias=True,
                 logit_bias=False)
CFG_L1FFN = dict(tag="f1", E=4, ER=32, top1=True, scaling=1.0, base_bias=False,
                 logit_bias=True)


def host_prep_l0ffn(inputs):
    A = np.asarray(inputs["L0_ffn_A"], np.float32)
    Bm = np.asarray(inputs["L0_ffn_B"], np.float32)
    e8 = np.zeros((8, 128), np.float32)
    for e in range(8):
        e8[e, e * 16:(e + 1) * 16] = 1.0
    return {
        "wb": _tile_w(np.asarray(inputs["L0_ffn_W_base"], np.float32).T),
        "wr": _c(np.asarray(inputs["L0_ffn_W_router"], np.float32).T),
        "at": _c(A.reshape(128, D).T.reshape(DK, 128, 128).transpose(1, 0, 2)),
        "bcat": _c(Bm.transpose(0, 2, 1).reshape(128, D)),
        "e8": e8,
        "idf": np.eye(128, dtype=np.float32),
        "bb": _c(np.asarray(inputs["L0_ffn_b_base"], np.float32).reshape(32, 128).T),
    }


def host_prep_l1ffn(inputs):
    A = np.asarray(inputs["L1_ffn_lora_A"], np.float32)
    Bm = np.asarray(inputs["L1_ffn_lora_B"], np.float32)
    e8 = np.zeros((4, 128), np.float32)
    for e in range(4):
        e8[e, e * 8:(e + 1) * 8] = 1.0
    return {
        "wb": _tile_w(np.asarray(inputs["L1_ffn_W_up"], np.float32).T),
        "wr": _c(np.concatenate([np.asarray(inputs["L1_ffn_gate_w"], np.float32).T,
                                 np.zeros((D, 4), np.float32)], axis=1)),
        "at": _c(np.pad(A.reshape(32, D).T.reshape(DK, 128, 32), ((0,0),(0,0),(0,96))
                       ).transpose(1, 0, 2)),
        "bcat": _c(Bm.transpose(0, 2, 1).reshape(32, D)),
        "e8": e8,
        "idf": np.eye(128, dtype=np.float32),
        "eb": _c(np.asarray(inputs["L1_ffn_expert_bias"], np.float32)),
    }


def run_ffn(x_fm, shared, cfg):
    nc = build_ffn(cfg)
    in_maps = []
    for c in range(8):
        m = dict(shared)
        m["x_in"] = _c(x_fm[:, c * TOK:(c + 1) * TOK])
        in_maps.append(m)
    res = run_bass_kernel_spmd(nc, in_maps, core_ids=list(range(8)))
    return np.concatenate([res.results[c]["x_out"] for c in range(8)], axis=1)


def build_l1a1():
    nc = bass.Bass("TRN2", target_bir_lowering=False)
    io = {
        "x_in": nc.dram_tensor("x_in", [D, TOK], F32, kind="ExternalInput").ap(),
        "wi": nc.dram_tensor("wi", [37, 128, DK, 512], F32, kind="ExternalInput").ap(),
        "gate": nc.dram_tensor("gate", [2, 64, 128, TH], BF16, kind="ExternalOutput").ap(),
        "hbc": nc.dram_tensor("hbc", [10240, TOK], BF16, kind="ExternalOutput").ap(),
        "dtr": nc.dram_tensor("dtr", [128, TOK], F32, kind="ExternalOutput").ap(),
    }
    p = Prog(nc)
    emit_l1a1(p, io)
    return nc


def build_l1a2():
    nc = bass.Bass("TRN2", target_bir_lowering=False)
    io = {
        "hbc": nc.dram_tensor("hbc", [10240, TOK], BF16, kind="ExternalInput").ap(),
        "halo": nc.dram_tensor("halo", [10240, 3], BF16, kind="ExternalInput").ap(),
        "dtr": nc.dram_tensor("dtr", [128, TOK], F32, kind="ExternalInput").ap(),
        "cw": nc.dram_tensor("cw", [128, 80, 4], F32, kind="ExternalInput").ap(),
        "cb": nc.dram_tensor("cb", [128, 80], F32, kind="ExternalInput").ap(),
        "dtbias": nc.dram_tensor("dtbias", [128, 1], F32, kind="ExternalInput").ap(),
        "alog": nc.dram_tensor("alog", [128, 1], F32, kind="ExternalInput").ap(),
        "idf": nc.dram_tensor("idf", [128, 128], F32, kind="ExternalInput").ap(),
        "idb": nc.dram_tensor("idb", [128, 128], BF16, kind="ExternalInput").ap(),
        "postc": nc.dram_tensor("postc", [8, 128, 80, 128], BF16, kind="ExternalOutput").ap(),
        "dt": nc.dram_tensor("dt", [128, TOK], F32, kind="ExternalOutput").ap(),
        "send": nc.dram_tensor("send", [128, 8192], F32, kind="ExternalOutput").ap(),
    }
    p = Prog(nc)
    emit_l1a2(p, io)
    return nc


def build_l1a3():
    nc = bass.Bass("TRN2", target_bir_lowering=False)
    io = {
        "postc": nc.dram_tensor("postc", [8, 128, 80, 128], BF16, kind="ExternalInput").ap(),
        "gate": nc.dram_tensor("gate", [2, 64, 128, TH], BF16, kind="ExternalInput").ap(),
        "dt": nc.dram_tensor("dt", [128, TOK], F32, kind="ExternalInput").ap(),
        "x_in": nc.dram_tensor("x_in", [D, TOK], F32, kind="ExternalInput").ap(),
        "sinit": nc.dram_tensor("sinit", [128, 8192], F32, kind="ExternalInput").ap(),
        "alog": nc.dram_tensor("alog", [128, 1], F32, kind="ExternalInput").ap(),
        "normw": nc.dram_tensor("normw", [128, 64], F32, kind="ExternalInput").ap(),
        "dvec": nc.dram_tensor("dvec", [128, 64], F32, kind="ExternalInput").ap(),
        "mrep": nc.dram_tensor("mrep", [128, 1024], BF16, kind="ExternalInput").ap(),
        "wo": nc.dram_tensor("wo", [8, 8, 128, 8, 512], F32, kind="ExternalInput").ap(),
        "idf": nc.dram_tensor("idf", [128, 128], F32, kind="ExternalInput").ap(),
        "idb": nc.dram_tensor("idb", [128, 128], BF16, kind="ExternalInput").ap(),
        "x_out": nc.dram_tensor("x_out", [D, TOK], F32, kind="ExternalOutput").ap(),
    }
    p = Prog(nc)
    emit_l1a3(p, io)
    return nc


def build_launchA():
    nc = bass.Bass("TRN2", target_bir_lowering=False)
    E, ER = CFG_L0FFN["E"], CFG_L0FFN["ER"]
    io = {
        "x_in": nc.dram_tensor("x_in", [D, TOK], F32, kind="ExternalInput").ap(),
        "wq": nc.dram_tensor("wq", [2, 128, DK, 512], F32, kind="ExternalInput").ap(),
        "bq": nc.dram_tensor("bq", [128, 8], F32, kind="ExternalInput").ap(),
        "m": nc.dram_tensor("m", [8, 128, 128], F32, kind="ExternalInput").ap(),
        "z": nc.dram_tensor("z", [8, 128], F32, kind="ExternalInput").ap(),
        "wo4": nc.dram_tensor("wo4", [8, 4, 128, 4096], F32, kind="ExternalInput").ap(),
        "idb": nc.dram_tensor("idb", [128, 128], BF16, kind="ExternalInput").ap(),
        "wb": nc.dram_tensor("wb", [8, 128, DK, 512], F32, kind="ExternalInput").ap(),
        "wr": nc.dram_tensor("wr", [D, 8], F32, kind="ExternalInput").ap(),
        "at": nc.dram_tensor("at", [128, DK, 128], F32, kind="ExternalInput").ap(),
        "bcat": nc.dram_tensor("bcat", [ER, D], F32, kind="ExternalInput").ap(),
        "e8": nc.dram_tensor("e8", [E, 128], F32, kind="ExternalInput").ap(),
        "idf": nc.dram_tensor("idf", [128, 128], F32, kind="ExternalInput").ap(),
        "bb": nc.dram_tensor("bb", [128, 32], F32, kind="ExternalInput").ap(),
        "wi": nc.dram_tensor("wi", [37, 128, DK, 512], F32, kind="ExternalInput").ap(),
        "x2": nc.dram_tensor("x2", [D, TOK], F32, kind="ExternalOutput").ap(),
        "gate": nc.dram_tensor("gate", [2, 64, 128, TH], BF16, kind="ExternalOutput").ap(),
        "hbc": nc.dram_tensor("hbc", [10240, TOK], BF16, kind="ExternalOutput").ap(),
        "dtr": nc.dram_tensor("dtr", [128, TOK], F32, kind="ExternalOutput").ap(),
    }
    x1 = nc.dram_tensor("x1_scr", [D, TOK], F32, kind="Internal").ap()
    p = Prog(nc)
    get_cons(p)
    base0 = nc.sbuf_base
    with nc.named_scope("l0attn"):
        emit_l0attn(p, dict(io, x_out=x1))
    done = ("oS", p.at("oS"))
    nc.sbuf_base = base0
    nc.psum_base = 0
    with nc.named_scope("l0ffn"):
        emit_ffn(p, dict(io, x_in=x1, x_out=io["x2"]), CFG_L0FFN, entry=done)
    done = ("fS", p.at("fS"))
    nc.sbuf_base = base0
    nc.psum_base = 0
    with nc.named_scope("l1a1"):
        emit_l1a1(p, dict(io, x_in=io["x2"]), entry=done)
    with nc.Block("la_fin", no_gpsimd_drain=True) as B:
        @B.sync
        def _(s):
            s.wait_ge(p.sem("iS"), p.at("iS"))
            s.wait_ge(p.sem("fS"), p.at("fS"))
    return nc


def build_launchC():
    nc = bass.Bass("TRN2", target_bir_lowering=False)
    E, ER = CFG_L1FFN["E"], CFG_L1FFN["ER"]
    io = {
        "postc": nc.dram_tensor("postc", [8, 128, 80, 128], BF16, kind="ExternalInput").ap(),
        "gate": nc.dram_tensor("gate", [2, 64, 128, TH], BF16, kind="ExternalInput").ap(),
        "dt": nc.dram_tensor("dt", [128, TOK], F32, kind="ExternalInput").ap(),
        "x_in": nc.dram_tensor("x_in", [D, TOK], F32, kind="ExternalInput").ap(),
        "sinit": nc.dram_tensor("sinit", [128, 8192], F32, kind="ExternalInput").ap(),
        "alog": nc.dram_tensor("alog", [128, 1], F32, kind="ExternalInput").ap(),
        "normw": nc.dram_tensor("normw", [128, 64], F32, kind="ExternalInput").ap(),
        "dvec": nc.dram_tensor("dvec", [128, 64], F32, kind="ExternalInput").ap(),
        "mrep": nc.dram_tensor("mrep", [128, 1024], BF16, kind="ExternalInput").ap(),
        "wo": nc.dram_tensor("wo", [8, 8, 128, 8, 512], F32, kind="ExternalInput").ap(),
        "idf": nc.dram_tensor("idf", [128, 128], F32, kind="ExternalInput").ap(),
        "idb": nc.dram_tensor("idb", [128, 128], BF16, kind="ExternalInput").ap(),
        "wb": nc.dram_tensor("wb", [8, 128, DK, 512], F32, kind="ExternalInput").ap(),
        "wr": nc.dram_tensor("wr", [D, 8], F32, kind="ExternalInput").ap(),
        "at": nc.dram_tensor("at", [128, DK, 128], F32, kind="ExternalInput").ap(),
        "bcat": nc.dram_tensor("bcat", [ER, D], F32, kind="ExternalInput").ap(),
        "e8": nc.dram_tensor("e8", [E, 128], F32, kind="ExternalInput").ap(),
        "eb": nc.dram_tensor("eb", [E], F32, kind="ExternalInput").ap(),
        "x_out": nc.dram_tensor("x_out", [D, TOK], F32, kind="ExternalOutput").ap(),
    }
    x3 = nc.dram_tensor("x3_scr", [D, TOK], F32, kind="Internal").ap()
    p = Prog(nc)
    get_cons(p)
    base0 = nc.sbuf_base
    with nc.named_scope("l1a3"):
        emit_l1a3(p, dict(io, x_out=x3))
    done = ("ySO", p.at("ySO"))
    nc.sbuf_base = base0
    nc.psum_base = 0
    with nc.named_scope("l1ffn"):
        emit_ffn(p, dict(io, x_in=x3), CFG_L1FFN, entry=done)
    with nc.Block("lc_fin", no_gpsimd_drain=True) as B:
        @B.sync
        def _(s):
            s.wait_ge(p.sem("fS"), p.at("fS"))
    return nc


def host_prep_l1a(inputs):
    cw = np.asarray(inputs["L1_attn_conv_w"], np.float32)[:, 0, :]
    return {
        "wi": _tile_w(np.asarray(inputs["L1_attn_in_proj_w"], np.float32).T),
        "cw": _c(cw.reshape(80, 128, 4).transpose(1, 0, 2)),
        "cb": _c(np.asarray(inputs["L1_attn_conv_b"], np.float32).reshape(80, 128).T),
        "dtbias": _c(np.asarray(inputs["L1_attn_dt_bias"], np.float32).reshape(128, 1)),
        "alog": _c(np.asarray(inputs["L1_attn_A_log"], np.float32).reshape(128, 1)),
        "idf": np.eye(128, dtype=np.float32),
        "idb": np.eye(128, dtype=ml_dtypes.bfloat16),
        "wo": _c(np.transpose(np.asarray(inputs["L1_attn_out_proj_w"], np.float32
                  ).T.reshape(8, 8, 128, 8, 512), (3, 0, 2, 1, 4))),
        "normw": _c(np.asarray(inputs["L1_attn_norm_w"], np.float32).reshape(64, 128).T),
        "dvec": _c(np.repeat(np.asarray(inputs["L1_attn_D"], np.float32), 64
                             ).reshape(64, 128).T),
        "mrep": np.tile(np.where(np.triu(np.ones((128, 128), bool)), 0.0, -30000.0
                                 ).astype(ml_dtypes.bfloat16), (1, 8)),
    }


def x_to_fm(x):
    return _c(np.asarray(x, np.float32).reshape(8192, 4096).T)


def fm_to_x(xfm):
    return _c(xfm.T).reshape(4, 2048, 4096)


def run_l0attn(x_fm, inputs):
    nc = build_l0attn()
    shared = host_prep_l0attn(inputs)
    in_maps = []
    for c in range(8):
        m = dict(shared)
        m["x_in"] = _c(x_fm[:, c * TOK:(c + 1) * TOK])
        in_maps.append(m)
    res = run_bass_kernel_spmd(nc, in_maps, core_ids=list(range(8)))
    return np.concatenate([res.results[c]["x_out"] for c in range(8)], axis=1)


def kernel(**inputs):
    _ = inputs["L0_attn_Wk"], inputs["L0_attn_Wv"]
    x0 = np.asarray(inputs["L0_attn_hidden_states"], np.float32)
    x_fm = x_to_fm(x0)

    sA = {}
    sA.update(host_prep_l0attn(inputs))
    sA.update(host_prep_l0ffn(inputs))
    shared = host_prep_l1a(inputs)
    sA["wi"] = shared["wi"]
    sA["idb"] = np.eye(128, dtype=ml_dtypes.bfloat16)

    nc = build_launchA()
    in_maps = []
    for c in range(8):
        m = dict(sA)
        m["x_in"] = _c(x_fm[:, c * TOK:(c + 1) * TOK])
        in_maps.append(m)
    resA = run_bass_kernel_spmd(nc, in_maps, core_ids=list(range(8)))

    nc = build_l1a2()
    in_maps = []
    for c in range(8):
        r = resA.results[c]
        if c % 2 == 1:
            halo = _c(resA.results[c - 1]["hbc"][:, TOK - 3:TOK])
        else:
            halo = np.zeros((10240, 3), ml_dtypes.bfloat16)
        m = {k: shared[k] for k in ("cw", "cb", "dtbias", "alog", "idf", "idb")}
        m.update({"hbc": r["hbc"], "halo": halo, "dtr": r["dtr"]})
        in_maps.append(m)
    resB = run_bass_kernel_spmd(nc, in_maps, core_ids=list(range(8)))

    sC = host_prep_l1ffn(inputs)
    nc = build_launchC()
    in_maps = []
    for c in range(8):
        rB = resB.results[c]
        sinit = (resB.results[c - 1]["send"] if c % 2 == 1
                 else np.zeros((128, 8192), np.float32))
        m = dict(sC)
        m.update({k: shared[k] for k in ("alog", "normw", "dvec", "mrep", "wo")})
        m["idf"] = np.eye(128, dtype=np.float32)
        m["idb"] = np.eye(128, dtype=ml_dtypes.bfloat16)
        m.update({"postc": rB["postc"], "gate": resA.results[c]["gate"], "dt": rB["dt"],
                  "x_in": _c(resA.results[c]["x2"]), "sinit": _c(sinit)})
        in_maps.append(m)
    resC = run_bass_kernel_spmd(nc, in_maps, core_ids=list(range(8)))
    x4 = np.concatenate([resC.results[c]["x_out"] for c in range(8)], axis=1)
    return fm_to_x(x4)
